# version 28
# baseline (speedup 1.0000x reference)
import numpy as np
import ml_dtypes
from contextlib import ExitStack

import concourse.bass as bass
import concourse.mybir as mybir
from concourse.bass_utils import run_bass_kernel_spmd

F32 = mybir.dt.float32
BF16 = mybir.dt.bfloat16
I32 = mybir.dt.int32
AF = mybir.ActivationFunctionType
ALU = mybir.AluOpType
AX = mybir.AxisListType.X

B, T, C = 16, 2048, 1024
H, HS = 16, 64
FF, R = 4096, 16
NCORES = 8
SEQ = 2
ROWS = SEQ * T
CC = C // 128
FC = FF // 128


class Sem:
    def __init__(self, es, nc, name):
        self.h = es.enter_context(nc.semaphore(name))
        self.n = 0


class SemPool:

    def __init__(self, es, nc, name, n):
        self.sems = [Sem(es, nc, f"{name}{i}") for i in range(n)]
        self.i = 0

    def take(self):
        s = self.sems[self.i % len(self.sems)]
        self.i += 1
        return s


class KB:
    ENGINES = ("sync", "gpsimd", "vector", "scalar", "tensor")

    def __init__(self, nc):
        self.nc = nc
        self.prog = {e: [] for e in self.ENGINES}
        self.serial = False
        self.esem = {}
        self.pools = {}

    def dma(self, eng, fn, waits=(), pool=None):
        s = self.pools[pool or eng].take()
        w = list(waits) + [(s, s.n)]
        self.op(eng, fn, waits=w, inc=s, inck=16)
        return (s, s.n)

    def op(self, eng, fn, waits=(), inc=None, inck=1):
        if self.serial and self.esem:
            waits = list(waits) + [(s, s.n) for e2, s in self.esem.items()
                                   if e2 != eng]
            if inc is None and eng in self.esem:
                inc = self.esem[eng]
                inck = 1
        if eng in ("vector", "scalar", "gpsimd") and eng in self.esem:
            waits = list(waits) + [(self.esem[eng], self.esem[eng].n)]
            if inc is None:
                inc = self.esem[eng]
                inck = 1
        waits = [(s, v) for (s, v) in waits if s is not None]
        w = tuple((s.h, v) for (s, v) in waits)
        if inc is not None:
            inc.n += inck

        def run(E, fn=fn, w=w, inc=inc, inck=inck):
            for (h, v) in w:
                if callable(v):
                    v = v()
                if v > 0:
                    E.wait_ge(h, v)
            inst = fn(E)
            if inc is not None:
                inst.then_inc(inc.h, inck)

        self.prog[eng].append(run)
        return inc.n if inc is not None else None

    def raw(self, eng, fn):
        self.prog[eng].append(fn)

    def emit(self):
        with self.nc.Block() as block:
            for eng in self.ENGINES:
                fns = self.prog[eng]
                if not fns:
                    continue

                def section(E, fns=fns):
                    for f in fns:
                        f(E)

                getattr(block, eng)(section)
        self.prog = {e: [] for e in self.ENGINES}


def build_program():
    nc = bass.Bass("TRN2", target_bir_lowering=False)

    d_x = nc.dram_tensor("x", [ROWS, C], F32, kind="ExternalInput").ap()
    d_xT = nc.dram_tensor("xT", [C, ROWS], F32, kind="ExternalInput").ap()
    d_wqkvT = nc.dram_tensor("wqkvT", [C, 3 * C], F32, kind="ExternalInput").ap()
    d_wprojT = nc.dram_tensor("wprojT", [C, C], F32, kind="ExternalInput").ap()
    d_bprow = nc.dram_tensor("bprow", [1, C], F32, kind="ExternalInput").ap()
    d_wfcT = nc.dram_tensor("wfcT", [C, FF], F32, kind="ExternalInput").ap()
    d_bfccol = nc.dram_tensor("bfccol", [128, FC], F32, kind="ExternalInput").ap()
    d_afcT = nc.dram_tensor("afcT", [C, R], F32, kind="ExternalInput").ap()
    d_bfcT = nc.dram_tensor("bfcT", [R, FF], F32, kind="ExternalInput").ap()
    d_wpF = nc.dram_tensor("wpF", [FF, C], F32, kind="ExternalInput").ap()
    d_bprow2 = nc.dram_tensor("bprow2", [1, C], F32, kind="ExternalInput").ap()
    d_aprojT = nc.dram_tensor("aprojT", [FF, R], F32, kind="ExternalInput").ap()
    d_bprojT = nc.dram_tensor("bprojT", [R, C], F32, kind="ExternalInput").ap()
    d_identb = nc.dram_tensor("identb", [128, 128], BF16, kind="ExternalInput").ap()
    d_identf = nc.dram_tensor("identf", [128, 128], F32, kind="ExternalInput").ap()
    d_onesf = nc.dram_tensor("onesf", [1, 128], F32, kind="ExternalInput").ap()
    d_ones16 = nc.dram_tensor("ones16", [128, 16], BF16, kind="ExternalInput").ap()
    d_onesb = nc.dram_tensor("onesb", [1, 128], BF16, kind="ExternalInput").ap()
    d_masks = nc.dram_tensor("masks", [128, 896], BF16, kind="ExternalInput").ap()
    d_out = nc.dram_tensor("out", [ROWS, C], F32, kind="ExternalOutput").ap()
    d_x1 = nc.dram_tensor("x1scr", [ROWS, C], F32).ap()
    d_scr8 = nc.dram_tensor("scr8", [1, 8], F32).ap()
    d_recb = nc.dram_tensor("recb", [1, 512], F32).ap()

    es = ExitStack()
    with es:
        pIN = SemPool(es, nc, "dIN", 4)
        pGP = SemPool(es, nc, "dGP", 2)
        pOUT = SemPool(es, nc, "dOUT", 2)
        NONE = (None, 0)
        sPE = Sem(es, nc, "sPE")
        sDV = Sem(es, nc, "sDV")
        sAC = Sem(es, nc, "sAC")
        sPL = Sem(es, nc, "sPL")

        g_identb = es.enter_context(nc.sbuf_tensor("g_identb", [128, 128], BF16))
        g_identf = es.enter_context(nc.sbuf_tensor("g_identf", [128, 128], F32))
        g_onesf = es.enter_context(nc.sbuf_tensor("g_onesf", [1, 128], F32))

        g_bprow2 = es.enter_context(nc.sbuf_tensor("g_bprow2", [1, C], BF16))
        g_onesb = es.enter_context(nc.sbuf_tensor("g_onesb", [1, 128], BF16))
        g_bfccol = es.enter_context(nc.sbuf_tensor("g_bfccol", [128, FC], F32))
        g_afcT = es.enter_context(nc.sbuf_tensor("g_afcT", [128, CC * R], BF16))
        g_bfcT = es.enter_context(nc.sbuf_tensor("g_bfcT", [R, FF], BF16))
        g_aprojT = es.enter_context(nc.sbuf_tensor("g_aprojT", [128, FC * R], BF16))
        g_bprojT = es.enter_context(nc.sbuf_tensor("g_bprojT", [R, C], BF16))
        g_pre = es.enter_context(nc.sbuf_tensor("g_pre", [128, C], F32))
        g_xo = es.enter_context(nc.sbuf_tensor("g_xo", [128, C], F32))
        g_sm = es.enter_context(nc.sbuf_tensor("g_sm", [128, 16], F32))

        g_srow = es.enter_context(nc.sbuf_tensor("g_srow", [1, 8], F32))
        g_scl = es.enter_context(nc.sbuf_tensor("g_scl", [128, 8], F32))
        g_am = es.enter_context(nc.sbuf_tensor("g_am", [128, 2], F32))
        g_scr = es.enter_context(nc.sbuf_tensor("g_scr", [128, 4], F32))

        psA = es.enter_context(nc.psum_tensor("psA", [128, 2048], F32))
        psO = es.enter_context(nc.psum_tensor("psO", [128, 512], F32))
        psBC = es.enter_context(nc.psum_tensor("psBC", [128, 512], F32))
        psT = es.enter_context(nc.psum_tensor("psT", [128, 512], F32))
        psU = es.enter_context(nc.psum_tensor("psU", [128, 512], F32))

        kb = KB(nc)
        kb.esem = {"tensor": sPE, "vector": sDV, "scalar": sAC,
                   "gpsimd": sPL}
        kb.pools = {"sync": pIN, "gpsimd": pGP, "out": pOUT}
        import os
        kb.serial = bool(int(os.environ.get("KSERIAL", "0")))
        _conc = set(os.environ.get("KCONC", "").split(","))

        def region(name):
            if os.environ.get("KSERIAL", "0") == "2":
                kb.serial = name not in _conc

        def dve_fence(ap):
            return sDV.n

        def act_fence(ap):
            return sAC.n

        setup_pairs = [(g_identb, d_identb), (g_identf, d_identf),
                       (g_onesf, d_onesf), (g_bfccol, d_bfccol),
                       (g_onesb, d_onesb)]
        for (dst, src) in setup_pairs:
            kb.dma("sync", lambda E, dst=dst, src=src:
                   E.dma_start(out=dst[:, :], in_=src))


        stg_state = {"free": [NONE, NONE]}

        def staged_cast(dst_ap, src_ap, n):
            slot = stg_state.setdefault("slot", 0)
            stg_state["slot"] = 1 - slot
            vx = kb.dma("sync", lambda E, slot=slot, src_ap=src_ap, n=n: E.dma_start(
                out=g_wstg[0:src_ap.shape[0], slot * 512: slot * 512 + n],
                in_=src_ap),
                waits=[stg_state["free"][slot]])
            vc = kb.op("vector", lambda E, dst_ap=dst_ap, slot=slot, n=n, src_ap=src_ap:
                       E.tensor_copy(dst_ap,
                                     g_wstg[0:src_ap.shape[0], slot * 512: slot * 512 + n]),
                       waits=[vx], inc=sDV)
            stg_state["free"][slot] = (sDV, vc)
            return vc


        esA = ExitStack()
        with esA:
            a_wqkv = esA.enter_context(nc.sbuf_tensor("a_wqkv", [128, CC * 3 * C], BF16))
            a_wproj = esA.enter_context(nc.sbuf_tensor("a_wproj", [128, CC * C], BF16))
            a_K = esA.enter_context(nc.sbuf_tensor("a_K", [128, CC * T], BF16))
            a_V = esA.enter_context(nc.sbuf_tensor("a_V", [128, 16 * (H * 65)], BF16))
            a_xT = esA.enter_context(nc.sbuf_tensor("a_xT", [128, CC * 512], BF16))
            a_xf = esA.enter_context(nc.sbuf_tensor("a_xf", [128, 2 * C], F32))
            a_Q = esA.enter_context(nc.sbuf_tensor("a_Q", [128, CC * 512], BF16))
            a_tE = esA.enter_context(nc.sbuf_tensor("a_tE", [128, 1024], F32))
            a_p2 = esA.enter_context(nc.sbuf_tensor("a_p2", [128, 2048], BF16))
            a_ph = esA.enter_context(nc.sbuf_tensor("a_ph", [128, 1024], BF16))
            a_O = esA.enter_context(nc.sbuf_tensor("a_O", [128, CC * 512], BF16))
            g_wstg = esA.enter_context(nc.sbuf_tensor("g_wstg", [128, 1024], F32))
            g_masks = esA.enter_context(nc.sbuf_tensor("g_masks", [128, 896], BF16))
            g_bprow = esA.enter_context(nc.sbuf_tensor("g_bprow", [1, C], BF16))
            g_ones16 = esA.enter_context(nc.sbuf_tensor("g_ones16", [128, 16], BF16))
            g_rec = esA.enter_context(nc.sbuf_tensor("g_rec", [1, 1024], F32))
            g_bc = esA.enter_context(nc.sbuf_tensor("g_bc", [64, 512], F32))
            for (dst, src) in [(g_ones16, d_ones16), (g_masks, d_masks)]:
                kb.dma("sync", lambda E, dst=dst, src=src:
                       E.dma_start(out=dst[:, :], in_=src))
            consts_waits = [(s, s.n) for s in pIN.sems]
            for cc in range(CC):
                staged_cast(g_afcT[:, cc * R:(cc + 1) * R],
                            d_afcT[cc * 128:(cc + 1) * 128, :], R)
            for hh in range(FF // 512):
                staged_cast(g_bfcT[0:16, hh * 512:(hh + 1) * 512],
                            d_bfcT[:, hh * 512:(hh + 1) * 512], 512)
            for fc in range(FC):
                staged_cast(g_aprojT[:, fc * R:(fc + 1) * R],
                            d_aprojT[fc * 128:(fc + 1) * 128, :], R)
            for hh in range(C // 512):
                staged_cast(g_bprojT[0:16, hh * 512:(hh + 1) * 512],
                            d_bprojT[:, hh * 512:(hh + 1) * 512], 512)
            for hh in range(C // 512):
                staged_cast(g_bprow[0:1, hh * 512:(hh + 1) * 512],
                            d_bprow[:, hh * 512:(hh + 1) * 512], 512)
                staged_cast(g_bprow2[0:1, hh * 512:(hh + 1) * 512],
                            d_bprow2[:, hh * 512:(hh + 1) * 512], 512)
            lora_dv = sDV.n

            for cc in range(CC):
                for hh in range(3 * C // 512):
                    staged_cast(a_wqkv[:, cc * 3 * C + hh * 512: cc * 3 * C + (hh + 1) * 512],
                                d_wqkvT[cc * 128:(cc + 1) * 128, hh * 512:(hh + 1) * 512], 512)
            for cc in range(CC):
                for hh in range(C // 512):
                    staged_cast(a_wproj[:, cc * C + hh * 512: cc * C + (hh + 1) * 512],
                                d_wprojT[cc * 128:(cc + 1) * 128, hh * 512:(hh + 1) * 512], 512)
            wqkv_dv = dve_fence(a_wproj[:, CC * C - 1: CC * C])

            st = {
                "xT_rd": 0,
                "psA_free": [0, 0, 0, 0],
                "psO_free": 0,
                "psU_free": 0,
                "psBC_free": 0,
                "xf_free": [0, 0],
                "ln_busy": 0,
                "ln_busy_ac": 0,
                "xo_free": NONE,
                "sm_free": 0,
            }

            def load_xT(t0, extra_waits=(), tag="R1"):
                region(tag)
                last = 0
                for cc in range(CC):
                    slot = stg_state.setdefault("slot", 0)
                    stg_state["slot"] = 1 - slot
                    vx = kb.dma("sync", lambda E, cc=cc, slot=slot, t0=t0: E.dma_start(
                        out=g_wstg[:, slot * 512:(slot + 1) * 512],
                        in_=d_xT[cc * 128:(cc + 1) * 128, t0:t0 + 512]),
                        waits=[stg_state["free"][slot],
                               (sPE, st["xT_rd"])] + (list(extra_waits) if cc == 0 else []))
                    vc = kb.op("vector", lambda E, cc=cc, slot=slot: E.tensor_copy(
                        a_xT[:, cc * 512:(cc + 1) * 512],
                        g_wstg[:, slot * 512:(slot + 1) * 512]),
                        waits=[vx], inc=sDV)
                    stg_state["free"][slot] = (sDV, vc)
                    last = dve_fence(a_xT[:, cc * 512 + 511: cc * 512 + 512])
                return last

            def load_xT_pool(t0):
                last = 0
                for cc in range(CC):
                    slot = stg_state.setdefault("slot", 0)
                    stg_state["slot"] = 1 - slot
                    vx = kb.dma("sync", lambda E, cc=cc, slot=slot, t0=t0: E.dma_start(
                        out=g_wstg[:, slot * 512:(slot + 1) * 512],
                        in_=d_xT[cc * 128:(cc + 1) * 128, t0:t0 + 512]),
                        waits=[stg_state["free"][slot], (sPE, st["xT_rd"])])
                    vc = kb.op("gpsimd", lambda E, cc=cc, slot=slot: E.tensor_copy(
                        a_xT[:, cc * 512:(cc + 1) * 512],
                        g_wstg[:, slot * 512:(slot + 1) * 512]),
                        waits=[vx], inc=sPL)
                    stg_state["free"][slot] = (sPL, vc)
                    last = vc
                return (sPL, last)

            def ln_and_store(ps_halves_waits, resid_cols, dst_rows, tag,
                             sqbuf=None):
                rbuf, rcol = resid_cols
                adds = []
                for oh in range(2):
                    ps_ap, w = ps_halves_waits[oh]
                    w = list(w)
                    if oh == 0:
                        w += [(sAC, st["ln_busy_ac"]), (sDV, st["ln_busy"])]
                    va = kb.op("vector", lambda E, ps_ap=ps_ap, oh=oh, rbuf=rbuf, rcol=rcol:
                               E.tensor_tensor(
                                   g_pre[:, oh * 512:(oh + 1) * 512], ps_ap,
                                   rbuf[:, rcol + oh * 512: rcol + (oh + 1) * 512],
                                   ALU.add),
                               waits=w, inc=sDV)
                    adds.append(va)
                vaddf = dve_fence(g_pre[:, 1023:1024])
                v1 = kb.op("vector", lambda E: E.tensor_reduce(
                    g_sm[:, 0:1], g_pre[:, :], AX, ALU.add),
                    waits=[(sDV, st["sm_free"])], inc=sDV)
                if sqbuf is None:
                    vsq = kb.op("gpsimd", lambda E: E.tensor_tensor(
                        g_xo[:, :], g_pre[:, :], g_pre[:, :], ALU.mult),
                        waits=[(sDV, vaddf), st["xo_free"]], inc=sPL)
                    v2 = kb.op("vector", lambda E: E.tensor_reduce(
                        g_sm[:, 1:2], g_xo[:, :], AX, ALU.add),
                        waits=[(sPL, vsq)], inc=sDV)
                else:
                    sq = sqbuf
                    vsq = kb.op("gpsimd", lambda E, sq=sq: E.tensor_tensor(
                        sq[:, 0:C], g_pre[:, :], g_pre[:, :], ALU.mult),
                        waits=[(sDV, vaddf)], inc=sPL)
                    v2 = kb.op("vector", lambda E, sq=sq: E.tensor_reduce(
                        g_sm[:, 1:2], sq[:, 0:C], AX, ALU.add),
                        waits=[(sPL, vsq)], inc=sDV)
                v3 = kb.op("vector", lambda E: E.tensor_scalar(
                    g_sm[:, 2:3], g_sm[:, 0:1], 1.0 / C, None, ALU.mult),
                    waits=[(sDV, v2)], inc=sDV)
                v4 = kb.op("vector", lambda E: E.tensor_tensor(
                    g_sm[:, 3:4], g_sm[:, 2:3], g_sm[:, 2:3], ALU.mult),
                    waits=[(sDV, v3)], inc=sDV)
                v5 = kb.op("vector", lambda E: E.scalar_tensor_tensor(
                    g_sm[:, 4:5], g_sm[:, 1:2], 1.0 / C, g_sm[:, 3:4],
                    ALU.mult, ALU.subtract),
                    waits=[(sDV, v4)], inc=sDV)
                v6 = kb.op("vector", lambda E: E.tensor_scalar(
                    g_sm[:, 4:5], g_sm[:, 4:5], 1.0, 1e-5, ALU.mult, ALU.add),
                    waits=[(sDV, v5)], inc=sDV)
                v6f = dve_fence(g_sm[:, 4:5])
                v7 = kb.op("scalar", lambda E: E.activation(
                    g_sm[:, 5:6], g_sm[:, 4:5], AF.Ln),
                    waits=[(sDV, v6f)], inc=sAC)
                v8 = kb.op("scalar", lambda E: E.activation(
                    g_sm[:, 6:7], g_sm[:, 5:6], AF.Exp, scale=-0.5),
                    waits=[(sAC, v7)], inc=sAC)
                v8f = act_fence(g_sm[:, 6:7])
                st["ln_busy_ac"] = v8f
                v9 = kb.op("vector", lambda E: E.tensor_scalar(
                    g_xo[:, :], g_pre[:, :], g_sm[:, 2:3], g_sm[:, 6:7],
                    ALU.subtract, ALU.mult),
                    waits=[(sAC, v8f), st["xo_free"]], inc=sDV)
                st["ln_busy"] = v9
                st["sm_free"] = v9
                v9f = dve_fence(g_xo[:, 1023:1024])
                dram, row0 = dst_rows
                vo = kb.dma("gpsimd", lambda E, dram=dram, row0=row0: E.dma_start(
                    out=dram[row0:row0 + 128, :], in_=g_xo[:, :]),
                    waits=[(sDV, v9f)], pool="out")
                st["xo_free"] = vo
                return v9, vo, adds[1]

            kv_ready = 0
            scores_last = 0

            for s in range(SEQ):
                base = s * T
                kv_war = [(sPE, sPE.n)] if s > 0 else []
                region("R1")
                for tg in range(4):
                    w = kv_war if tg == 0 else []
                    xw = ([(sDV, wqkv_dv)] + consts_waits) if (s == 0 and tg == 0) else []
                    xtg_ready = load_xT(base + tg * 512, extra_waits=w + xw)
                    for kf in range(CC):
                        bank = kf % 4
                        w = [(sDV, xtg_ready), (sDV, st["psA_free"][bank])]
                        if s == 0 and tg == 0 and kf == 0:
                            w.append((sDV, wqkv_dv))
                        for cc in range(CC):
                            kb.op("tensor", lambda E, bank=bank, cc=cc, kf=kf: E.matmul(
                                psA[:, bank * 512:(bank + 1) * 512],
                                a_wqkv[:, cc * 3 * C + C + kf * 128: cc * 3 * C + C + (kf + 1) * 128],
                                a_xT[:, cc * 512:(cc + 1) * 512],
                                start=(cc == 0), stop=(cc == CC - 1)),
                                waits=(w if cc == 0 else []),
                                inc=(sPE if cc == CC - 1 else None))
                            if cc == 0:
                                w = []
                        vk = kb.op("vector", lambda E, bank=bank, kf=kf, tg=tg: E.tensor_copy(
                            a_K[:, kf * T + tg * 512: kf * T + (tg + 1) * 512],
                            psA[:, bank * 512:(bank + 1) * 512]),
                            waits=[(sPE, sPE.n)], inc=sDV)
                        st["psA_free"][bank] = vk
                        dve_fence(a_K[:, kf * T + (tg + 1) * 512 - 1: kf * T + (tg + 1) * 512])
                    for tc in range(4):
                        tch = tg * 4 + tc
                        for fh in range(2):
                            bank = (tc * 2 + fh) % 4
                            w = [(sDV, st["psA_free"][bank])]
                            for cc in range(CC):
                                kb.op("tensor", lambda E, bank=bank, cc=cc, tc=tc, fh=fh: E.matmul(
                                    psA[:, bank * 512:(bank + 1) * 512],
                                    a_xT[:, cc * 512 + tc * 128: cc * 512 + (tc + 1) * 128],
                                    a_wqkv[:, cc * 3 * C + 2 * C + fh * 512: cc * 3 * C + 2 * C + (fh + 1) * 512],
                                    start=(cc == 0), stop=(cc == CC - 1)),
                                    waits=(w if cc == 0 else []),
                                    inc=(sPE if cc == CC - 1 else None))
                                if cc == 0:
                                    w = []
                            vv = kb.op("vector", lambda E, bank=bank, tch=tch, fh=fh: E.tensor_copy(
                                a_V[:, tch * (H * 65) + fh * 520: tch * (H * 65) + (fh + 1) * 520]
                                .rearrange("p (h d) -> p h d", h=8)[:, :, 0:64],
                                psA[:, bank * 512:(bank + 1) * 512]
                                .rearrange("p (h d) -> p h d", h=8)),
                                waits=[(sPE, sPE.n)], inc=sDV)
                            st["psA_free"][bank] = vv
                            st["xT_rd"] = sPE.n
                            dve_fence(a_V[:, tch * (H * 65) + (fh + 1) * 520 - 2: tch * (H * 65) + (fh + 1) * 520 - 1])
                        kb.op("vector", lambda E, tch=tch: E.tensor_copy(
                            a_V[:, tch * (H * 65): (tch + 1) * (H * 65)]
                            .rearrange("p (h d) -> p h d", h=16)[:, :, 64:65],
                            g_ones16[:, :].rearrange("p (h o) -> p h o", o=1)),
                            waits=(consts_waits if (s == 0 and tg == 0 and tc == 0) else []),
                            inc=sDV)
                        dve_fence(a_V[:, tch * (H * 65) + 15 * 65 + 64: tch * (H * 65) + 15 * 65 + 65])
                dve_fence(a_K[:, CC * T - 1: CC * T])
                kv_ready = dve_fence(a_V[:, 16 * H * 65 - 2: 16 * H * 65 - 1])

                for j in range(4):
                    q0 = base + j * 512
                    if j == 0:
                        xtq_ready = load_xT(q0, tag="R2a")
                        region("R2b")
                        for qf in range(CC):
                            bank = qf % 4
                            w = [(sDV, xtq_ready), (sDV, st["psA_free"][bank]),
                                 (sAC, st.get("exp_bank", [0, 0])[bank // 2])]
                            for cc in range(CC):
                                kb.op("tensor", lambda E, bank=bank, cc=cc, qf=qf: E.matmul(
                                    psA[:, bank * 512:(bank + 1) * 512],
                                    a_wqkv[:, cc * 3 * C + qf * 128: cc * 3 * C + (qf + 1) * 128],
                                    a_xT[:, cc * 512:(cc + 1) * 512],
                                    start=(cc == 0), stop=(cc == CC - 1)),
                                    waits=(w if cc == 0 else []),
                                    inc=(sPE if cc == CC - 1 else None))
                                if cc == 0:
                                    w = []
                            st["xT_rd"] = sPE.n
                            vq = kb.op("vector", lambda E, bank=bank, qf=qf: E.tensor_copy(
                                a_Q[:, qf * 512:(qf + 1) * 512],
                                psA[:, bank * 512:(bank + 1) * 512]),
                                waits=[(sPE, sPE.n)], inc=sDV)
                            st["psA_free"][bank] = vq
                            dve_fence(a_Q[:, qf * 512 + 511: qf * 512 + 512])
                        st["q_rdy"] = (sDV, sDV.n)
                    q_ready_t = st["q_rdy"]

                    xf_dma = [NONE] * 4
                    for tc in range(2):
                        xf_dma[tc] = kb.dma("sync", lambda E, tc=tc, q0=q0: E.dma_start(
                            out=a_xf[:, (tc % 2) * C:((tc % 2) + 1) * C],
                            in_=d_x[q0 + tc * 128: q0 + (tc + 1) * 128, :]),
                            waits=[(sDV, st["xf_free"][tc % 2])])

                    region("R3")
                    jobs = []
                    for h in range(H):
                        for gg in range(2 * (j + 1)):
                            jobs.append((h, gg))
                    LA = 2
                    sq_of = {}
                    avpe_of = {}
                    pending = []

                    weave = j < 3
                    if weave:
                        xtq_next = load_xT_pool(base + (j + 1) * 512)

                    def emit_qb_chunk(k, scores_pe):
                        wq0 = [xtq_next, (sAC, st.get("psT_q", 0)),
                               (sPE, scores_pe)]
                        for cc in range(CC):
                            kb.op("tensor", lambda E, cc=cc, k=k: E.matmul(
                                psT[:, 0:512],
                                a_wqkv[:, cc * 3 * C + k * 128: cc * 3 * C + (k + 1) * 128],
                                a_xT[:, cc * 512:(cc + 1) * 512],
                                start=(cc == 0), stop=(cc == CC - 1)),
                                waits=(wq0 if cc == 0 else []),
                                inc=(sPE if cc == CC - 1 else None))
                        st["xT_rd"] = sPE.n
                        vq = kb.op("scalar", lambda E, k=k: E.activation(
                            a_Q[:, k * 512:(k + 1) * 512], psT[:, 0:512],
                            AF.Identity),
                            waits=[(sPE, sPE.n)], inc=sAC)
                        st["psT_q"] = vq
                        if k == CC - 1:
                            st["q_rdy"] = (sAC, vq)

                    def emit_scores_softplus(idx):
                        h, gg = jobs[idx]
                        pb = (h % 2) * 64
                        kchunk = h // 2
                        pp = idx % 2
                        w0 = [(sAC, st.get("exp_bank", [0, 0])[pp]),
                              q_ready_t, (sDV, kv_ready)]
                        for r2 in range(2):
                            kc = 2 * gg + r2
                            kb.op("tensor", lambda E, r2=r2, kc=kc, pb=pb, kchunk=kchunk, pp=pp: E.matmul(
                                psA[:, (2 * pp + r2) * 512:(2 * pp + r2 + 1) * 512],
                                a_K[pb:pb + 64, kchunk * T + kc * 128: kchunk * T + (kc + 1) * 128],
                                a_Q[pb:pb + 64, kchunk * 512:(kchunk + 1) * 512],
                                start=True, stop=True),
                                waits=(w0 if r2 == 0 else []),
                                inc=(sPE if r2 == 1 else None))
                        scores_done = sPE.n
                        if weave and h % 2 == 1 and gg == 2 * (j + 1) - 1:
                            emit_qb_chunk(h // 2, scores_done)
                        ve = kb.op("scalar", lambda E, pp=pp: E.activation(
                            a_tE[:, :], psA[:, pp * 1024:(pp + 1) * 1024],
                            AF.Exp, scale=0.125),
                            waits=[(sPE, scores_done)], inc=sAC)
                        st.setdefault("exp_bank", [0, 0])[pp] = ve
                        vl = kb.op("scalar", lambda E: E.activation(
                            a_ph[:, :], a_tE[:, :], AF.Ln, bias=1.0),
                            waits=[(sDV, st.get("ph_sq", 0))], inc=sAC)
                        vsq = kb.op("vector", lambda E, pp=pp: E.tensor_tensor(
                            a_p2[:, pp * 1024:(pp + 1) * 1024],
                            a_ph[:, :], a_ph[:, :], ALU.mult),
                            waits=[(sAC, vl),
                                   (sPE, avpe_of[idx - 2] if idx >= 2 else 0)],
                            inc=sDV)
                        st["ph_sq"] = vsq
                        ngg = 2 * (j + 1)
                        if gg >= ngg - 2:
                            for r2 in range(2):
                                r4 = 2 * gg + r2 - 4 * j
                                vsq = kb.op("vector", lambda E, pp=pp, r2=r2, r4=r4: E.tensor_tensor(
                                    a_p2[:, pp * 1024 + r2 * 512: pp * 1024 + (r2 + 1) * 512],
                                    a_p2[:, pp * 1024 + r2 * 512: pp * 1024 + (r2 + 1) * 512],
                                    g_masks[:, 384 - 128 * r4: 896 - 128 * r4],
                                    ALU.mult), inc=sDV)
                        sq_of[idx] = vsq

                    def emit_av(idx):
                        h, gg = jobs[idx]
                        pp = idx % 2
                        psAV = psO if h % 2 == 0 else psU
                        avfree_key = "psO_free" if h % 2 == 0 else "psU_free"
                        wav = [(sDV, sq_of[idx])]
                        if gg == 0:
                            wav.append((sDV, st[avfree_key]))
                        for r2 in range(2):
                            kc = 2 * gg + r2
                            kb.op("tensor", lambda E, kc=kc, r2=r2, h=h, pp=pp, psAV=psAV: E.matmul(
                                psAV[0:65, :],
                                a_V[:, kc * (H * 65) + h * 65: kc * (H * 65) + (h + 1) * 65],
                                a_p2[:, pp * 1024 + r2 * 512: pp * 1024 + (r2 + 1) * 512],
                                start=(kc == 0), stop=(kc == 4 * j + 3),
                                skip_group_check=True),
                                waits=(wav if r2 == 0 else []),
                                inc=(sPE if r2 == 1 else None))
                        avpe_of[idx] = sPE.n
                        if gg == 2 * (j + 1) - 1:
                            for (rid, fn) in list(pending):
                                pending.remove((rid, fn))
                                fn()
                            emit_norm_part1(idx, h)

                    def emit_norm_part1(idx, h):
                        psAV = psO if h % 2 == 0 else psU
                        av_done = avpe_of[idx]
                        vr1 = kb.op("vector", lambda E, psAV=psAV: E.tensor_scalar(
                            g_rec[0:1, 0:512], psAV[64:65, :], 1e-9, None, ALU.add),
                            waits=[(sPE, av_done),
                                   (sAC, st.get("rec_ln_rd", 0)),
                                   st.get("bc_dma1", NONE)], inc=sDV)
                        vl1 = kb.op("scalar", lambda E: E.activation(
                            g_rec[0:1, 512:1024], g_rec[0:1, 0:512], AF.Ln),
                            waits=[(sDV, vr1)], inc=sAC)
                        st["rec_ln_rd"] = vl1
                        vl2 = kb.op("scalar", lambda E: E.activation(
                            g_rec[0:1, 0:512], g_rec[0:1, 512:1024],
                            AF.Exp, scale=-1.0), inc=sAC)
                        vl2f = act_fence(g_rec[0:1, 511:512])
                        pending.append((idx + 3,
                                        lambda h=h, vl2f=vl2f: emit_norm_part2(h, vl2f)))

                    def emit_norm_part2(h, vl2f):
                        psAV = psO if h % 2 == 0 else psU
                        avfree_key = "psO_free" if h % 2 == 0 else "psU_free"
                        pb = (h % 2) * 64
                        kchunk = h // 2
                        d1 = kb.dma("sync", lambda E: E.dma_start(
                            out=d_recb, in_=g_rec[0:1, 0:512]),
                            waits=[(sAC, vl2f)])
                        st["bc_dma1"] = d1
                        d2 = kb.dma("sync", lambda E: E.dma_start(
                            out=g_bc[:, :], in_=d_recb.broadcast_to([64, 512])),
                            waits=[d1, (sDV, st.get("bc_free", 0))])
                        vm = kb.op("vector", lambda E, pb=pb, kchunk=kchunk, psAV=psAV: E.tensor_tensor(
                            a_O[pb:pb + 64, kchunk * 512:(kchunk + 1) * 512],
                            psAV[0:64, :], g_bc[:, :], ALU.mult),
                            waits=[d2], inc=sDV)
                        st[avfree_key] = vm
                        st["bc_free"] = vm

                    n = len(jobs)
                    for idx in range(n + LA):
                        for (rid, fn) in [p for p in pending if p[0] <= idx]:
                            pending.remove((rid, fn))
                            fn()
                        if LA <= idx < n + LA:
                            emit_av(idx - LA)
                        if idx < n:
                            emit_scores_softplus(idx)
                    for (rid, fn) in list(pending):
                        pending.remove((rid, fn))
                        fn()
                    assert not pending
                    O_ready = sDV.n

                    region("R5")
                    for tc in range(4):
                        pair = (psO, psBC)
                        fkeys = ("psO_free", "psBC_free")
                        proj_done = [0, 0]
                        for oh in range(2):
                            ps = pair[oh]
                            wfree = [(sDV, st.get(fkeys[oh], 0))]
                            w = [(sDV, O_ready)] + wfree
                            if s == 0 and j == 0 and tc == 0:
                                w.extend(consts_waits)
                            kb.op("tensor", lambda E, ps=ps, oh=oh: E.matmul(
                                ps[:, :], g_onesb[0:1, 0:128],
                                g_bprow[0:1, oh * 512:(oh + 1) * 512],
                                start=True, stop=False), waits=w, inc=None)
                            for cc in range(CC):
                                kb.op("tensor", lambda E, ps=ps, cc=cc, tc=tc, oh=oh: E.matmul(
                                    ps[:, :],
                                    a_O[:, cc * 512 + tc * 128: cc * 512 + (tc + 1) * 128],
                                    a_wproj[:, cc * C + oh * 512: cc * C + (oh + 1) * 512],
                                    start=False, stop=(cc == CC - 1)),
                                    inc=(sPE if cc == CC - 1 else None))
                            proj_done[oh] = sPE.n
                        slot = tc % 2
                        va, vo, vadds = ln_and_store(
                            [(pair[0][:, :], [(sPE, proj_done[0]), xf_dma[tc]]),
                             (pair[1][:, :], [(sPE, proj_done[1])])],
                            (a_xf, slot * C),
                            (d_x1, q0 + tc * 128), "ln1", sqbuf=None)
                        st[fkeys[0]] = vadds
                        st[fkeys[1]] = vadds
                        st["xf_free"][slot] = vadds
                        if tc + 2 < 4:
                            xf_dma[tc + 2] = kb.dma("sync", lambda E, tc=tc, q0=q0, slot=slot: E.dma_start(
                                out=a_xf[:, slot * C:(slot + 1) * C],
                                in_=d_x[q0 + (tc + 2) * 128: q0 + (tc + 3) * 128, :]),
                                waits=[(sDV, va)])

            attn_done = dict(pe=sPE.n, dv=sDV.n, ac=sAC.n,
                             do=[(s, s.n) for s in pOUT.sems])
            kb.emit()

        esB = ExitStack()
        with esB:
            b_Wfc = esB.enter_context(nc.sbuf_tensor("b_Wfc", [128, CC * FF], BF16))
            b_Wp = esB.enter_context(nc.sbuf_tensor("b_Wp", [128, FC * C], BF16))
            b_ws = esB.enter_context(nc.sbuf_tensor("b_ws", [128, 2 * 1024], F32))
            b_z = esB.enter_context(nc.sbuf_tensor("b_z", [128, 1024], F32))
            b_zi = esB.enter_context(nc.sbuf_tensor("b_zi", [128, 1024], I32))
            b_d = esB.enter_context(nc.sbuf_tensor("b_d", [128, 1024], F32))
            b_r = esB.enter_context(nc.sbuf_tensor("b_r", [128, 1024], F32))
            b_x1f = esB.enter_context(nc.sbuf_tensor("b_x1f", [128, 4 * C], F32))
            b_X1T = esB.enter_context(nc.sbuf_tensor("b_X1T", [128, CC * 256], BF16))
            b_h = esB.enter_context(nc.sbuf_tensor("b_h", [128, 512], BF16))
            b_u = esB.enter_context(nc.sbuf_tensor("b_u", [16, 512], BF16))

            W = dict(pe=[(sPE, attn_done["pe"])], dv=[(sDV, attn_done["dv"])],
                     ac=[(sAC, attn_done["ac"])])
            barrier = W["pe"] + W["dv"] + W["ac"] + attn_done["do"]

            def w_steps(which):
                if which == 0:
                    return [(d_wfcT[cc * 128:(cc + 1) * 128, hh * 1024:(hh + 1) * 1024],
                             b_Wfc[:, cc * FF + hh * 1024: cc * FF + (hh + 1) * 1024], 1024)
                            for hh in range(4) for cc in range(CC)]
                return [(d_wpF[fc * 128:(fc + 1) * 128, :],
                         b_Wp[:, fc * C:(fc + 1) * C], 1024)
                        for fc in range(FC)]

            ws_free = [attn_done["dv"]] * 4
            am_v = [0, 0]
            accs = [(g_am[:, 0:1], g_am[:, 1:2]), (g_scr[:, 0:1], g_scr[:, 1:2])]
            for which in range(2):
                hsteps = []
                for (src, dst, n) in w_steps(which):
                    hsteps.append(src[:, 0:512])
                    hsteps.append(src[:, 512:n])
                red_prev = [0, 0]
                for i, hsrc in enumerate(hsteps):
                    slot = i % 4
                    ch = i % 2
                    accb, tmpb = accs[ch]
                    vd = kb.dma("sync", lambda E, hsrc=hsrc, slot=slot: E.dma_start(
                        out=b_ws[:, slot * 512:(slot + 1) * 512], in_=hsrc),
                        waits=[(sDV, ws_free[slot])] +
                              (barrier if (which == 0 and i == 0) else []))
                    vr = kb.op("vector", lambda E, slot=slot, tmpb=tmpb: E.tensor_reduce(
                        tmpb, b_ws[:, slot * 512:(slot + 1) * 512],
                        AX, ALU.max, apply_absolute_value=True),
                        waits=[vd, (sDV, red_prev[ch])], inc=sDV)
                    if i < 2:
                        vacc = kb.op("vector", lambda E, accb=accb, tmpb=tmpb: E.tensor_scalar(
                            accb, tmpb, 1.0, None, ALU.mult),
                            waits=[(sDV, vr)], inc=sDV)
                    else:
                        vacc = kb.op("vector", lambda E, accb=accb, tmpb=tmpb: E.tensor_tensor(
                            accb, accb, tmpb, ALU.max),
                            waits=[(sDV, vr)], inc=sDV)
                    red_prev[ch] = vacc
                    ws_free[slot] = vacc
                vmrg = kb.op("vector", lambda E: E.tensor_tensor(
                    g_am[:, 0:1], g_am[:, 0:1], g_scr[:, 0:1], ALU.max),
                    waits=[(sDV, red_prev[0]), (sDV, red_prev[1])], inc=sDV)
                vt = kb.op("tensor", lambda E: E.transpose(
                    psU[0:1, 0:128], g_am[:, 0:1], g_identf[:, :]),
                    waits=[(sDV, vmrg)], inc=sPE)
                vx = kb.op("vector", lambda E, which=which: E.tensor_reduce(
                    g_srow[0:1, 4 * which: 4 * which + 1], psU[0:1, 0:128],
                    AX, ALU.max), waits=[(sPE, vt)], inc=sDV)
                q_p1_psU = vx
                v1 = kb.op("vector", lambda E, which=which: E.reciprocal(
                    g_srow[0:1, 4 * which + 1: 4 * which + 2],
                    g_srow[0:1, 4 * which: 4 * which + 1]),
                    waits=[(sDV, vx)], inc=sDV)
                v2 = kb.op("vector", lambda E, which=which: E.tensor_scalar(
                    g_srow[0:1, 4 * which + 1: 4 * which + 2],
                    g_srow[0:1, 4 * which + 1: 4 * which + 2], 127.0, None,
                    ALU.mult), waits=[(sDV, v1)], inc=sDV)
                v3 = kb.op("vector", lambda E, which=which: E.tensor_scalar(
                    g_srow[0:1, 4 * which + 2: 4 * which + 3],
                    g_srow[0:1, 4 * which: 4 * which + 1], 1.0 / 127.0, None,
                    ALU.mult), waits=[(sDV, v2)], inc=sDV)
                v4 = kb.op("vector", lambda E, which=which: E.tensor_scalar(
                    g_srow[0:1, 4 * which + 3: 4 * which + 4],
                    g_srow[0:1, 4 * which: 4 * which + 1], -1.0 / 127.0, None,
                    ALU.mult), waits=[(sDV, v3)], inc=sDV)
                am_v[which] = v4
            vsrf = dve_fence(g_srow[0:1, 7:8])
            vb1 = kb.dma("gpsimd", lambda E: E.dma_start(out=d_scr8, in_=g_srow[0:1, :]),
                         waits=[(sDV, vsrf)])
            scl_ready = kb.dma("gpsimd", lambda E: E.dma_start(
                out=g_scl[:, :], in_=d_scr8.broadcast_to([128, 8])),
                waits=[vb1])

            ws_free2 = [(sDV, max(ws_free[0], ws_free[1])),
                        (sDV, max(ws_free[2], ws_free[3]))]
            qst = {"qi": 0, "first": True}
            qfc_gate = [0, 0, 0, 0]
            wp_gate = [0] * FC

            def emit_p2_step(which, src, dst, n):
                sc = 4 * which
                slot = qst["qi"] % 2
                qst["qi"] += 1
                vd = kb.dma("sync", lambda E, src=src, slot=slot, n=n: E.dma_start(
                    out=b_ws[:, slot * 1024: slot * 1024 + n], in_=src),
                    waits=[ws_free2[slot]])
                wq = [vd]
                if qst["first"]:
                    wq.append(scl_ready)
                    qst["first"] = False
                vz = kb.op("vector", lambda E, slot=slot, n=n, sc=sc: E.tensor_scalar(
                    b_d[:, 0:n], b_ws[:, slot * 1024: slot * 1024 + n],
                    g_scl[:, sc + 1:sc + 2], None, ALU.mult),
                    waits=wq, inc=sDV)
                ws_free2[slot] = (sDV, vz)
                kb.op("vector", lambda E, n=n: E.tensor_copy(
                    b_zi[:, 0:n], b_d[:, 0:n]))
                kb.op("vector", lambda E, n=n: E.tensor_copy(
                    b_r[:, 0:n], b_zi[:, 0:n]))
                kb.op("vector", lambda E, n=n: E.tensor_tensor(
                    b_d[:, 0:n], b_d[:, 0:n], b_r[:, 0:n], ALU.subtract))
                kb.op("vector", lambda E, n=n: E.scalar_tensor_tensor(
                    b_r[:, 0:n], b_d[:, 0:n], 0.5, b_r[:, 0:n],
                    ALU.is_ge, ALU.add))
                kb.op("vector", lambda E, n=n: E.scalar_tensor_tensor(
                    b_r[:, 0:n], b_d[:, 0:n], -0.5, b_r[:, 0:n],
                    ALU.is_ge, ALU.add))
                vfin = kb.op("vector", lambda E, dst=dst, n=n, sc=sc: E.tensor_scalar(
                    dst, b_r[:, 0:n], g_scl[:, sc + 2:sc + 3],
                    g_scl[:, sc + 3:sc + 4], ALU.mult, ALU.add), inc=sDV)
                return vfin

            wfc_steps = w_steps(0)
            wp_steps = w_steps(1)

            def emit_p2_all():
                fi = 0
                pi = 0
                for hh in range(4):
                    for _ in range(8):
                        vfin = emit_p2_step(0, *wfc_steps[fi])
                        fi += 1
                    qfc_gate[hh] = vfin
                    nwp = 4 if hh < 3 else 20
                    for _ in range(nwp):
                        if pi >= FC:
                            break
                        wp_gate[pi] = emit_p2_step(1, *wp_steps[pi])
                        pi += 1

            stB = {
                "x1f_free": [attn_done["dv"]] * 4,
                "x1rd": [attn_done["pe"]] * 4,
                "psT_free": attn_done["dv"],
                "X1T_free": attn_done["pe"],
                "h_free": [attn_done["pe"], attn_done["pe"]],
                "psF_free": [attn_done["ac"]] * 3,
                "psT_gelu": attn_done["ac"],
                "psA_free": attn_done["dv"],
                "psU_free": attn_done["dv"],
                "ln_busy": st["ln_busy"], "ln_busy_ac": st["ln_busy_ac"],
                "xo_free": st["xo_free"], "sm_free": st["sm_free"],
                "lnq": [],
            }
            st.update({})

            x1_all = attn_done["do"]

            gelu_done = {}

            def tile_prologue(s, it):
                rows0 = s * T + it * 256
                tp = (s * 8 + it) % 2
                x1v = [NONE, NONE]
                for tc in range(2):
                    fslot = tp * 2 + tc
                    x1v[tc] = kb.dma("sync", lambda E, tc=tc, rows0=rows0, fslot=fslot: E.dma_start(
                        out=b_x1f[:, fslot * C:(fslot + 1) * C],
                        in_=d_x1[rows0 + tc * 128: rows0 + (tc + 1) * 128, :]),
                        waits=[(sDV, stB["x1f_free"][fslot]),
                               (sPE, stB["x1rd"][fslot])] +
                              (x1_all if (s == 0 and it == 0) else []))
                    for b2 in range(2):
                        for i4 in range(4):
                            i = b2 * 4 + i4
                            w = []
                            if i4 == 0:
                                w = [x1v[tc], (sDV, stB["psT_free"]),
                                     (sAC, stB["psT_gelu"]),
                                     (sPE, stB["X1T_free"])]
                            kb.op("tensor", lambda E, fslot=fslot, i=i, i4=i4: E.transpose(
                                psT[:, i4 * 128:(i4 + 1) * 128],
                                b_x1f[:, fslot * C + i * 128: fslot * C + (i + 1) * 128],
                                g_identf[:, :]),
                                waits=w, inc=(sPE if i4 == 3 else None))
                        ve = kb.op("vector", lambda E, tc=tc, b2=b2: E.tensor_copy(
                            b_X1T[:, :].rearrange("p (c t) -> p c t", c=CC)
                            [:, b2 * 4:(b2 + 1) * 4, tc * 128:(tc + 1) * 128],
                            psT[:, :].rearrange("p (c t) -> p c t", c=4)),
                            waits=[(sPE, sPE.n)], inc=sDV)
                        stB["psT_free"] = ve
                    stB["x1rd"][fslot] = sPE.n
                    dve_fence(b_X1T[:, 7 * 256 + tc * 128 + 127: 7 * 256 + tc * 128 + 128])
                x1t_ready = sDV.n
                wu = [(sDV, x1t_ready), (sDV, stB["psU_free"]),
                      (sDV, q_p1_psU)]
                for cc in range(CC):
                    kb.op("tensor", lambda E, cc=cc: E.matmul(
                        psU[0:16, 0:256],
                        g_afcT[:, cc * R:(cc + 1) * R],
                        b_X1T[:, cc * 256:(cc + 1) * 256],
                        start=(cc == 0), stop=(cc == CC - 1)),
                        waits=(wu if cc == 0 else []),
                        inc=(sPE if cc == CC - 1 else None))
                    if cc == 0:
                        wu = []
                vu = kb.op("vector", lambda E: E.tensor_scalar(
                    b_u[0:16, 0:256], psU[0:16, 0:256], 2.0, None, ALU.mult),
                    waits=[(sPE, sPE.n)], inc=sDV)
                u_ready = dve_fence(b_u[0:16, 255:256])
                return u_ready, tp, rows0

            tiles = [(s, it) for s in range(SEQ) for it in range(8)]
            pro = tile_prologue(*tiles[0])
            emit_p2_all()
            for tix, (s, it) in enumerate(tiles):
                if True:
                    u_ready, tp, rows0 = pro


                    proj_state = {"first": True}

                    def emit_fc(f):
                        sl = f % 3
                        psF = (psO, psBC, psT)[sl]
                        wfc = [(sAC, stB["psF_free"][sl]), (sDV, u_ready),
                               (sDV, qfc_gate[f // CC])]
                        if sl == 2:
                            wfc.append((sDV, stB["psT_free"]))
                        for cc in range(CC):
                            kb.op("tensor", lambda E, psF=psF, cc=cc, f=f: E.matmul(
                                psF[:, 0:256],
                                b_Wfc[:, cc * FF + f * 128: cc * FF + (f + 1) * 128],
                                b_X1T[:, cc * 256:(cc + 1) * 256],
                                start=(cc == 0), stop=False),
                                waits=(wfc if cc == 0 else []))
                            if cc == 0:
                                wfc = []
                        kb.op("tensor", lambda E, psF=psF, f=f: E.matmul(
                            psF[:, 0:256], g_bfcT[0:16, f * 128:(f + 1) * 128],
                            b_u[0:16, 0:256], start=False, stop=True), inc=sPE)
                        return sPE.n

                    def emit_gelu(f, fc_done):
                        pp = f % 2
                        sl = f % 3
                        psF = (psO, psBC, psT)[sl]
                        vg = kb.op("scalar", lambda E, psF=psF, pp=pp, f=f: E.activation(
                            b_h[:, pp * 256:(pp + 1) * 256], psF[:, 0:256],
                            AF.Gelu, bias=g_bfccol[:, f:f + 1]),
                            waits=[(sPE, fc_done), (sPE, stB["h_free"][pp])],
                            inc=sAC)
                        stB["psF_free"][sl] = vg
                        if sl == 2:
                            stB["psT_gelu"] = vg
                        vgf = act_fence(b_h[:, pp * 256 + 255: pp * 256 + 256])
                        gelu_done[f] = vgf

                    def emit_proj_u2(f):
                        pp = f % 2
                        wpj = [(sAC, gelu_done[f]), (sDV, wp_gate[f])]
                        if proj_state["first"]:
                            wpj.append((sDV, stB["psA_free"]))
                            for q in range(4):
                                oh = q % 2
                                kb.op("tensor", lambda E, q=q, oh=oh: E.matmul(
                                    psA[:, q * 512:(q + 1) * 512],
                                    g_onesb[0:1, 0:128],
                                    g_bprow2[0:1, oh * 512:(oh + 1) * 512],
                                    start=True, stop=False),
                                    waits=(wpj if q == 0 else []))
                                if q == 0:
                                    wpj = []
                            proj_state["first"] = False
                        for tc2 in range(2):
                            for oh in range(2):
                                q = tc2 * 2 + oh
                                kb.op("tensor", lambda E, q=q, tc2=tc2, oh=oh, pp=pp, f=f: E.matmul(
                                    psA[:, q * 512:(q + 1) * 512],
                                    b_h[:, pp * 256 + tc2 * 128: pp * 256 + (tc2 + 1) * 128],
                                    b_Wp[:, f * C + oh * 512: f * C + (oh + 1) * 512],
                                    start=False, stop=False),
                                    waits=(wpj if q == 0 else []))
                                if q == 0:
                                    wpj = []
                        kb.op("tensor", lambda E, f=f, pp=pp: E.matmul(
                            psU[0:16, 256:512],
                            g_aprojT[:, f * R:(f + 1) * R],
                            b_h[:, pp * 256:(pp + 1) * 256],
                            start=(f == 0), stop=(f == FC - 1)),
                            inc=sPE)
                        stB["h_free"][pp] = sPE.n

                    proj_state["first"] = True
                    fcd = emit_fc(0)
                    emit_gelu(0, fcd)
                    for f in range(1, FC):
                        if f == 1:
                            for epi_fn in stB["lnq"]:
                                epi_fn()
                            stB["lnq"] = []
                        fcd = emit_fc(f)
                        emit_proj_u2(f - 1)
                        emit_gelu(f, fcd)
                    emit_proj_u2(FC - 1)
                    vu2 = kb.op("vector", lambda E: E.tensor_scalar(
                        b_u[0:16, 256:512], psU[0:16, 256:512], 2.0, None,
                        ALU.mult), waits=[(sPE, sPE.n)], inc=sDV)
                    stB["psU_free"] = vu2
                    vu2f = dve_fence(b_u[0:16, 511:512])
                    wl = [(sDV, vu2f)]
                    for tc2 in range(2):
                        for oh in range(2):
                            q = tc2 * 2 + oh
                            kb.op("tensor", lambda E, q=q, tc2=tc2, oh=oh: E.matmul(
                                psA[:, q * 512:(q + 1) * 512],
                                b_u[0:16, 256 + tc2 * 128: 256 + (tc2 + 1) * 128],
                                g_bprojT[0:16, oh * 512:(oh + 1) * 512],
                                start=False, stop=True),
                                waits=(wl if q == 0 else []),
                                inc=(sPE if q == 3 else None))
                            if q == 0:
                                wl = []
                    ffn_done = sPE.n
                    if tix + 1 < len(tiles):
                        pro = tile_prologue(*tiles[tix + 1])

                    def epi(ffn_done=ffn_done, tp=tp, rows0=rows0):
                        st["ln_busy"] = stB["ln_busy"]
                        st["ln_busy_ac"] = stB["ln_busy_ac"]
                        st["xo_free"] = stB["xo_free"]
                        st["sm_free"] = stB["sm_free"]
                        last_apply = 0
                        for tc2 in range(2):
                            va, vo, vadds = ln_and_store(
                                [(psA[:, (tc2 * 2) * 512:(tc2 * 2 + 1) * 512],
                                  [(sPE, ffn_done)]),
                                 (psA[:, (tc2 * 2 + 1) * 512:(tc2 * 2 + 2) * 512], [])],
                                (b_x1f, (tp * 2 + tc2) * C),
                                (d_out, rows0 + tc2 * 128), "ln2", sqbuf=b_z)
                            last_apply = vadds
                            stB["x1f_free"][tp * 2 + tc2] = va
                        stB["psA_free"] = last_apply
                        stB["ln_busy"] = st["ln_busy"]
                        stB["ln_busy_ac"] = st["ln_busy_ac"]
                        stB["xo_free"] = st["xo_free"]
                        stB["sm_free"] = st["sm_free"]
                    stB["lnq"].append(epi)

            for epi_fn in stB["lnq"]:
                epi_fn()
            stB["lnq"] = []

            finals = [(s.h, s.n) for s in pOUT.sems]
            kb.raw("gpsimd", lambda E: [E.wait_ge(h, v) for (h, v) in finals if v > 0])
            kb.emit()

    return nc


_NC_CACHE = [None]


def _masks_host():
    kk = np.arange(128).reshape(128, 1)
    m = np.arange(896).reshape(1, 896)
    return (kk <= m - 384).astype(np.float32).astype(ml_dtypes.bfloat16)


def kernel(L0_attn_x, L0_attn_w_qkv, L0_attn_w_proj, L0_attn_b_proj,
           L0_ffn_W_fc, L0_ffn_b_fc, L0_ffn_A_fc, L0_ffn_B_fc,
           L0_ffn_W_proj, L0_ffn_b_proj, L0_ffn_A_proj, L0_ffn_B_proj):
    f32 = lambda a: np.ascontiguousarray(np.asarray(a, dtype=np.float32))
    x = f32(L0_attn_x).reshape(B, T, C)
    shared = {
        "wqkvT": f32(np.asarray(L0_attn_w_qkv, np.float32).T),
        "wprojT": f32(np.asarray(L0_attn_w_proj, np.float32).T),
        "bprow": f32(L0_attn_b_proj).reshape(1, C),
        "wfcT": f32(np.asarray(L0_ffn_W_fc, np.float32).T),
        "bfccol": f32(np.asarray(L0_ffn_b_fc, np.float32).reshape(FC, 128).T),
        "afcT": f32(np.asarray(L0_ffn_A_fc, np.float32).T),
        "bfcT": f32(np.asarray(L0_ffn_B_fc, np.float32).T),
        "wpF": f32(np.asarray(L0_ffn_W_proj, np.float32).T),
        "bprow2": f32(L0_ffn_b_proj).reshape(1, C),
        "aprojT": f32(np.asarray(L0_ffn_A_proj, np.float32).T),
        "bprojT": f32(np.asarray(L0_ffn_B_proj, np.float32).T),
        "identb": np.eye(128, dtype=ml_dtypes.bfloat16),
        "identf": np.eye(128, dtype=np.float32),
        "onesf": np.ones((1, 128), np.float32),
        "onesb": np.ones((1, 128), ml_dtypes.bfloat16),
        "ones16": np.ones((128, 16), ml_dtypes.bfloat16),
        "masks": _masks_host(),
    }
    if _NC_CACHE[0] is None:
        _NC_CACHE[0] = build_program()
    nc = _NC_CACHE[0]
    in_maps = []
    for c in range(NCORES):
        m = dict(shared)
        m["x"] = np.ascontiguousarray(
            x[c * SEQ:(c + 1) * SEQ].reshape(ROWS, C))
        m["xT"] = np.ascontiguousarray(m["x"].T)
        in_maps.append(m)
    res = run_bass_kernel_spmd(nc, in_maps, core_ids=list(range(NCORES)))
    out = np.concatenate(
        [np.asarray(r["out"], np.float32).reshape(SEQ, T, C)
         for r in res.results], axis=0)
    return out


# version 29
# speedup vs baseline: 1.0184x; 1.0184x over previous
import numpy as np
import ml_dtypes
from contextlib import ExitStack

import concourse.bass as bass
import concourse.mybir as mybir
from concourse.bass_utils import run_bass_kernel_spmd

F32 = mybir.dt.float32
BF16 = mybir.dt.bfloat16
I32 = mybir.dt.int32
AF = mybir.ActivationFunctionType
ALU = mybir.AluOpType
AX = mybir.AxisListType.X

B, T, C = 16, 2048, 1024
H, HS = 16, 64
FF, R = 4096, 16
NCORES = 8
SEQ = 2
ROWS = SEQ * T
CC = C // 128
FC = FF // 128


class Sem:
    def __init__(self, es, nc, name):
        self.h = es.enter_context(nc.semaphore(name))
        self.n = 0


class SemPool:

    def __init__(self, es, nc, name, n):
        self.sems = [Sem(es, nc, f"{name}{i}") for i in range(n)]
        self.i = 0

    def take(self):
        s = self.sems[self.i % len(self.sems)]
        self.i += 1
        return s


class KB:
    ENGINES = ("sync", "gpsimd", "vector", "scalar", "tensor")

    def __init__(self, nc):
        self.nc = nc
        self.prog = {e: [] for e in self.ENGINES}
        self.serial = False
        self.esem = {}
        self.pools = {}

    def dma(self, eng, fn, waits=(), pool=None):
        s = self.pools[pool or eng].take()
        w = list(waits) + [(s, s.n)]
        self.op(eng, fn, waits=w, inc=s, inck=16)
        return (s, s.n)

    def op(self, eng, fn, waits=(), inc=None, inck=1):
        if self.serial and self.esem:
            waits = list(waits) + [(s, s.n) for e2, s in self.esem.items()
                                   if e2 != eng]
            if inc is None and eng in self.esem:
                inc = self.esem[eng]
                inck = 1
        if eng in ("vector", "scalar", "gpsimd") and eng in self.esem:
            waits = list(waits) + [(self.esem[eng], self.esem[eng].n)]
            if inc is None:
                inc = self.esem[eng]
                inck = 1
        waits = [(s, v) for (s, v) in waits if s is not None]
        w = tuple((s.h, v) for (s, v) in waits)
        if inc is not None:
            inc.n += inck

        def run(E, fn=fn, w=w, inc=inc, inck=inck):
            for (h, v) in w:
                if callable(v):
                    v = v()
                if v > 0:
                    E.wait_ge(h, v)
            inst = fn(E)
            if inc is not None:
                inst.then_inc(inc.h, inck)

        self.prog[eng].append(run)
        return inc.n if inc is not None else None

    def raw(self, eng, fn):
        self.prog[eng].append(fn)

    def emit(self):
        with self.nc.Block() as block:
            for eng in self.ENGINES:
                fns = self.prog[eng]
                if not fns:
                    continue

                def section(E, fns=fns):
                    for f in fns:
                        f(E)

                getattr(block, eng)(section)
        self.prog = {e: [] for e in self.ENGINES}


def build_program():
    nc = bass.Bass("TRN2", target_bir_lowering=False)

    d_x = nc.dram_tensor("x", [ROWS, C], F32, kind="ExternalInput").ap()
    d_xT = nc.dram_tensor("xT", [C, ROWS], F32, kind="ExternalInput").ap()
    d_wqkvT = nc.dram_tensor("wqkvT", [C, 3 * C], F32, kind="ExternalInput").ap()
    d_wprojT = nc.dram_tensor("wprojT", [C, C], F32, kind="ExternalInput").ap()
    d_bprow = nc.dram_tensor("bprow", [1, C], F32, kind="ExternalInput").ap()
    d_wfcT = nc.dram_tensor("wfcT", [C, FF], F32, kind="ExternalInput").ap()
    d_bfccol = nc.dram_tensor("bfccol", [128, FC], F32, kind="ExternalInput").ap()
    d_afcT = nc.dram_tensor("afcT", [C, R], F32, kind="ExternalInput").ap()
    d_bfcT = nc.dram_tensor("bfcT", [R, FF], F32, kind="ExternalInput").ap()
    d_wpF = nc.dram_tensor("wpF", [FF, C], F32, kind="ExternalInput").ap()
    d_bprow2 = nc.dram_tensor("bprow2", [1, C], F32, kind="ExternalInput").ap()
    d_aprojT = nc.dram_tensor("aprojT", [FF, R], F32, kind="ExternalInput").ap()
    d_bprojT = nc.dram_tensor("bprojT", [R, C], F32, kind="ExternalInput").ap()
    d_identb = nc.dram_tensor("identb", [128, 128], BF16, kind="ExternalInput").ap()
    d_identf = nc.dram_tensor("identf", [128, 128], F32, kind="ExternalInput").ap()
    d_onesf = nc.dram_tensor("onesf", [1, 128], F32, kind="ExternalInput").ap()
    d_ones16 = nc.dram_tensor("ones16", [128, 16], BF16, kind="ExternalInput").ap()
    d_onesb = nc.dram_tensor("onesb", [1, 128], BF16, kind="ExternalInput").ap()
    d_masks = nc.dram_tensor("masks", [128, 896], BF16, kind="ExternalInput").ap()
    d_out = nc.dram_tensor("out", [ROWS, C], F32, kind="ExternalOutput").ap()
    d_x1 = nc.dram_tensor("x1scr", [ROWS, C], F32).ap()
    d_scr8 = nc.dram_tensor("scr8", [1, 8], F32).ap()
    d_recb = nc.dram_tensor("recb", [1, 512], F32).ap()

    es = ExitStack()
    with es:
        pIN = SemPool(es, nc, "dIN", 4)
        pGP = SemPool(es, nc, "dGP", 2)
        pOUT = SemPool(es, nc, "dOUT", 2)
        NONE = (None, 0)
        sPE = Sem(es, nc, "sPE")
        sDV = Sem(es, nc, "sDV")
        sAC = Sem(es, nc, "sAC")
        sPL = Sem(es, nc, "sPL")

        g_identb = es.enter_context(nc.sbuf_tensor("g_identb", [128, 128], BF16))
        g_identf = es.enter_context(nc.sbuf_tensor("g_identf", [128, 128], F32))
        g_onesf = es.enter_context(nc.sbuf_tensor("g_onesf", [1, 128], F32))

        g_bprow2 = es.enter_context(nc.sbuf_tensor("g_bprow2", [1, C], BF16))
        g_onesb = es.enter_context(nc.sbuf_tensor("g_onesb", [1, 128], BF16))
        g_bfccol = es.enter_context(nc.sbuf_tensor("g_bfccol", [128, FC], F32))
        g_afcT = es.enter_context(nc.sbuf_tensor("g_afcT", [128, CC * R], BF16))
        g_bfcT = es.enter_context(nc.sbuf_tensor("g_bfcT", [R, FF], BF16))
        g_aprojT = es.enter_context(nc.sbuf_tensor("g_aprojT", [128, FC * R], BF16))
        g_bprojT = es.enter_context(nc.sbuf_tensor("g_bprojT", [R, C], BF16))
        g_pre = es.enter_context(nc.sbuf_tensor("g_pre", [128, C], F32))
        g_xo = es.enter_context(nc.sbuf_tensor("g_xo", [128, C], F32))
        g_sm = es.enter_context(nc.sbuf_tensor("g_sm", [128, 16], F32))

        g_srow = es.enter_context(nc.sbuf_tensor("g_srow", [1, 8], F32))
        g_scl = es.enter_context(nc.sbuf_tensor("g_scl", [128, 8], F32))
        g_am = es.enter_context(nc.sbuf_tensor("g_am", [128, 2], F32))
        g_scr = es.enter_context(nc.sbuf_tensor("g_scr", [128, 4], F32))

        psA = es.enter_context(nc.psum_tensor("psA", [128, 2048], F32))
        psO = es.enter_context(nc.psum_tensor("psO", [128, 512], F32))
        psBC = es.enter_context(nc.psum_tensor("psBC", [128, 512], F32))
        psT = es.enter_context(nc.psum_tensor("psT", [128, 512], F32))
        psU = es.enter_context(nc.psum_tensor("psU", [128, 512], F32))

        kb = KB(nc)
        kb.esem = {"tensor": sPE, "vector": sDV, "scalar": sAC,
                   "gpsimd": sPL}
        kb.pools = {"sync": pIN, "gpsimd": pGP, "out": pOUT}
        import os
        kb.serial = bool(int(os.environ.get("KSERIAL", "0")))
        _conc = set(os.environ.get("KCONC", "").split(","))

        def region(name):
            if os.environ.get("KSERIAL", "0") == "2":
                kb.serial = name not in _conc

        def dve_fence(ap):
            return sDV.n

        def act_fence(ap):
            return sAC.n

        setup_pairs = [(g_identb, d_identb), (g_identf, d_identf),
                       (g_onesf, d_onesf), (g_bfccol, d_bfccol),
                       (g_onesb, d_onesb)]
        for (dst, src) in setup_pairs:
            kb.dma("sync", lambda E, dst=dst, src=src:
                   E.dma_start(out=dst[:, :], in_=src))


        stg_state = {"free": [NONE, NONE]}

        def staged_cast(dst_ap, src_ap, n):
            slot = stg_state.setdefault("slot", 0)
            stg_state["slot"] = 1 - slot
            vx = kb.dma("sync", lambda E, slot=slot, src_ap=src_ap, n=n: E.dma_start(
                out=g_wstg[0:src_ap.shape[0], slot * 512: slot * 512 + n],
                in_=src_ap),
                waits=[stg_state["free"][slot]])
            vc = kb.op("vector", lambda E, dst_ap=dst_ap, slot=slot, n=n, src_ap=src_ap:
                       E.tensor_copy(dst_ap,
                                     g_wstg[0:src_ap.shape[0], slot * 512: slot * 512 + n]),
                       waits=[vx], inc=sDV)
            stg_state["free"][slot] = (sDV, vc)
            return vc


        esA = ExitStack()
        with esA:
            a_wqkv = esA.enter_context(nc.sbuf_tensor("a_wqkv", [128, CC * 3 * C], BF16))
            a_wproj = esA.enter_context(nc.sbuf_tensor("a_wproj", [128, CC * C], BF16))
            a_K = esA.enter_context(nc.sbuf_tensor("a_K", [128, CC * T], BF16))
            a_V = esA.enter_context(nc.sbuf_tensor("a_V", [128, 16 * (H * 65)], BF16))
            a_xT = esA.enter_context(nc.sbuf_tensor("a_xT", [128, CC * 512], BF16))
            a_xf = esA.enter_context(nc.sbuf_tensor("a_xf", [128, 2 * C], F32))
            a_Q = esA.enter_context(nc.sbuf_tensor("a_Q", [128, CC * 512], BF16))
            a_tE = esA.enter_context(nc.sbuf_tensor("a_tE", [128, 1024], F32))
            a_p2 = esA.enter_context(nc.sbuf_tensor("a_p2", [128, 2048], BF16))
            a_ph = esA.enter_context(nc.sbuf_tensor("a_ph", [128, 1024], BF16))
            a_O = esA.enter_context(nc.sbuf_tensor("a_O", [128, CC * 512], BF16))
            g_wstg = esA.enter_context(nc.sbuf_tensor("g_wstg", [128, 1024], F32))
            g_masks = esA.enter_context(nc.sbuf_tensor("g_masks", [128, 896], BF16))
            g_bprow = esA.enter_context(nc.sbuf_tensor("g_bprow", [1, C], BF16))
            g_ones16 = esA.enter_context(nc.sbuf_tensor("g_ones16", [128, 16], BF16))
            g_rec = esA.enter_context(nc.sbuf_tensor("g_rec", [1, 1024], F32))
            g_bc = esA.enter_context(nc.sbuf_tensor("g_bc", [64, 512], F32))
            for (dst, src) in [(g_ones16, d_ones16), (g_masks, d_masks)]:
                kb.dma("sync", lambda E, dst=dst, src=src:
                       E.dma_start(out=dst[:, :], in_=src))
            consts_waits = [(s, s.n) for s in pIN.sems]
            for cc in range(CC):
                staged_cast(g_afcT[:, cc * R:(cc + 1) * R],
                            d_afcT[cc * 128:(cc + 1) * 128, :], R)
            for hh in range(FF // 512):
                staged_cast(g_bfcT[0:16, hh * 512:(hh + 1) * 512],
                            d_bfcT[:, hh * 512:(hh + 1) * 512], 512)
            for fc in range(FC):
                staged_cast(g_aprojT[:, fc * R:(fc + 1) * R],
                            d_aprojT[fc * 128:(fc + 1) * 128, :], R)
            for hh in range(C // 512):
                staged_cast(g_bprojT[0:16, hh * 512:(hh + 1) * 512],
                            d_bprojT[:, hh * 512:(hh + 1) * 512], 512)
            for hh in range(C // 512):
                staged_cast(g_bprow[0:1, hh * 512:(hh + 1) * 512],
                            d_bprow[:, hh * 512:(hh + 1) * 512], 512)
                staged_cast(g_bprow2[0:1, hh * 512:(hh + 1) * 512],
                            d_bprow2[:, hh * 512:(hh + 1) * 512], 512)
            lora_dv = sDV.n

            for cc in range(CC):
                for hh in range(3 * C // 512):
                    staged_cast(a_wqkv[:, cc * 3 * C + hh * 512: cc * 3 * C + (hh + 1) * 512],
                                d_wqkvT[cc * 128:(cc + 1) * 128, hh * 512:(hh + 1) * 512], 512)
            for cc in range(CC):
                for hh in range(C // 512):
                    staged_cast(a_wproj[:, cc * C + hh * 512: cc * C + (hh + 1) * 512],
                                d_wprojT[cc * 128:(cc + 1) * 128, hh * 512:(hh + 1) * 512], 512)
            wqkv_dv = dve_fence(a_wproj[:, CC * C - 1: CC * C])

            st = {
                "xT_rd": 0,
                "psA_free": [0, 0, 0, 0],
                "psO_free": 0,
                "psU_free": 0,
                "psBC_free": 0,
                "xf_free": [0, 0],
                "ln_busy": 0,
                "ln_busy_ac": 0,
                "xo_free": NONE,
                "sm_free": 0,
            }

            def load_xT(t0, extra_waits=(), tag="R1"):
                region(tag)
                last = 0
                for cc in range(CC):
                    slot = stg_state.setdefault("slot", 0)
                    stg_state["slot"] = 1 - slot
                    vx = kb.dma("sync", lambda E, cc=cc, slot=slot, t0=t0: E.dma_start(
                        out=g_wstg[:, slot * 512:(slot + 1) * 512],
                        in_=d_xT[cc * 128:(cc + 1) * 128, t0:t0 + 512]),
                        waits=[stg_state["free"][slot],
                               (sPE, st["xT_rd"])] + (list(extra_waits) if cc == 0 else []))
                    vc = kb.op("vector", lambda E, cc=cc, slot=slot: E.tensor_copy(
                        a_xT[:, cc * 512:(cc + 1) * 512],
                        g_wstg[:, slot * 512:(slot + 1) * 512]),
                        waits=[vx], inc=sDV)
                    stg_state["free"][slot] = (sDV, vc)
                    last = dve_fence(a_xT[:, cc * 512 + 511: cc * 512 + 512])
                return last

            def load_xT_pool(t0):
                last = 0
                for cc in range(CC):
                    slot = stg_state.setdefault("slot", 0)
                    stg_state["slot"] = 1 - slot
                    vx = kb.dma("sync", lambda E, cc=cc, slot=slot, t0=t0: E.dma_start(
                        out=g_wstg[:, slot * 512:(slot + 1) * 512],
                        in_=d_xT[cc * 128:(cc + 1) * 128, t0:t0 + 512]),
                        waits=[stg_state["free"][slot], (sPE, st["xT_rd"])])
                    vc = kb.op("gpsimd", lambda E, cc=cc, slot=slot: E.tensor_copy(
                        a_xT[:, cc * 512:(cc + 1) * 512],
                        g_wstg[:, slot * 512:(slot + 1) * 512]),
                        waits=[vx], inc=sPL)
                    stg_state["free"][slot] = (sPL, vc)
                    last = vc
                return (sPL, last)

            def ln_and_store(ps_halves_waits, resid_cols, dst_rows, tag,
                             sqbuf=None):
                rbuf, rcol = resid_cols
                adds = []
                for oh in range(2):
                    ps_ap, w = ps_halves_waits[oh]
                    w = list(w)
                    if oh == 0:
                        w += [(sAC, st["ln_busy_ac"]), (sDV, st["ln_busy"])]
                    va = kb.op("vector", lambda E, ps_ap=ps_ap, oh=oh, rbuf=rbuf, rcol=rcol:
                               E.tensor_tensor(
                                   g_pre[:, oh * 512:(oh + 1) * 512], ps_ap,
                                   rbuf[:, rcol + oh * 512: rcol + (oh + 1) * 512],
                                   ALU.add),
                               waits=w, inc=sDV)
                    adds.append(va)
                vaddf = dve_fence(g_pre[:, 1023:1024])
                v1 = kb.op("vector", lambda E: E.tensor_reduce(
                    g_sm[:, 0:1], g_pre[:, :], AX, ALU.add),
                    waits=[(sDV, st["sm_free"])], inc=sDV)
                if sqbuf is None:
                    vsq = kb.op("gpsimd", lambda E: E.tensor_tensor(
                        g_xo[:, :], g_pre[:, :], g_pre[:, :], ALU.mult),
                        waits=[(sDV, vaddf), st["xo_free"]], inc=sPL)
                    v2 = kb.op("vector", lambda E: E.tensor_reduce(
                        g_sm[:, 1:2], g_xo[:, :], AX, ALU.add),
                        waits=[(sPL, vsq)], inc=sDV)
                else:
                    sq = sqbuf
                    vsq = kb.op("gpsimd", lambda E, sq=sq: E.tensor_tensor(
                        sq[:, 0:C], g_pre[:, :], g_pre[:, :], ALU.mult),
                        waits=[(sDV, vaddf)], inc=sPL)
                    v2 = kb.op("vector", lambda E, sq=sq: E.tensor_reduce(
                        g_sm[:, 1:2], sq[:, 0:C], AX, ALU.add),
                        waits=[(sPL, vsq)], inc=sDV)
                v3 = kb.op("vector", lambda E: E.tensor_scalar(
                    g_sm[:, 2:3], g_sm[:, 0:1], 1.0 / C, None, ALU.mult),
                    waits=[(sDV, v2)], inc=sDV)
                v4 = kb.op("vector", lambda E: E.tensor_tensor(
                    g_sm[:, 3:4], g_sm[:, 2:3], g_sm[:, 2:3], ALU.mult),
                    waits=[(sDV, v3)], inc=sDV)
                v5 = kb.op("vector", lambda E: E.scalar_tensor_tensor(
                    g_sm[:, 4:5], g_sm[:, 1:2], 1.0 / C, g_sm[:, 3:4],
                    ALU.mult, ALU.subtract),
                    waits=[(sDV, v4)], inc=sDV)
                v6 = kb.op("vector", lambda E: E.tensor_scalar(
                    g_sm[:, 4:5], g_sm[:, 4:5], 1.0, 1e-5, ALU.mult, ALU.add),
                    waits=[(sDV, v5)], inc=sDV)
                v6f = dve_fence(g_sm[:, 4:5])
                v7 = kb.op("scalar", lambda E: E.activation(
                    g_sm[:, 5:6], g_sm[:, 4:5], AF.Ln),
                    waits=[(sDV, v6f)], inc=sAC)
                v8 = kb.op("scalar", lambda E: E.activation(
                    g_sm[:, 6:7], g_sm[:, 5:6], AF.Exp, scale=-0.5),
                    waits=[(sAC, v7)], inc=sAC)
                v8f = act_fence(g_sm[:, 6:7])
                st["ln_busy_ac"] = v8f
                v9 = kb.op("vector", lambda E: E.tensor_scalar(
                    g_xo[:, :], g_pre[:, :], g_sm[:, 2:3], g_sm[:, 6:7],
                    ALU.subtract, ALU.mult),
                    waits=[(sAC, v8f), st["xo_free"]], inc=sDV)
                st["ln_busy"] = v9
                st["sm_free"] = v9
                v9f = dve_fence(g_xo[:, 1023:1024])
                dram, row0 = dst_rows
                vo = kb.dma("gpsimd", lambda E, dram=dram, row0=row0: E.dma_start(
                    out=dram[row0:row0 + 128, :], in_=g_xo[:, :]),
                    waits=[(sDV, v9f)], pool="out")
                st["xo_free"] = vo
                return v9, vo, adds[1]

            kv_ready = 0
            scores_last = 0

            for s in range(SEQ):
                base = s * T
                kv_war = [(sPE, sPE.n)] if s > 0 else []
                region("R1")
                for tg in range(4):
                    w = kv_war if tg == 0 else []
                    xw = ([(sDV, wqkv_dv)] + consts_waits) if (s == 0 and tg == 0) else []
                    xtg_ready = load_xT(base + tg * 512, extra_waits=w + xw)
                    for kf in range(CC):
                        bank = kf % 4
                        w = [(sDV, xtg_ready), (sDV, st["psA_free"][bank])]
                        if s == 0 and tg == 0 and kf == 0:
                            w.append((sDV, wqkv_dv))
                        for cc in range(CC):
                            kb.op("tensor", lambda E, bank=bank, cc=cc, kf=kf: E.matmul(
                                psA[:, bank * 512:(bank + 1) * 512],
                                a_wqkv[:, cc * 3 * C + C + kf * 128: cc * 3 * C + C + (kf + 1) * 128],
                                a_xT[:, cc * 512:(cc + 1) * 512],
                                start=(cc == 0), stop=(cc == CC - 1)),
                                waits=(w if cc == 0 else []),
                                inc=(sPE if cc == CC - 1 else None))
                            if cc == 0:
                                w = []
                        vk = kb.op("vector", lambda E, bank=bank, kf=kf, tg=tg: E.tensor_copy(
                            a_K[:, kf * T + tg * 512: kf * T + (tg + 1) * 512],
                            psA[:, bank * 512:(bank + 1) * 512]),
                            waits=[(sPE, sPE.n)], inc=sDV)
                        st["psA_free"][bank] = vk
                        dve_fence(a_K[:, kf * T + (tg + 1) * 512 - 1: kf * T + (tg + 1) * 512])
                    for tc in range(4):
                        tch = tg * 4 + tc
                        for fh in range(2):
                            bank = (tc * 2 + fh) % 4
                            w = [(sDV, st["psA_free"][bank])]
                            for cc in range(CC):
                                kb.op("tensor", lambda E, bank=bank, cc=cc, tc=tc, fh=fh: E.matmul(
                                    psA[:, bank * 512:(bank + 1) * 512],
                                    a_xT[:, cc * 512 + tc * 128: cc * 512 + (tc + 1) * 128],
                                    a_wqkv[:, cc * 3 * C + 2 * C + fh * 512: cc * 3 * C + 2 * C + (fh + 1) * 512],
                                    start=(cc == 0), stop=(cc == CC - 1)),
                                    waits=(w if cc == 0 else []),
                                    inc=(sPE if cc == CC - 1 else None))
                                if cc == 0:
                                    w = []
                            vv = kb.op("vector", lambda E, bank=bank, tch=tch, fh=fh: E.tensor_copy(
                                a_V[:, tch * (H * 65) + fh * 520: tch * (H * 65) + (fh + 1) * 520]
                                .rearrange("p (h d) -> p h d", h=8)[:, :, 0:64],
                                psA[:, bank * 512:(bank + 1) * 512]
                                .rearrange("p (h d) -> p h d", h=8)),
                                waits=[(sPE, sPE.n)], inc=sDV)
                            st["psA_free"][bank] = vv
                            st["xT_rd"] = sPE.n
                            dve_fence(a_V[:, tch * (H * 65) + (fh + 1) * 520 - 2: tch * (H * 65) + (fh + 1) * 520 - 1])
                        kb.op("vector", lambda E, tch=tch: E.tensor_copy(
                            a_V[:, tch * (H * 65): (tch + 1) * (H * 65)]
                            .rearrange("p (h d) -> p h d", h=16)[:, :, 64:65],
                            g_ones16[:, :].rearrange("p (h o) -> p h o", o=1)),
                            waits=(consts_waits if (s == 0 and tg == 0 and tc == 0) else []),
                            inc=sDV)
                        dve_fence(a_V[:, tch * (H * 65) + 15 * 65 + 64: tch * (H * 65) + 15 * 65 + 65])
                dve_fence(a_K[:, CC * T - 1: CC * T])
                kv_ready = dve_fence(a_V[:, 16 * H * 65 - 2: 16 * H * 65 - 1])

                for j in range(4):
                    q0 = base + j * 512
                    if j == 0:
                        xtq_ready = load_xT(q0, tag="R2a")
                        region("R2b")
                        for qf in range(CC):
                            bank = qf % 4
                            w = [(sDV, st["psA_free"][bank]),
                                 (sAC, st.get("exp_bank", [0, 0])[bank // 2])]
                            if qf == 0:
                                w.append((sDV, xtq_ready))
                            for cc in range(CC):
                                kb.op("tensor", lambda E, bank=bank, cc=cc, qf=qf: E.matmul(
                                    psA[:, bank * 512:(bank + 1) * 512],
                                    a_wqkv[:, cc * 3 * C + qf * 128: cc * 3 * C + (qf + 1) * 128],
                                    a_xT[:, cc * 512:(cc + 1) * 512],
                                    start=(cc == 0), stop=(cc == CC - 1)),
                                    waits=(w if cc == 0 else []),
                                    inc=(sPE if cc == CC - 1 else None))
                                if cc == 0:
                                    w = []
                            st["xT_rd"] = sPE.n
                            vq = kb.op("vector", lambda E, bank=bank, qf=qf: E.tensor_copy(
                                a_Q[:, qf * 512:(qf + 1) * 512],
                                psA[:, bank * 512:(bank + 1) * 512]),
                                waits=[(sPE, sPE.n)], inc=sDV)
                            st["psA_free"][bank] = vq
                            dve_fence(a_Q[:, qf * 512 + 511: qf * 512 + 512])
                        st["q_rdy"] = (sDV, sDV.n)
                    q_ready_t = st["q_rdy"]

                    xf_dma = [NONE] * 4
                    for tc in range(2):
                        xf_dma[tc] = kb.dma("sync", lambda E, tc=tc, q0=q0: E.dma_start(
                            out=a_xf[:, (tc % 2) * C:((tc % 2) + 1) * C],
                            in_=d_x[q0 + tc * 128: q0 + (tc + 1) * 128, :]),
                            waits=[(sDV, st["xf_free"][tc % 2])])

                    region("R3")
                    jobs = []
                    for h in range(H):
                        for gg in range(2 * (j + 1)):
                            jobs.append((h, gg))
                    LA = 2
                    sq_of = {}
                    avpe_of = {}
                    pending = []

                    weave = j < 3
                    if weave:
                        xtq_next = load_xT_pool(base + (j + 1) * 512)

                    def emit_qb_chunk(k, scores_pe):
                        wq0 = [xtq_next, (sAC, st.get("psT_q", 0)),
                               (sPE, scores_pe)]
                        for cc in range(CC):
                            kb.op("tensor", lambda E, cc=cc, k=k: E.matmul(
                                psT[:, 0:512],
                                a_wqkv[:, cc * 3 * C + k * 128: cc * 3 * C + (k + 1) * 128],
                                a_xT[:, cc * 512:(cc + 1) * 512],
                                start=(cc == 0), stop=(cc == CC - 1)),
                                waits=(wq0 if cc == 0 else []),
                                inc=(sPE if cc == CC - 1 else None))
                        st["xT_rd"] = sPE.n
                        vq = kb.op("scalar", lambda E, k=k: E.activation(
                            a_Q[:, k * 512:(k + 1) * 512], psT[:, 0:512],
                            AF.Identity),
                            waits=[(sPE, sPE.n)], inc=sAC)
                        st["psT_q"] = vq
                        if k == CC - 1:
                            st["q_rdy"] = (sAC, vq)

                    def emit_scores_softplus(idx):
                        h, gg = jobs[idx]
                        pb = (h % 2) * 64
                        kchunk = h // 2
                        pp = idx % 2
                        w0 = [(sAC, st.get("exp_bank", [0, 0])[pp])]
                        if idx < 2:
                            w0 += [q_ready_t, (sDV, kv_ready)]
                        for r2 in range(2):
                            kc = 2 * gg + r2
                            kb.op("tensor", lambda E, r2=r2, kc=kc, pb=pb, kchunk=kchunk, pp=pp: E.matmul(
                                psA[:, (2 * pp + r2) * 512:(2 * pp + r2 + 1) * 512],
                                a_K[pb:pb + 64, kchunk * T + kc * 128: kchunk * T + (kc + 1) * 128],
                                a_Q[pb:pb + 64, kchunk * 512:(kchunk + 1) * 512],
                                start=True, stop=True),
                                waits=(w0 if r2 == 0 else []),
                                inc=(sPE if r2 == 1 else None))
                        scores_done = sPE.n
                        if weave and h % 2 == 1 and gg == 2 * (j + 1) - 1:
                            emit_qb_chunk(h // 2, scores_done)
                        ve = kb.op("scalar", lambda E, pp=pp: E.activation(
                            a_tE[:, :], psA[:, pp * 1024:(pp + 1) * 1024],
                            AF.Exp, scale=0.125),
                            waits=[(sPE, scores_done)], inc=sAC)
                        st.setdefault("exp_bank", [0, 0])[pp] = ve
                        vl = kb.op("scalar", lambda E: E.activation(
                            a_ph[:, :], a_tE[:, :], AF.Ln, bias=1.0),
                            waits=[(sDV, st.get("ph_sq", 0))], inc=sAC)
                        vsq = kb.op("vector", lambda E, pp=pp: E.tensor_tensor(
                            a_p2[:, pp * 1024:(pp + 1) * 1024],
                            a_ph[:, :], a_ph[:, :], ALU.mult),
                            waits=[(sAC, vl),
                                   (sPE, avpe_of[idx - 2] if idx >= 2 else 0)],
                            inc=sDV)
                        st["ph_sq"] = vsq
                        ngg = 2 * (j + 1)
                        if gg >= ngg - 2:
                            for r2 in range(2):
                                r4 = 2 * gg + r2 - 4 * j
                                vsq = kb.op("vector", lambda E, pp=pp, r2=r2, r4=r4: E.tensor_tensor(
                                    a_p2[:, pp * 1024 + r2 * 512: pp * 1024 + (r2 + 1) * 512],
                                    a_p2[:, pp * 1024 + r2 * 512: pp * 1024 + (r2 + 1) * 512],
                                    g_masks[:, 384 - 128 * r4: 896 - 128 * r4],
                                    ALU.mult), inc=sDV)
                        sq_of[idx] = vsq

                    def emit_av(idx):
                        h, gg = jobs[idx]
                        pp = idx % 2
                        psAV = psO if h % 2 == 0 else psU
                        avfree_key = "psO_free" if h % 2 == 0 else "psU_free"
                        wav = [(sDV, sq_of[idx])]
                        if gg == 0:
                            wav.append((sDV, st[avfree_key]))
                        for r2 in range(2):
                            kc = 2 * gg + r2
                            kb.op("tensor", lambda E, kc=kc, r2=r2, h=h, pp=pp, psAV=psAV: E.matmul(
                                psAV[0:65, :],
                                a_V[:, kc * (H * 65) + h * 65: kc * (H * 65) + (h + 1) * 65],
                                a_p2[:, pp * 1024 + r2 * 512: pp * 1024 + (r2 + 1) * 512],
                                start=(kc == 0), stop=(kc == 4 * j + 3),
                                skip_group_check=True),
                                waits=(wav if r2 == 0 else []),
                                inc=(sPE if r2 == 1 else None))
                        avpe_of[idx] = sPE.n
                        if gg == 2 * (j + 1) - 1:
                            for (rid, fn) in list(pending):
                                pending.remove((rid, fn))
                                fn()
                            emit_norm_part1(idx, h)

                    def emit_norm_part1(idx, h):
                        psAV = psO if h % 2 == 0 else psU
                        av_done = avpe_of[idx]
                        vr1 = kb.op("vector", lambda E, psAV=psAV: E.tensor_scalar(
                            g_rec[0:1, 0:512], psAV[64:65, :], 1e-9, None, ALU.add),
                            waits=[(sPE, av_done),
                                   (sAC, st.get("rec_ln_rd", 0)),
                                   st.get("bc_dma1", NONE)], inc=sDV)
                        vl1 = kb.op("scalar", lambda E: E.activation(
                            g_rec[0:1, 512:1024], g_rec[0:1, 0:512], AF.Ln),
                            waits=[(sDV, vr1)], inc=sAC)
                        st["rec_ln_rd"] = vl1
                        vl2 = kb.op("scalar", lambda E: E.activation(
                            g_rec[0:1, 0:512], g_rec[0:1, 512:1024],
                            AF.Exp, scale=-1.0), inc=sAC)
                        vl2f = act_fence(g_rec[0:1, 511:512])
                        pending.append((idx + 3,
                                        lambda h=h, vl2f=vl2f: emit_norm_part2(h, vl2f)))

                    def emit_norm_part2(h, vl2f):
                        psAV = psO if h % 2 == 0 else psU
                        avfree_key = "psO_free" if h % 2 == 0 else "psU_free"
                        pb = (h % 2) * 64
                        kchunk = h // 2
                        d1 = kb.dma("sync", lambda E: E.dma_start(
                            out=d_recb, in_=g_rec[0:1, 0:512]),
                            waits=[(sAC, vl2f)])
                        st["bc_dma1"] = d1
                        d2 = kb.dma("sync", lambda E: E.dma_start(
                            out=g_bc[:, :], in_=d_recb.broadcast_to([64, 512])),
                            waits=[d1, (sDV, st.get("bc_free", 0))])
                        vm = kb.op("vector", lambda E, pb=pb, kchunk=kchunk, psAV=psAV: E.tensor_tensor(
                            a_O[pb:pb + 64, kchunk * 512:(kchunk + 1) * 512],
                            psAV[0:64, :], g_bc[:, :], ALU.mult),
                            waits=[d2], inc=sDV)
                        st[avfree_key] = vm
                        st["bc_free"] = vm

                    n = len(jobs)
                    for idx in range(n + LA):
                        for (rid, fn) in [p for p in pending if p[0] <= idx]:
                            pending.remove((rid, fn))
                            fn()
                        if LA <= idx < n + LA:
                            emit_av(idx - LA)
                        if idx < n:
                            emit_scores_softplus(idx)
                    for (rid, fn) in list(pending):
                        pending.remove((rid, fn))
                        fn()
                    assert not pending
                    O_ready = sDV.n

                    region("R5")
                    for tc in range(4):
                        pair = (psO, psBC)
                        fkeys = ("psO_free", "psBC_free")
                        proj_done = [0, 0]
                        for oh in range(2):
                            ps = pair[oh]
                            wfree = [(sDV, st.get(fkeys[oh], 0))]
                            w = list(wfree)
                            if tc == 0 and oh == 0:
                                w.append((sDV, O_ready))
                            if s == 0 and j == 0 and tc == 0:
                                w.extend(consts_waits)
                            kb.op("tensor", lambda E, ps=ps, oh=oh: E.matmul(
                                ps[:, :], g_onesb[0:1, 0:128],
                                g_bprow[0:1, oh * 512:(oh + 1) * 512],
                                start=True, stop=False), waits=w, inc=None)
                            for cc in range(CC):
                                kb.op("tensor", lambda E, ps=ps, cc=cc, tc=tc, oh=oh: E.matmul(
                                    ps[:, :],
                                    a_O[:, cc * 512 + tc * 128: cc * 512 + (tc + 1) * 128],
                                    a_wproj[:, cc * C + oh * 512: cc * C + (oh + 1) * 512],
                                    start=False, stop=(cc == CC - 1)),
                                    inc=(sPE if cc == CC - 1 else None))
                            proj_done[oh] = sPE.n
                        slot = tc % 2
                        va, vo, vadds = ln_and_store(
                            [(pair[0][:, :], [(sPE, proj_done[0]), xf_dma[tc]]),
                             (pair[1][:, :], [(sPE, proj_done[1])])],
                            (a_xf, slot * C),
                            (d_x1, q0 + tc * 128), "ln1", sqbuf=None)
                        st[fkeys[0]] = vadds
                        st[fkeys[1]] = vadds
                        st["xf_free"][slot] = vadds
                        if tc + 2 < 4:
                            xf_dma[tc + 2] = kb.dma("sync", lambda E, tc=tc, q0=q0, slot=slot: E.dma_start(
                                out=a_xf[:, slot * C:(slot + 1) * C],
                                in_=d_x[q0 + (tc + 2) * 128: q0 + (tc + 3) * 128, :]),
                                waits=[(sDV, va)])

            attn_done = dict(pe=sPE.n, dv=sDV.n, ac=sAC.n,
                             do=[(s, s.n) for s in pOUT.sems])
            kb.emit()

        esB = ExitStack()
        with esB:
            b_Wfc = esB.enter_context(nc.sbuf_tensor("b_Wfc", [128, CC * FF], BF16))
            b_Wp = esB.enter_context(nc.sbuf_tensor("b_Wp", [128, FC * C], BF16))
            b_ws = esB.enter_context(nc.sbuf_tensor("b_ws", [128, 2 * 1024], F32))
            b_z = esB.enter_context(nc.sbuf_tensor("b_z", [128, 1024], F32))
            b_zi = esB.enter_context(nc.sbuf_tensor("b_zi", [128, 1024], I32))
            b_d = esB.enter_context(nc.sbuf_tensor("b_d", [128, 1024], F32))
            b_r = esB.enter_context(nc.sbuf_tensor("b_r", [128, 1024], F32))
            b_x1f = esB.enter_context(nc.sbuf_tensor("b_x1f", [128, 4 * C], F32))
            b_X1T = esB.enter_context(nc.sbuf_tensor("b_X1T", [128, CC * 256], BF16))
            b_h = esB.enter_context(nc.sbuf_tensor("b_h", [128, 512], BF16))
            b_u = esB.enter_context(nc.sbuf_tensor("b_u", [16, 512], BF16))

            W = dict(pe=[(sPE, attn_done["pe"])], dv=[(sDV, attn_done["dv"])],
                     ac=[(sAC, attn_done["ac"])])
            barrier = W["pe"] + W["dv"] + W["ac"] + attn_done["do"]

            def w_steps(which):
                if which == 0:
                    return [(d_wfcT[cc * 128:(cc + 1) * 128, hh * 1024:(hh + 1) * 1024],
                             b_Wfc[:, cc * FF + hh * 1024: cc * FF + (hh + 1) * 1024], 1024)
                            for hh in range(4) for cc in range(CC)]
                return [(d_wpF[fc * 128:(fc + 1) * 128, :],
                         b_Wp[:, fc * C:(fc + 1) * C], 1024)
                        for fc in range(FC)]

            ws_free = [attn_done["dv"]] * 4
            am_v = [0, 0]
            accs = [(g_am[:, 0:1], g_am[:, 1:2]), (g_scr[:, 0:1], g_scr[:, 1:2])]
            for which in range(2):
                hsteps = []
                for (src, dst, n) in w_steps(which):
                    hsteps.append(src[:, 0:512])
                    hsteps.append(src[:, 512:n])
                red_prev = [0, 0]
                for i, hsrc in enumerate(hsteps):
                    slot = i % 4
                    ch = i % 2
                    accb, tmpb = accs[ch]
                    vd = kb.dma("sync", lambda E, hsrc=hsrc, slot=slot: E.dma_start(
                        out=b_ws[:, slot * 512:(slot + 1) * 512], in_=hsrc),
                        waits=[(sDV, ws_free[slot])] +
                              (barrier if (which == 0 and i == 0) else []))
                    vr = kb.op("vector", lambda E, slot=slot, tmpb=tmpb: E.tensor_reduce(
                        tmpb, b_ws[:, slot * 512:(slot + 1) * 512],
                        AX, ALU.max, apply_absolute_value=True),
                        waits=[vd, (sDV, red_prev[ch])], inc=sDV)
                    if i < 2:
                        vacc = kb.op("vector", lambda E, accb=accb, tmpb=tmpb: E.tensor_scalar(
                            accb, tmpb, 1.0, None, ALU.mult),
                            waits=[(sDV, vr)], inc=sDV)
                    else:
                        vacc = kb.op("vector", lambda E, accb=accb, tmpb=tmpb: E.tensor_tensor(
                            accb, accb, tmpb, ALU.max),
                            waits=[(sDV, vr)], inc=sDV)
                    red_prev[ch] = vacc
                    ws_free[slot] = vacc
                vmrg = kb.op("vector", lambda E: E.tensor_tensor(
                    g_am[:, 0:1], g_am[:, 0:1], g_scr[:, 0:1], ALU.max),
                    waits=[(sDV, red_prev[0]), (sDV, red_prev[1])], inc=sDV)
                vt = kb.op("tensor", lambda E: E.transpose(
                    psU[0:1, 0:128], g_am[:, 0:1], g_identf[:, :]),
                    waits=[(sDV, vmrg)], inc=sPE)
                vx = kb.op("vector", lambda E, which=which: E.tensor_reduce(
                    g_srow[0:1, 4 * which: 4 * which + 1], psU[0:1, 0:128],
                    AX, ALU.max), waits=[(sPE, vt)], inc=sDV)
                q_p1_psU = vx
                v1 = kb.op("vector", lambda E, which=which: E.reciprocal(
                    g_srow[0:1, 4 * which + 1: 4 * which + 2],
                    g_srow[0:1, 4 * which: 4 * which + 1]),
                    waits=[(sDV, vx)], inc=sDV)
                v2 = kb.op("vector", lambda E, which=which: E.tensor_scalar(
                    g_srow[0:1, 4 * which + 1: 4 * which + 2],
                    g_srow[0:1, 4 * which + 1: 4 * which + 2], 127.0, None,
                    ALU.mult), waits=[(sDV, v1)], inc=sDV)
                v3 = kb.op("vector", lambda E, which=which: E.tensor_scalar(
                    g_srow[0:1, 4 * which + 2: 4 * which + 3],
                    g_srow[0:1, 4 * which: 4 * which + 1], 1.0 / 127.0, None,
                    ALU.mult), waits=[(sDV, v2)], inc=sDV)
                v4 = kb.op("vector", lambda E, which=which: E.tensor_scalar(
                    g_srow[0:1, 4 * which + 3: 4 * which + 4],
                    g_srow[0:1, 4 * which: 4 * which + 1], -1.0 / 127.0, None,
                    ALU.mult), waits=[(sDV, v3)], inc=sDV)
                am_v[which] = v4
            vsrf = dve_fence(g_srow[0:1, 7:8])
            vb1 = kb.dma("gpsimd", lambda E: E.dma_start(out=d_scr8, in_=g_srow[0:1, :]),
                         waits=[(sDV, vsrf)])
            scl_ready = kb.dma("gpsimd", lambda E: E.dma_start(
                out=g_scl[:, :], in_=d_scr8.broadcast_to([128, 8])),
                waits=[vb1])

            ws_free2 = [(sDV, max(ws_free[0], ws_free[1])),
                        (sDV, max(ws_free[2], ws_free[3]))]
            qst = {"qi": 0, "first": True}
            qfc_gate = [0, 0, 0, 0]
            wp_gate = [0] * FC

            def emit_p2_step(which, src, dst, n):
                sc = 4 * which
                slot = qst["qi"] % 2
                qst["qi"] += 1
                vd = kb.dma("sync", lambda E, src=src, slot=slot, n=n: E.dma_start(
                    out=b_ws[:, slot * 1024: slot * 1024 + n], in_=src),
                    waits=[ws_free2[slot]])
                wq = [vd]
                if qst["first"]:
                    wq.append(scl_ready)
                    qst["first"] = False
                vz = kb.op("vector", lambda E, slot=slot, n=n, sc=sc: E.tensor_scalar(
                    b_d[:, 0:n], b_ws[:, slot * 1024: slot * 1024 + n],
                    g_scl[:, sc + 1:sc + 2], None, ALU.mult),
                    waits=wq, inc=sDV)
                ws_free2[slot] = (sDV, vz)
                kb.op("vector", lambda E, n=n: E.tensor_copy(
                    b_zi[:, 0:n], b_d[:, 0:n]))
                kb.op("vector", lambda E, n=n: E.tensor_copy(
                    b_r[:, 0:n], b_zi[:, 0:n]))
                kb.op("vector", lambda E, n=n: E.tensor_tensor(
                    b_d[:, 0:n], b_d[:, 0:n], b_r[:, 0:n], ALU.subtract))
                kb.op("vector", lambda E, n=n: E.scalar_tensor_tensor(
                    b_r[:, 0:n], b_d[:, 0:n], 0.5, b_r[:, 0:n],
                    ALU.is_ge, ALU.add))
                kb.op("vector", lambda E, n=n: E.scalar_tensor_tensor(
                    b_r[:, 0:n], b_d[:, 0:n], -0.5, b_r[:, 0:n],
                    ALU.is_ge, ALU.add))
                vfin = kb.op("vector", lambda E, dst=dst, n=n, sc=sc: E.tensor_scalar(
                    dst, b_r[:, 0:n], g_scl[:, sc + 2:sc + 3],
                    g_scl[:, sc + 3:sc + 4], ALU.mult, ALU.add), inc=sDV)
                return vfin

            wfc_steps = w_steps(0)
            wp_steps = w_steps(1)

            def emit_p2_all():
                fi = 0
                pi = 0
                for hh in range(4):
                    for _ in range(8):
                        vfin = emit_p2_step(0, *wfc_steps[fi])
                        fi += 1
                    qfc_gate[hh] = vfin
                    nwp = 4 if hh < 3 else 20
                    for _ in range(nwp):
                        if pi >= FC:
                            break
                        wp_gate[pi] = emit_p2_step(1, *wp_steps[pi])
                        pi += 1

            stB = {
                "x1f_free": [attn_done["dv"]] * 4,
                "x1rd": [attn_done["pe"]] * 4,
                "psT_free": attn_done["dv"],
                "X1T_free": attn_done["pe"],
                "h_free": [attn_done["pe"], attn_done["pe"]],
                "psF_free": [attn_done["ac"], attn_done["ac"]],
                "psA_free": attn_done["dv"],
                "psU_free": attn_done["dv"],
                "ln_busy": st["ln_busy"], "ln_busy_ac": st["ln_busy_ac"],
                "xo_free": st["xo_free"], "sm_free": st["sm_free"],
                "lnq": [],
            }
            st.update({})

            x1_all = attn_done["do"]

            gelu_done = {}

            def tile_prologue(s, it):
                rows0 = s * T + it * 256
                tp = (s * 8 + it) % 2
                x1v = [NONE, NONE]
                for tc in range(2):
                    fslot = tp * 2 + tc
                    x1v[tc] = kb.dma("sync", lambda E, tc=tc, rows0=rows0, fslot=fslot: E.dma_start(
                        out=b_x1f[:, fslot * C:(fslot + 1) * C],
                        in_=d_x1[rows0 + tc * 128: rows0 + (tc + 1) * 128, :]),
                        waits=[(sDV, stB["x1f_free"][fslot]),
                               (sPE, stB["x1rd"][fslot])] +
                              (x1_all if (s == 0 and it == 0) else []))
                    for b2 in range(2):
                        for i4 in range(4):
                            i = b2 * 4 + i4
                            w = []
                            if i4 == 0:
                                w = [x1v[tc], (sDV, stB["psT_free"]),
                                     (sPE, stB["X1T_free"])]
                            kb.op("tensor", lambda E, fslot=fslot, i=i, i4=i4: E.transpose(
                                psT[:, i4 * 128:(i4 + 1) * 128],
                                b_x1f[:, fslot * C + i * 128: fslot * C + (i + 1) * 128],
                                g_identf[:, :]),
                                waits=w, inc=(sPE if i4 == 3 else None))
                        ve = kb.op("vector", lambda E, tc=tc, b2=b2: E.tensor_copy(
                            b_X1T[:, :].rearrange("p (c t) -> p c t", c=CC)
                            [:, b2 * 4:(b2 + 1) * 4, tc * 128:(tc + 1) * 128],
                            psT[:, :].rearrange("p (c t) -> p c t", c=4)),
                            waits=[(sPE, sPE.n)], inc=sDV)
                        stB["psT_free"] = ve
                    stB["x1rd"][fslot] = sPE.n
                    dve_fence(b_X1T[:, 7 * 256 + tc * 128 + 127: 7 * 256 + tc * 128 + 128])
                x1t_ready = sDV.n
                wu = [(sDV, x1t_ready), (sDV, stB["psU_free"]),
                      (sDV, q_p1_psU)]
                for cc in range(CC):
                    kb.op("tensor", lambda E, cc=cc: E.matmul(
                        psU[0:16, 0:256],
                        g_afcT[:, cc * R:(cc + 1) * R],
                        b_X1T[:, cc * 256:(cc + 1) * 256],
                        start=(cc == 0), stop=(cc == CC - 1)),
                        waits=(wu if cc == 0 else []),
                        inc=(sPE if cc == CC - 1 else None))
                    if cc == 0:
                        wu = []
                vu = kb.op("vector", lambda E: E.tensor_scalar(
                    b_u[0:16, 0:256], psU[0:16, 0:256], 2.0, None, ALU.mult),
                    waits=[(sPE, sPE.n)], inc=sDV)
                u_ready = dve_fence(b_u[0:16, 255:256])
                return u_ready, tp, rows0

            tiles = [(s, it) for s in range(SEQ) for it in range(8)]
            pro = tile_prologue(*tiles[0])
            emit_p2_all()
            for tix, (s, it) in enumerate(tiles):
                if True:
                    u_ready, tp, rows0 = pro


                    proj_state = {"first": True}

                    def emit_fc(f):
                        pp = f % 2
                        psF = psO if pp == 0 else psBC
                        wfc = [(sAC, stB["psF_free"][pp])]
                        if f == 0:
                            wfc.append((sDV, u_ready))
                        if f % CC == 0:
                            wfc.append((sDV, qfc_gate[f // CC]))
                        for cc in range(CC):
                            kb.op("tensor", lambda E, psF=psF, cc=cc, f=f: E.matmul(
                                psF[:, 0:256],
                                b_Wfc[:, cc * FF + f * 128: cc * FF + (f + 1) * 128],
                                b_X1T[:, cc * 256:(cc + 1) * 256],
                                start=(cc == 0), stop=False),
                                waits=(wfc if cc == 0 else []))
                            if cc == 0:
                                wfc = []
                        kb.op("tensor", lambda E, psF=psF, f=f: E.matmul(
                            psF[:, 0:256], g_bfcT[0:16, f * 128:(f + 1) * 128],
                            b_u[0:16, 0:256], start=False, stop=True), inc=sPE)
                        return sPE.n

                    def emit_gelu(f, fc_done):
                        pp = f % 2
                        psF = psO if pp == 0 else psBC
                        vg = kb.op("scalar", lambda E, psF=psF, pp=pp, f=f: E.activation(
                            b_h[:, pp * 256:(pp + 1) * 256], psF[:, 0:256],
                            AF.Gelu, bias=g_bfccol[:, f:f + 1]),
                            waits=[(sPE, fc_done), (sPE, stB["h_free"][pp])],
                            inc=sAC)
                        stB["psF_free"][pp] = vg
                        vgf = act_fence(b_h[:, pp * 256 + 255: pp * 256 + 256])
                        gelu_done[f] = vgf

                    def emit_proj_u2(f):
                        pp = f % 2
                        wpj = [(sAC, gelu_done[f]), (sDV, wp_gate[f])]
                        if proj_state["first"]:
                            wpj.append((sDV, stB["psA_free"]))
                            for q in range(4):
                                oh = q % 2
                                kb.op("tensor", lambda E, q=q, oh=oh: E.matmul(
                                    psA[:, q * 512:(q + 1) * 512],
                                    g_onesb[0:1, 0:128],
                                    g_bprow2[0:1, oh * 512:(oh + 1) * 512],
                                    start=True, stop=False),
                                    waits=(wpj if q == 0 else []))
                                if q == 0:
                                    wpj = []
                            proj_state["first"] = False
                        for tc2 in range(2):
                            for oh in range(2):
                                q = tc2 * 2 + oh
                                kb.op("tensor", lambda E, q=q, tc2=tc2, oh=oh, pp=pp, f=f: E.matmul(
                                    psA[:, q * 512:(q + 1) * 512],
                                    b_h[:, pp * 256 + tc2 * 128: pp * 256 + (tc2 + 1) * 128],
                                    b_Wp[:, f * C + oh * 512: f * C + (oh + 1) * 512],
                                    start=False, stop=False),
                                    waits=(wpj if q == 0 else []))
                                if q == 0:
                                    wpj = []
                        kb.op("tensor", lambda E, f=f, pp=pp: E.matmul(
                            psU[0:16, 256:512],
                            g_aprojT[:, f * R:(f + 1) * R],
                            b_h[:, pp * 256:(pp + 1) * 256],
                            start=(f == 0), stop=(f == FC - 1)),
                            inc=sPE)
                        stB["h_free"][pp] = sPE.n

                    proj_state["first"] = True
                    fcd = emit_fc(0)
                    emit_gelu(0, fcd)
                    for f in range(1, FC):
                        if f == 1:
                            for epi_fn in stB["lnq"]:
                                epi_fn()
                            stB["lnq"] = []
                        fcd = emit_fc(f)
                        emit_proj_u2(f - 1)
                        emit_gelu(f, fcd)
                    emit_proj_u2(FC - 1)
                    vu2 = kb.op("vector", lambda E: E.tensor_scalar(
                        b_u[0:16, 256:512], psU[0:16, 256:512], 2.0, None,
                        ALU.mult), waits=[(sPE, sPE.n)], inc=sDV)
                    stB["psU_free"] = vu2
                    vu2f = dve_fence(b_u[0:16, 511:512])
                    wl = [(sDV, vu2f)]
                    for tc2 in range(2):
                        for oh in range(2):
                            q = tc2 * 2 + oh
                            kb.op("tensor", lambda E, q=q, tc2=tc2, oh=oh: E.matmul(
                                psA[:, q * 512:(q + 1) * 512],
                                b_u[0:16, 256 + tc2 * 128: 256 + (tc2 + 1) * 128],
                                g_bprojT[0:16, oh * 512:(oh + 1) * 512],
                                start=False, stop=True),
                                waits=(wl if q == 0 else []),
                                inc=(sPE if q == 3 else None))
                            if q == 0:
                                wl = []
                    ffn_done = sPE.n
                    if tix + 1 < len(tiles):
                        pro = tile_prologue(*tiles[tix + 1])

                    def epi(ffn_done=ffn_done, tp=tp, rows0=rows0):
                        st["ln_busy"] = stB["ln_busy"]
                        st["ln_busy_ac"] = stB["ln_busy_ac"]
                        st["xo_free"] = stB["xo_free"]
                        st["sm_free"] = stB["sm_free"]
                        last_apply = 0
                        for tc2 in range(2):
                            va, vo, vadds = ln_and_store(
                                [(psA[:, (tc2 * 2) * 512:(tc2 * 2 + 1) * 512],
                                  [(sPE, ffn_done)]),
                                 (psA[:, (tc2 * 2 + 1) * 512:(tc2 * 2 + 2) * 512], [])],
                                (b_x1f, (tp * 2 + tc2) * C),
                                (d_out, rows0 + tc2 * 128), "ln2", sqbuf=b_z)
                            last_apply = vadds
                            stB["x1f_free"][tp * 2 + tc2] = va
                        stB["psA_free"] = last_apply
                        stB["ln_busy"] = st["ln_busy"]
                        stB["ln_busy_ac"] = st["ln_busy_ac"]
                        stB["xo_free"] = st["xo_free"]
                        stB["sm_free"] = st["sm_free"]
                    stB["lnq"].append(epi)

            for epi_fn in stB["lnq"]:
                epi_fn()
            stB["lnq"] = []

            finals = [(s.h, s.n) for s in pOUT.sems]
            kb.raw("gpsimd", lambda E: [E.wait_ge(h, v) for (h, v) in finals if v > 0])
            kb.emit()

    return nc


_NC_CACHE = [None]


def _masks_host():
    kk = np.arange(128).reshape(128, 1)
    m = np.arange(896).reshape(1, 896)
    return (kk <= m - 384).astype(np.float32).astype(ml_dtypes.bfloat16)


def kernel(L0_attn_x, L0_attn_w_qkv, L0_attn_w_proj, L0_attn_b_proj,
           L0_ffn_W_fc, L0_ffn_b_fc, L0_ffn_A_fc, L0_ffn_B_fc,
           L0_ffn_W_proj, L0_ffn_b_proj, L0_ffn_A_proj, L0_ffn_B_proj):
    f32 = lambda a: np.ascontiguousarray(np.asarray(a, dtype=np.float32))
    x = f32(L0_attn_x).reshape(B, T, C)
    shared = {
        "wqkvT": f32(np.asarray(L0_attn_w_qkv, np.float32).T),
        "wprojT": f32(np.asarray(L0_attn_w_proj, np.float32).T),
        "bprow": f32(L0_attn_b_proj).reshape(1, C),
        "wfcT": f32(np.asarray(L0_ffn_W_fc, np.float32).T),
        "bfccol": f32(np.asarray(L0_ffn_b_fc, np.float32).reshape(FC, 128).T),
        "afcT": f32(np.asarray(L0_ffn_A_fc, np.float32).T),
        "bfcT": f32(np.asarray(L0_ffn_B_fc, np.float32).T),
        "wpF": f32(np.asarray(L0_ffn_W_proj, np.float32).T),
        "bprow2": f32(L0_ffn_b_proj).reshape(1, C),
        "aprojT": f32(np.asarray(L0_ffn_A_proj, np.float32).T),
        "bprojT": f32(np.asarray(L0_ffn_B_proj, np.float32).T),
        "identb": np.eye(128, dtype=ml_dtypes.bfloat16),
        "identf": np.eye(128, dtype=np.float32),
        "onesf": np.ones((1, 128), np.float32),
        "onesb": np.ones((1, 128), ml_dtypes.bfloat16),
        "ones16": np.ones((128, 16), ml_dtypes.bfloat16),
        "masks": _masks_host(),
    }
    if _NC_CACHE[0] is None:
        _NC_CACHE[0] = build_program()
    nc = _NC_CACHE[0]
    in_maps = []
    for c in range(NCORES):
        m = dict(shared)
        m["x"] = np.ascontiguousarray(
            x[c * SEQ:(c + 1) * SEQ].reshape(ROWS, C))
        m["xT"] = np.ascontiguousarray(m["x"].T)
        in_maps.append(m)
    res = run_bass_kernel_spmd(nc, in_maps, core_ids=list(range(NCORES)))
    out = np.concatenate(
        [np.asarray(r["out"], np.float32).reshape(SEQ, T, C)
         for r in res.results], axis=0)
    return out


# version 30
# speedup vs baseline: 1.0715x; 1.0521x over previous
import numpy as np
import ml_dtypes
from contextlib import ExitStack

import concourse.bass as bass
import concourse.mybir as mybir
from concourse.bass_utils import run_bass_kernel_spmd

F32 = mybir.dt.float32
BF16 = mybir.dt.bfloat16
I32 = mybir.dt.int32
AF = mybir.ActivationFunctionType
ALU = mybir.AluOpType
AX = mybir.AxisListType.X

B, T, C = 16, 2048, 1024
H, HS = 16, 64
FF, R = 4096, 16
NCORES = 8
SEQ = 2
ROWS = SEQ * T
CC = C // 128
FC = FF // 128


class Sem:
    def __init__(self, es, nc, name):
        self.h = es.enter_context(nc.semaphore(name))
        self.n = 0


class SemPool:

    def __init__(self, es, nc, name, n):
        self.sems = [Sem(es, nc, f"{name}{i}") for i in range(n)]
        self.i = 0

    def take(self):
        s = self.sems[self.i % len(self.sems)]
        self.i += 1
        return s


class KB:
    ENGINES = ("sync", "gpsimd", "vector", "scalar", "tensor")

    def __init__(self, nc):
        self.nc = nc
        self.prog = {e: [] for e in self.ENGINES}
        self.serial = False
        self.esem = {}
        self.pools = {}

    def dma(self, eng, fn, waits=(), pool=None):
        s = self.pools[pool or eng].take()
        w = list(waits) + [(s, s.n)]
        self.op(eng, fn, waits=w, inc=s, inck=16)
        return (s, s.n)

    def op(self, eng, fn, waits=(), inc=None, inck=1):
        if self.serial and self.esem:
            waits = list(waits) + [(s, s.n) for e2, s in self.esem.items()
                                   if e2 != eng]
            if inc is None and eng in self.esem:
                inc = self.esem[eng]
                inck = 1
        if eng in ("vector", "scalar", "gpsimd") and eng in self.esem:
            waits = list(waits) + [(self.esem[eng], self.esem[eng].n)]
            if inc is None:
                inc = self.esem[eng]
                inck = 1
        waits = [(s, v) for (s, v) in waits if s is not None]
        w = tuple((s.h, v) for (s, v) in waits)
        if inc is not None:
            inc.n += inck

        def run(E, fn=fn, w=w, inc=inc, inck=inck):
            for (h, v) in w:
                if callable(v):
                    v = v()
                if v > 0:
                    E.wait_ge(h, v)
            inst = fn(E)
            if inc is not None:
                inst.then_inc(inc.h, inck)

        self.prog[eng].append(run)
        return inc.n if inc is not None else None

    def raw(self, eng, fn):
        self.prog[eng].append(fn)

    def emit(self):
        with self.nc.Block() as block:
            for eng in self.ENGINES:
                fns = self.prog[eng]
                if not fns:
                    continue

                def section(E, fns=fns):
                    for f in fns:
                        f(E)

                getattr(block, eng)(section)
        self.prog = {e: [] for e in self.ENGINES}


def build_program():
    nc = bass.Bass("TRN2", target_bir_lowering=False)

    d_x = nc.dram_tensor("x", [ROWS, C], F32, kind="ExternalInput").ap()
    d_xT = nc.dram_tensor("xT", [C, ROWS], F32, kind="ExternalInput").ap()
    d_wqkvT = nc.dram_tensor("wqkvT", [C, 3 * C], F32, kind="ExternalInput").ap()
    d_wprojT = nc.dram_tensor("wprojT", [C, C], F32, kind="ExternalInput").ap()
    d_bprow = nc.dram_tensor("bprow", [1, C], F32, kind="ExternalInput").ap()
    d_wfcT = nc.dram_tensor("wfcT", [C, FF], F32, kind="ExternalInput").ap()
    d_bfccol = nc.dram_tensor("bfccol", [128, FC], F32, kind="ExternalInput").ap()
    d_afcT = nc.dram_tensor("afcT", [C, R], F32, kind="ExternalInput").ap()
    d_bfcT = nc.dram_tensor("bfcT", [R, FF], F32, kind="ExternalInput").ap()
    d_wpF = nc.dram_tensor("wpF", [FF, C], F32, kind="ExternalInput").ap()
    d_bprow2 = nc.dram_tensor("bprow2", [1, C], F32, kind="ExternalInput").ap()
    d_aprojT = nc.dram_tensor("aprojT", [FF, R], F32, kind="ExternalInput").ap()
    d_bprojT = nc.dram_tensor("bprojT", [R, C], F32, kind="ExternalInput").ap()
    d_identb = nc.dram_tensor("identb", [128, 128], BF16, kind="ExternalInput").ap()
    d_identf = nc.dram_tensor("identf", [128, 128], F32, kind="ExternalInput").ap()
    d_onesf = nc.dram_tensor("onesf", [1, 128], F32, kind="ExternalInput").ap()
    d_ones16 = nc.dram_tensor("ones16", [128, 16], BF16, kind="ExternalInput").ap()
    d_onesb = nc.dram_tensor("onesb", [1, 128], BF16, kind="ExternalInput").ap()
    d_masks = nc.dram_tensor("masks", [128, 896], BF16, kind="ExternalInput").ap()
    d_out = nc.dram_tensor("out", [ROWS, C], F32, kind="ExternalOutput").ap()
    d_x1 = nc.dram_tensor("x1scr", [ROWS, C], F32).ap()
    d_scr8 = nc.dram_tensor("scr8", [1, 8], F32).ap()
    d_recb = nc.dram_tensor("recb", [1, 512], F32).ap()

    es = ExitStack()
    with es:
        pIN = SemPool(es, nc, "dIN", 4)
        pGP = SemPool(es, nc, "dGP", 2)
        pOUT = SemPool(es, nc, "dOUT", 2)
        NONE = (None, 0)
        sPE = Sem(es, nc, "sPE")
        sDV = Sem(es, nc, "sDV")
        sAC = Sem(es, nc, "sAC")
        sPL = Sem(es, nc, "sPL")

        g_identb = es.enter_context(nc.sbuf_tensor("g_identb", [128, 128], BF16))
        g_identf = es.enter_context(nc.sbuf_tensor("g_identf", [128, 128], F32))
        g_onesf = es.enter_context(nc.sbuf_tensor("g_onesf", [1, 128], F32))

        g_bprow2 = es.enter_context(nc.sbuf_tensor("g_bprow2", [1, C], BF16))
        g_onesb = es.enter_context(nc.sbuf_tensor("g_onesb", [1, 128], BF16))
        g_bfccol = es.enter_context(nc.sbuf_tensor("g_bfccol", [128, FC], F32))
        g_afcT = es.enter_context(nc.sbuf_tensor("g_afcT", [128, CC * R], BF16))
        g_bfcT = es.enter_context(nc.sbuf_tensor("g_bfcT", [R, FF], BF16))
        g_aprojT = es.enter_context(nc.sbuf_tensor("g_aprojT", [128, FC * R], BF16))
        g_bprojT = es.enter_context(nc.sbuf_tensor("g_bprojT", [R, C], BF16))
        g_pre = es.enter_context(nc.sbuf_tensor("g_pre", [128, C], F32))
        g_xo = es.enter_context(nc.sbuf_tensor("g_xo", [128, C], F32))
        g_sm = es.enter_context(nc.sbuf_tensor("g_sm", [128, 16], F32))

        g_srow = es.enter_context(nc.sbuf_tensor("g_srow", [1, 8], F32))
        g_scl = es.enter_context(nc.sbuf_tensor("g_scl", [128, 8], F32))
        g_am = es.enter_context(nc.sbuf_tensor("g_am", [128, 2], F32))
        g_scr = es.enter_context(nc.sbuf_tensor("g_scr", [128, 4], F32))

        psA = es.enter_context(nc.psum_tensor("psA", [128, 2048], F32))
        psO = es.enter_context(nc.psum_tensor("psO", [128, 512], F32))
        psBC = es.enter_context(nc.psum_tensor("psBC", [128, 512], F32))
        psT = es.enter_context(nc.psum_tensor("psT", [128, 512], F32))
        psU = es.enter_context(nc.psum_tensor("psU", [128, 512], F32))

        kb = KB(nc)
        kb.esem = {"tensor": sPE, "vector": sDV, "scalar": sAC,
                   "gpsimd": sPL}
        kb.pools = {"sync": pIN, "gpsimd": pGP, "out": pOUT}
        import os
        kb.serial = bool(int(os.environ.get("KSERIAL", "0")))
        _conc = set(os.environ.get("KCONC", "").split(","))

        def region(name):
            if os.environ.get("KSERIAL", "0") == "2":
                kb.serial = name not in _conc

        def dve_fence(ap):
            return sDV.n

        def act_fence(ap):
            return sAC.n

        setup_pairs = [(g_identb, d_identb), (g_identf, d_identf),
                       (g_onesf, d_onesf), (g_bfccol, d_bfccol),
                       (g_onesb, d_onesb)]
        for (dst, src) in setup_pairs:
            kb.dma("sync", lambda E, dst=dst, src=src:
                   E.dma_start(out=dst[:, :], in_=src))


        stg_state = {"free": [NONE, NONE]}

        def staged_cast(dst_ap, src_ap, n):
            slot = stg_state.setdefault("slot", 0)
            stg_state["slot"] = 1 - slot
            vx = kb.dma("sync", lambda E, slot=slot, src_ap=src_ap, n=n: E.dma_start(
                out=g_wstg[0:src_ap.shape[0], slot * 512: slot * 512 + n],
                in_=src_ap),
                waits=[stg_state["free"][slot]])
            vc = kb.op("vector", lambda E, dst_ap=dst_ap, slot=slot, n=n, src_ap=src_ap:
                       E.tensor_copy(dst_ap,
                                     g_wstg[0:src_ap.shape[0], slot * 512: slot * 512 + n]),
                       waits=[vx], inc=sDV)
            stg_state["free"][slot] = (sDV, vc)
            return vc


        esA = ExitStack()
        with esA:
            a_wqkv = esA.enter_context(nc.sbuf_tensor("a_wqkv", [128, CC * 3 * C], BF16))
            a_wproj = esA.enter_context(nc.sbuf_tensor("a_wproj", [128, CC * C], BF16))
            a_K = esA.enter_context(nc.sbuf_tensor("a_K", [128, CC * T], BF16))
            a_V = esA.enter_context(nc.sbuf_tensor("a_V", [128, 16 * (H * 65)], BF16))
            a_xT = esA.enter_context(nc.sbuf_tensor("a_xT", [128, CC * 512], BF16))
            a_xf = esA.enter_context(nc.sbuf_tensor("a_xf", [128, 2 * C], F32))
            a_Q = esA.enter_context(nc.sbuf_tensor("a_Q", [128, CC * 512], BF16))
            a_tE = esA.enter_context(nc.sbuf_tensor("a_tE", [128, 1024], F32))
            a_p2 = esA.enter_context(nc.sbuf_tensor("a_p2", [128, 2048], BF16))
            a_ph = esA.enter_context(nc.sbuf_tensor("a_ph", [128, 1024], BF16))
            a_O = esA.enter_context(nc.sbuf_tensor("a_O", [128, CC * 512], BF16))
            g_wstg = esA.enter_context(nc.sbuf_tensor("g_wstg", [128, 1024], F32))
            g_masks = esA.enter_context(nc.sbuf_tensor("g_masks", [128, 896], BF16))
            g_bprow = esA.enter_context(nc.sbuf_tensor("g_bprow", [1, C], BF16))
            g_ones16 = esA.enter_context(nc.sbuf_tensor("g_ones16", [128, 16], BF16))
            g_rec = esA.enter_context(nc.sbuf_tensor("g_rec", [1, 1024], F32))
            g_bc = esA.enter_context(nc.sbuf_tensor("g_bc", [64, 512], F32))
            for (dst, src) in [(g_ones16, d_ones16), (g_masks, d_masks)]:
                kb.dma("sync", lambda E, dst=dst, src=src:
                       E.dma_start(out=dst[:, :], in_=src))
            consts_waits = [(s, s.n) for s in pIN.sems]
            for cc in range(CC):
                staged_cast(g_afcT[:, cc * R:(cc + 1) * R],
                            d_afcT[cc * 128:(cc + 1) * 128, :], R)
            for hh in range(FF // 512):
                staged_cast(g_bfcT[0:16, hh * 512:(hh + 1) * 512],
                            d_bfcT[:, hh * 512:(hh + 1) * 512], 512)
            for fc in range(FC):
                staged_cast(g_aprojT[:, fc * R:(fc + 1) * R],
                            d_aprojT[fc * 128:(fc + 1) * 128, :], R)
            for hh in range(C // 512):
                staged_cast(g_bprojT[0:16, hh * 512:(hh + 1) * 512],
                            d_bprojT[:, hh * 512:(hh + 1) * 512], 512)
            for hh in range(C // 512):
                staged_cast(g_bprow[0:1, hh * 512:(hh + 1) * 512],
                            d_bprow[:, hh * 512:(hh + 1) * 512], 512)
                staged_cast(g_bprow2[0:1, hh * 512:(hh + 1) * 512],
                            d_bprow2[:, hh * 512:(hh + 1) * 512], 512)
            lora_dv = sDV.n

            for cc in range(CC):
                for hh in range(3 * C // 512):
                    staged_cast(a_wqkv[:, cc * 3 * C + hh * 512: cc * 3 * C + (hh + 1) * 512],
                                d_wqkvT[cc * 128:(cc + 1) * 128, hh * 512:(hh + 1) * 512], 512)
            for cc in range(CC):
                for hh in range(C // 512):
                    staged_cast(a_wproj[:, cc * C + hh * 512: cc * C + (hh + 1) * 512],
                                d_wprojT[cc * 128:(cc + 1) * 128, hh * 512:(hh + 1) * 512], 512)
            wqkv_dv = dve_fence(a_wproj[:, CC * C - 1: CC * C])

            st = {
                "xT_rd": 0,
                "psA_free": [0, 0, 0, 0],
                "psO_free": 0,
                "psU_free": 0,
                "psBC_free": 0,
                "xf_free": [0, 0],
                "ln_busy": 0,
                "ln_busy_ac": 0,
                "xo_free": NONE,
                "sm_free": 0,
            }

            def load_xT(t0, extra_waits=(), tag="R1"):
                region(tag)
                last = 0
                for cc in range(CC):
                    slot = stg_state.setdefault("slot", 0)
                    stg_state["slot"] = 1 - slot
                    vx = kb.dma("sync", lambda E, cc=cc, slot=slot, t0=t0: E.dma_start(
                        out=g_wstg[:, slot * 512:(slot + 1) * 512],
                        in_=d_xT[cc * 128:(cc + 1) * 128, t0:t0 + 512]),
                        waits=[stg_state["free"][slot],
                               (sPE, st["xT_rd"])] + (list(extra_waits) if cc == 0 else []))
                    vc = kb.op("vector", lambda E, cc=cc, slot=slot: E.tensor_copy(
                        a_xT[:, cc * 512:(cc + 1) * 512],
                        g_wstg[:, slot * 512:(slot + 1) * 512]),
                        waits=[vx], inc=sDV)
                    stg_state["free"][slot] = (sDV, vc)
                    last = dve_fence(a_xT[:, cc * 512 + 511: cc * 512 + 512])
                return last

            def load_xT_pool(t0):
                last = 0
                for cc in range(CC):
                    slot = stg_state.setdefault("slot", 0)
                    stg_state["slot"] = 1 - slot
                    vx = kb.dma("sync", lambda E, cc=cc, slot=slot, t0=t0: E.dma_start(
                        out=g_wstg[:, slot * 512:(slot + 1) * 512],
                        in_=d_xT[cc * 128:(cc + 1) * 128, t0:t0 + 512]),
                        waits=[stg_state["free"][slot], (sPE, st["xT_rd"])])
                    vc = kb.op("gpsimd", lambda E, cc=cc, slot=slot: E.tensor_copy(
                        a_xT[:, cc * 512:(cc + 1) * 512],
                        g_wstg[:, slot * 512:(slot + 1) * 512]),
                        waits=[vx], inc=sPL)
                    stg_state["free"][slot] = (sPL, vc)
                    last = vc
                return (sPL, last)

            def ln_and_store(ps_halves_waits, resid_cols, dst_rows, tag,
                             sqbuf=None):
                rbuf, rcol = resid_cols
                adds = []
                for oh in range(2):
                    ps_ap, w = ps_halves_waits[oh]
                    w = list(w)
                    if oh == 0:
                        w += [(sAC, st["ln_busy_ac"]), (sDV, st["ln_busy"])]
                    va = kb.op("vector", lambda E, ps_ap=ps_ap, oh=oh, rbuf=rbuf, rcol=rcol:
                               E.tensor_tensor(
                                   g_pre[:, oh * 512:(oh + 1) * 512], ps_ap,
                                   rbuf[:, rcol + oh * 512: rcol + (oh + 1) * 512],
                                   ALU.add),
                               waits=w, inc=sDV)
                    adds.append(va)
                vaddf = dve_fence(g_pre[:, 1023:1024])
                v1 = kb.op("vector", lambda E: E.tensor_reduce(
                    g_sm[:, 0:1], g_pre[:, :], AX, ALU.add),
                    waits=[(sDV, st["sm_free"])], inc=sDV)
                if sqbuf is None:
                    vsq = kb.op("gpsimd", lambda E: E.tensor_tensor(
                        g_xo[:, :], g_pre[:, :], g_pre[:, :], ALU.mult),
                        waits=[(sDV, vaddf), st["xo_free"]], inc=sPL)
                    v2 = kb.op("vector", lambda E: E.tensor_reduce(
                        g_sm[:, 1:2], g_xo[:, :], AX, ALU.add),
                        waits=[(sPL, vsq)], inc=sDV)
                else:
                    sq = sqbuf
                    vsq = kb.op("gpsimd", lambda E, sq=sq: E.tensor_tensor(
                        sq[:, 0:C], g_pre[:, :], g_pre[:, :], ALU.mult),
                        waits=[(sDV, vaddf)], inc=sPL)
                    v2 = kb.op("vector", lambda E, sq=sq: E.tensor_reduce(
                        g_sm[:, 1:2], sq[:, 0:C], AX, ALU.add),
                        waits=[(sPL, vsq)], inc=sDV)
                v3 = kb.op("vector", lambda E: E.tensor_scalar(
                    g_sm[:, 2:3], g_sm[:, 0:1], 1.0 / C, None, ALU.mult),
                    waits=[(sDV, v2)], inc=sDV)
                v4 = kb.op("vector", lambda E: E.tensor_tensor(
                    g_sm[:, 3:4], g_sm[:, 2:3], g_sm[:, 2:3], ALU.mult),
                    waits=[(sDV, v3)], inc=sDV)
                v5 = kb.op("vector", lambda E: E.scalar_tensor_tensor(
                    g_sm[:, 4:5], g_sm[:, 1:2], 1.0 / C, g_sm[:, 3:4],
                    ALU.mult, ALU.subtract),
                    waits=[(sDV, v4)], inc=sDV)
                v6 = kb.op("vector", lambda E: E.tensor_scalar(
                    g_sm[:, 4:5], g_sm[:, 4:5], 1.0, 1e-5, ALU.mult, ALU.add),
                    waits=[(sDV, v5)], inc=sDV)
                v6f = dve_fence(g_sm[:, 4:5])
                v7 = kb.op("scalar", lambda E: E.activation(
                    g_sm[:, 5:6], g_sm[:, 4:5], AF.Ln),
                    waits=[(sDV, v6f)], inc=sAC)
                v8 = kb.op("scalar", lambda E: E.activation(
                    g_sm[:, 6:7], g_sm[:, 5:6], AF.Exp, scale=-0.5),
                    waits=[(sAC, v7)], inc=sAC)
                v8f = act_fence(g_sm[:, 6:7])
                st["ln_busy_ac"] = v8f
                v9 = kb.op("vector", lambda E: E.tensor_scalar(
                    g_xo[:, :], g_pre[:, :], g_sm[:, 2:3], g_sm[:, 6:7],
                    ALU.subtract, ALU.mult),
                    waits=[(sAC, v8f), st["xo_free"]], inc=sDV)
                st["ln_busy"] = v9
                st["sm_free"] = v9
                v9f = dve_fence(g_xo[:, 1023:1024])
                dram, row0 = dst_rows
                vo = kb.dma("gpsimd", lambda E, dram=dram, row0=row0: E.dma_start(
                    out=dram[row0:row0 + 128, :], in_=g_xo[:, :]),
                    waits=[(sDV, v9f)], pool="out")
                st["xo_free"] = vo
                return v9, vo, adds[1]

            kv_ready = 0
            scores_last = 0

            for s in range(SEQ):
                base = s * T
                kv_war = [(sPE, sPE.n)] if s > 0 else []
                region("R1")
                for tg in range(4):
                    w = kv_war if tg == 0 else []
                    xw = ([(sDV, wqkv_dv)] + consts_waits) if (s == 0 and tg == 0) else []
                    xtg_ready = load_xT(base + tg * 512, extra_waits=w + xw)
                    for kf in range(CC):
                        bank = kf % 4
                        w = [(sDV, xtg_ready), (sDV, st["psA_free"][bank])]
                        if s == 0 and tg == 0 and kf == 0:
                            w.append((sDV, wqkv_dv))
                        for cc in range(CC):
                            kb.op("tensor", lambda E, bank=bank, cc=cc, kf=kf: E.matmul(
                                psA[:, bank * 512:(bank + 1) * 512],
                                a_wqkv[:, cc * 3 * C + C + kf * 128: cc * 3 * C + C + (kf + 1) * 128],
                                a_xT[:, cc * 512:(cc + 1) * 512],
                                start=(cc == 0), stop=(cc == CC - 1)),
                                waits=(w if cc == 0 else []),
                                inc=(sPE if cc == CC - 1 else None))
                            if cc == 0:
                                w = []
                        vk = kb.op("vector", lambda E, bank=bank, kf=kf, tg=tg: E.tensor_copy(
                            a_K[:, kf * T + tg * 512: kf * T + (tg + 1) * 512],
                            psA[:, bank * 512:(bank + 1) * 512]),
                            waits=[(sPE, sPE.n)], inc=sDV)
                        st["psA_free"][bank] = vk
                        dve_fence(a_K[:, kf * T + (tg + 1) * 512 - 1: kf * T + (tg + 1) * 512])
                    for tc in range(4):
                        tch = tg * 4 + tc
                        for fh in range(2):
                            bank = (tc * 2 + fh) % 4
                            w = [(sDV, st["psA_free"][bank])]
                            for cc in range(CC):
                                kb.op("tensor", lambda E, bank=bank, cc=cc, tc=tc, fh=fh: E.matmul(
                                    psA[:, bank * 512:(bank + 1) * 512],
                                    a_xT[:, cc * 512 + tc * 128: cc * 512 + (tc + 1) * 128],
                                    a_wqkv[:, cc * 3 * C + 2 * C + fh * 512: cc * 3 * C + 2 * C + (fh + 1) * 512],
                                    start=(cc == 0), stop=(cc == CC - 1)),
                                    waits=(w if cc == 0 else []),
                                    inc=(sPE if cc == CC - 1 else None))
                                if cc == 0:
                                    w = []
                            vv = kb.op("vector", lambda E, bank=bank, tch=tch, fh=fh: E.tensor_copy(
                                a_V[:, tch * (H * 65) + fh * 520: tch * (H * 65) + (fh + 1) * 520]
                                .rearrange("p (h d) -> p h d", h=8)[:, :, 0:64],
                                psA[:, bank * 512:(bank + 1) * 512]
                                .rearrange("p (h d) -> p h d", h=8)),
                                waits=[(sPE, sPE.n)], inc=sDV)
                            st["psA_free"][bank] = vv
                            st["xT_rd"] = sPE.n
                            dve_fence(a_V[:, tch * (H * 65) + (fh + 1) * 520 - 2: tch * (H * 65) + (fh + 1) * 520 - 1])
                        kb.op("vector", lambda E, tch=tch: E.tensor_copy(
                            a_V[:, tch * (H * 65): (tch + 1) * (H * 65)]
                            .rearrange("p (h d) -> p h d", h=16)[:, :, 64:65],
                            g_ones16[:, :].rearrange("p (h o) -> p h o", o=1)),
                            waits=(consts_waits if (s == 0 and tg == 0 and tc == 0) else []),
                            inc=sDV)
                        dve_fence(a_V[:, tch * (H * 65) + 15 * 65 + 64: tch * (H * 65) + 15 * 65 + 65])
                dve_fence(a_K[:, CC * T - 1: CC * T])
                kv_ready = dve_fence(a_V[:, 16 * H * 65 - 2: 16 * H * 65 - 1])

                for j in range(4):
                    q0 = base + j * 512
                    if j == 0:
                        xtq_ready = load_xT(q0, tag="R2a")
                        region("R2b")
                        for qf in range(CC):
                            bank = qf % 4
                            w = [(sDV, xtq_ready), (sDV, st["psA_free"][bank]),
                                 (sAC, st.get("exp_bank", [0, 0])[bank // 2])]
                            for cc in range(CC):
                                kb.op("tensor", lambda E, bank=bank, cc=cc, qf=qf: E.matmul(
                                    psA[:, bank * 512:(bank + 1) * 512],
                                    a_wqkv[:, cc * 3 * C + qf * 128: cc * 3 * C + (qf + 1) * 128],
                                    a_xT[:, cc * 512:(cc + 1) * 512],
                                    start=(cc == 0), stop=(cc == CC - 1)),
                                    waits=(w if cc == 0 else []),
                                    inc=(sPE if cc == CC - 1 else None))
                                if cc == 0:
                                    w = []
                            st["xT_rd"] = sPE.n
                            vq = kb.op("vector", lambda E, bank=bank, qf=qf: E.tensor_copy(
                                a_Q[:, qf * 512:(qf + 1) * 512],
                                psA[:, bank * 512:(bank + 1) * 512]),
                                waits=[(sPE, sPE.n)], inc=sDV)
                            st["psA_free"][bank] = vq
                            dve_fence(a_Q[:, qf * 512 + 511: qf * 512 + 512])
                        st["q_rdy"] = (sDV, sDV.n)
                    q_ready_t = st["q_rdy"]

                    xf_dma = [NONE] * 4
                    for tc in range(2):
                        xf_dma[tc] = kb.dma("sync", lambda E, tc=tc, q0=q0: E.dma_start(
                            out=a_xf[:, (tc % 2) * C:((tc % 2) + 1) * C],
                            in_=d_x[q0 + tc * 128: q0 + (tc + 1) * 128, :]),
                            waits=[(sDV, st["xf_free"][tc % 2])])

                    region("R3")
                    jobs = []
                    for h in range(H):
                        for gg in range(2 * (j + 1)):
                            jobs.append((h, gg))
                    LA = 3
                    sq_of = {}
                    avpe_of = {}
                    pending = []

                    weave = j < 3
                    if weave:
                        xtq_next = load_xT_pool(base + (j + 1) * 512)

                    def emit_qb_chunk(k, scores_pe):
                        wq0 = [xtq_next, (sAC, st.get("psT_q", 0)),
                               (sPE, scores_pe)]
                        for cc in range(CC):
                            kb.op("tensor", lambda E, cc=cc, k=k: E.matmul(
                                psT[:, 0:512],
                                a_wqkv[:, cc * 3 * C + k * 128: cc * 3 * C + (k + 1) * 128],
                                a_xT[:, cc * 512:(cc + 1) * 512],
                                start=(cc == 0), stop=(cc == CC - 1)),
                                waits=(wq0 if cc == 0 else []),
                                inc=(sPE if cc == CC - 1 else None))
                        st["xT_rd"] = sPE.n
                        vq = kb.op("scalar", lambda E, k=k: E.activation(
                            a_Q[:, k * 512:(k + 1) * 512], psT[:, 0:512],
                            AF.Identity),
                            waits=[(sPE, sPE.n)], inc=sAC)
                        st["psT_q"] = vq
                        if k == CC - 1:
                            st["q_rdy"] = (sAC, vq)

                    def emit_scores_softplus(idx):
                        h, gg = jobs[idx]
                        pb = (h % 2) * 64
                        kchunk = h // 2
                        pp = idx % 2
                        w0 = [(sAC, st.get("exp_bank", [0, 0])[pp]),
                              q_ready_t, (sDV, kv_ready)]
                        for r2 in range(2):
                            kc = 2 * gg + r2
                            kb.op("tensor", lambda E, r2=r2, kc=kc, pb=pb, kchunk=kchunk, pp=pp: E.matmul(
                                psA[:, (2 * pp + r2) * 512:(2 * pp + r2 + 1) * 512],
                                a_K[pb:pb + 64, kchunk * T + kc * 128: kchunk * T + (kc + 1) * 128],
                                a_Q[pb:pb + 64, kchunk * 512:(kchunk + 1) * 512],
                                start=True, stop=True),
                                waits=(w0 if r2 == 0 else []),
                                inc=(sPE if r2 == 1 else None))
                        scores_done = sPE.n
                        if weave and h % 2 == 1 and gg == 2 * (j + 1) - 1:
                            emit_qb_chunk(h // 2, scores_done)
                        ve = kb.op("scalar", lambda E, pp=pp: E.activation(
                            a_tE[:, :], psA[:, pp * 1024:(pp + 1) * 1024],
                            AF.Exp, scale=0.125),
                            waits=[(sPE, scores_done)], inc=sAC)
                        st.setdefault("exp_bank", [0, 0])[pp] = ve
                        vl = kb.op("scalar", lambda E: E.activation(
                            a_ph[:, :], a_tE[:, :], AF.Ln, bias=1.0),
                            waits=[(sDV, st.get("ph_sq", 0))], inc=sAC)
                        vsq = kb.op("vector", lambda E, pp=pp: E.tensor_tensor(
                            a_p2[:, pp * 1024:(pp + 1) * 1024],
                            a_ph[:, :], a_ph[:, :], ALU.mult),
                            waits=[(sAC, vl),
                                   (sPE, (lambda i2=idx - 2, d=avpe_of:
                                          d.get(i2, 0))
                                    if idx >= 2 else 0)],
                            inc=sDV)
                        st["ph_sq"] = vsq
                        ngg = 2 * (j + 1)
                        if gg >= ngg - 2:
                            for r2 in range(2):
                                r4 = 2 * gg + r2 - 4 * j
                                vsq = kb.op("vector", lambda E, pp=pp, r2=r2, r4=r4: E.tensor_tensor(
                                    a_p2[:, pp * 1024 + r2 * 512: pp * 1024 + (r2 + 1) * 512],
                                    a_p2[:, pp * 1024 + r2 * 512: pp * 1024 + (r2 + 1) * 512],
                                    g_masks[:, 384 - 128 * r4: 896 - 128 * r4],
                                    ALU.mult), inc=sDV)
                        sq_of[idx] = vsq

                    def emit_av(idx):
                        h, gg = jobs[idx]
                        pp = idx % 2
                        psAV = psO if h % 2 == 0 else psU
                        avfree_key = "psO_free" if h % 2 == 0 else "psU_free"
                        wav = [(sDV, sq_of[idx])]
                        if gg == 0:
                            wav.append((sDV, st[avfree_key]))
                        for r2 in range(2):
                            kc = 2 * gg + r2
                            kb.op("tensor", lambda E, kc=kc, r2=r2, h=h, pp=pp, psAV=psAV: E.matmul(
                                psAV[0:65, :],
                                a_V[:, kc * (H * 65) + h * 65: kc * (H * 65) + (h + 1) * 65],
                                a_p2[:, pp * 1024 + r2 * 512: pp * 1024 + (r2 + 1) * 512],
                                start=(kc == 0), stop=(kc == 4 * j + 3),
                                skip_group_check=True),
                                waits=(wav if r2 == 0 else []),
                                inc=(sPE if r2 == 1 else None))
                        avpe_of[idx] = sPE.n
                        if gg == 2 * (j + 1) - 1:
                            for (rid, fn) in list(pending):
                                pending.remove((rid, fn))
                                fn()
                            emit_norm_part1(idx, h)

                    def emit_norm_part1(idx, h):
                        psAV = psO if h % 2 == 0 else psU
                        av_done = avpe_of[idx]
                        vr1 = kb.op("vector", lambda E, psAV=psAV: E.tensor_scalar(
                            g_rec[0:1, 0:512], psAV[64:65, :], 1e-9, None, ALU.add),
                            waits=[(sPE, av_done),
                                   (sAC, st.get("rec_ln_rd", 0)),
                                   st.get("bc_dma1", NONE)], inc=sDV)
                        vl1 = kb.op("scalar", lambda E: E.activation(
                            g_rec[0:1, 512:1024], g_rec[0:1, 0:512], AF.Ln),
                            waits=[(sDV, vr1)], inc=sAC)
                        st["rec_ln_rd"] = vl1
                        vl2 = kb.op("scalar", lambda E: E.activation(
                            g_rec[0:1, 0:512], g_rec[0:1, 512:1024],
                            AF.Exp, scale=-1.0), inc=sAC)
                        vl2f = act_fence(g_rec[0:1, 511:512])
                        pending.append((idx + 3,
                                        lambda h=h, vl2f=vl2f: emit_norm_part2(h, vl2f)))

                    def emit_norm_part2(h, vl2f):
                        psAV = psO if h % 2 == 0 else psU
                        avfree_key = "psO_free" if h % 2 == 0 else "psU_free"
                        pb = (h % 2) * 64
                        kchunk = h // 2
                        d1 = kb.dma("sync", lambda E: E.dma_start(
                            out=d_recb, in_=g_rec[0:1, 0:512]),
                            waits=[(sAC, vl2f)])
                        st["bc_dma1"] = d1
                        d2 = kb.dma("sync", lambda E: E.dma_start(
                            out=g_bc[:, :], in_=d_recb.broadcast_to([64, 512])),
                            waits=[d1, (sDV, st.get("bc_free", 0))])
                        vm = kb.op("vector", lambda E, pb=pb, kchunk=kchunk, psAV=psAV: E.tensor_tensor(
                            a_O[pb:pb + 64, kchunk * 512:(kchunk + 1) * 512],
                            psAV[0:64, :], g_bc[:, :], ALU.mult),
                            waits=[d2], inc=sDV)
                        st[avfree_key] = vm
                        st["bc_free"] = vm

                    n = len(jobs)
                    for idx in range(n + LA):
                        for (rid, fn) in [p for p in pending if p[0] <= idx]:
                            pending.remove((rid, fn))
                            fn()
                        if LA <= idx < n + LA:
                            emit_av(idx - LA)
                        if idx < n:
                            emit_scores_softplus(idx)
                    for (rid, fn) in list(pending):
                        pending.remove((rid, fn))
                        fn()
                    assert not pending
                    O_ready = sDV.n

                    region("R5")
                    for tc in range(4):
                        pair = (psO, psBC)
                        fkeys = ("psO_free", "psBC_free")
                        proj_done = [0, 0]
                        for oh in range(2):
                            ps = pair[oh]
                            wfree = [(sDV, st.get(fkeys[oh], 0))]
                            w = [(sDV, O_ready)] + wfree
                            if s == 0 and j == 0 and tc == 0:
                                w.extend(consts_waits)
                            kb.op("tensor", lambda E, ps=ps, oh=oh: E.matmul(
                                ps[:, :], g_onesb[0:1, 0:128],
                                g_bprow[0:1, oh * 512:(oh + 1) * 512],
                                start=True, stop=False), waits=w, inc=None)
                            for cc in range(CC):
                                kb.op("tensor", lambda E, ps=ps, cc=cc, tc=tc, oh=oh: E.matmul(
                                    ps[:, :],
                                    a_O[:, cc * 512 + tc * 128: cc * 512 + (tc + 1) * 128],
                                    a_wproj[:, cc * C + oh * 512: cc * C + (oh + 1) * 512],
                                    start=False, stop=(cc == CC - 1)),
                                    inc=(sPE if cc == CC - 1 else None))
                            proj_done[oh] = sPE.n
                        slot = tc % 2
                        va, vo, vadds = ln_and_store(
                            [(pair[0][:, :], [(sPE, proj_done[0]), xf_dma[tc]]),
                             (pair[1][:, :], [(sPE, proj_done[1])])],
                            (a_xf, slot * C),
                            (d_x1, q0 + tc * 128), "ln1", sqbuf=None)
                        st[fkeys[0]] = vadds
                        st[fkeys[1]] = vadds
                        st["xf_free"][slot] = vadds
                        if tc + 2 < 4:
                            xf_dma[tc + 2] = kb.dma("sync", lambda E, tc=tc, q0=q0, slot=slot: E.dma_start(
                                out=a_xf[:, slot * C:(slot + 1) * C],
                                in_=d_x[q0 + (tc + 2) * 128: q0 + (tc + 3) * 128, :]),
                                waits=[(sDV, va)])

            attn_done = dict(pe=sPE.n, dv=sDV.n, ac=sAC.n,
                             do=[(s, s.n) for s in pOUT.sems])
            kb.emit()

        esB = ExitStack()
        with esB:
            b_Wfc = esB.enter_context(nc.sbuf_tensor("b_Wfc", [128, CC * FF], BF16))
            b_Wp = esB.enter_context(nc.sbuf_tensor("b_Wp", [128, FC * C], BF16))
            b_ws = esB.enter_context(nc.sbuf_tensor("b_ws", [128, 2 * 1024], F32))
            b_z = esB.enter_context(nc.sbuf_tensor("b_z", [128, 1024], F32))
            b_zi = esB.enter_context(nc.sbuf_tensor("b_zi", [128, 1024], I32))
            b_d = esB.enter_context(nc.sbuf_tensor("b_d", [128, 1024], F32))
            b_r = esB.enter_context(nc.sbuf_tensor("b_r", [128, 1024], F32))
            b_x1f = esB.enter_context(nc.sbuf_tensor("b_x1f", [128, 4 * C], F32))
            b_X1T = esB.enter_context(nc.sbuf_tensor("b_X1T", [128, CC * 256], BF16))
            b_h = esB.enter_context(nc.sbuf_tensor("b_h", [128, 512], BF16))
            b_u = esB.enter_context(nc.sbuf_tensor("b_u", [16, 512], BF16))

            W = dict(pe=[(sPE, attn_done["pe"])], dv=[(sDV, attn_done["dv"])],
                     ac=[(sAC, attn_done["ac"])])
            barrier = W["pe"] + W["dv"] + W["ac"] + attn_done["do"]

            def w_steps(which):
                if which == 0:
                    return [(d_wfcT[cc * 128:(cc + 1) * 128, hh * 1024:(hh + 1) * 1024],
                             b_Wfc[:, cc * FF + hh * 1024: cc * FF + (hh + 1) * 1024], 1024)
                            for hh in range(4) for cc in range(CC)]
                return [(d_wpF[fc * 128:(fc + 1) * 128, :],
                         b_Wp[:, fc * C:(fc + 1) * C], 1024)
                        for fc in range(FC)]

            ws_free = [attn_done["dv"]] * 4
            am_v = [0, 0]
            accs = [(g_am[:, 0:1], g_am[:, 1:2]), (g_scr[:, 0:1], g_scr[:, 1:2])]
            for which in range(2):
                hsteps = []
                for (src, dst, n) in w_steps(which):
                    hsteps.append(src[:, 0:512])
                    hsteps.append(src[:, 512:n])
                red_prev = [0, 0]
                for i, hsrc in enumerate(hsteps):
                    slot = i % 4
                    ch = i % 2
                    accb, tmpb = accs[ch]
                    vd = kb.dma("sync", lambda E, hsrc=hsrc, slot=slot: E.dma_start(
                        out=b_ws[:, slot * 512:(slot + 1) * 512], in_=hsrc),
                        waits=[(sDV, ws_free[slot])] +
                              (barrier if (which == 0 and i == 0) else []))
                    vr = kb.op("vector", lambda E, slot=slot, tmpb=tmpb: E.tensor_reduce(
                        tmpb, b_ws[:, slot * 512:(slot + 1) * 512],
                        AX, ALU.max, apply_absolute_value=True),
                        waits=[vd, (sDV, red_prev[ch])], inc=sDV)
                    if i < 2:
                        vacc = kb.op("vector", lambda E, accb=accb, tmpb=tmpb: E.tensor_scalar(
                            accb, tmpb, 1.0, None, ALU.mult),
                            waits=[(sDV, vr)], inc=sDV)
                    else:
                        vacc = kb.op("vector", lambda E, accb=accb, tmpb=tmpb: E.tensor_tensor(
                            accb, accb, tmpb, ALU.max),
                            waits=[(sDV, vr)], inc=sDV)
                    red_prev[ch] = vacc
                    ws_free[slot] = vacc
                vmrg = kb.op("vector", lambda E: E.tensor_tensor(
                    g_am[:, 0:1], g_am[:, 0:1], g_scr[:, 0:1], ALU.max),
                    waits=[(sDV, red_prev[0]), (sDV, red_prev[1])], inc=sDV)
                vt = kb.op("tensor", lambda E: E.transpose(
                    psU[0:1, 0:128], g_am[:, 0:1], g_identf[:, :]),
                    waits=[(sDV, vmrg)], inc=sPE)
                vx = kb.op("vector", lambda E, which=which: E.tensor_reduce(
                    g_srow[0:1, 4 * which: 4 * which + 1], psU[0:1, 0:128],
                    AX, ALU.max), waits=[(sPE, vt)], inc=sDV)
                q_p1_psU = vx
                v1 = kb.op("vector", lambda E, which=which: E.reciprocal(
                    g_srow[0:1, 4 * which + 1: 4 * which + 2],
                    g_srow[0:1, 4 * which: 4 * which + 1]),
                    waits=[(sDV, vx)], inc=sDV)
                v2 = kb.op("vector", lambda E, which=which: E.tensor_scalar(
                    g_srow[0:1, 4 * which + 1: 4 * which + 2],
                    g_srow[0:1, 4 * which + 1: 4 * which + 2], 127.0, None,
                    ALU.mult), waits=[(sDV, v1)], inc=sDV)
                v3 = kb.op("vector", lambda E, which=which: E.tensor_scalar(
                    g_srow[0:1, 4 * which + 2: 4 * which + 3],
                    g_srow[0:1, 4 * which: 4 * which + 1], 1.0 / 127.0, None,
                    ALU.mult), waits=[(sDV, v2)], inc=sDV)
                v4 = kb.op("vector", lambda E, which=which: E.tensor_scalar(
                    g_srow[0:1, 4 * which + 3: 4 * which + 4],
                    g_srow[0:1, 4 * which: 4 * which + 1], -1.0 / 127.0, None,
                    ALU.mult), waits=[(sDV, v3)], inc=sDV)
                am_v[which] = v4
            vsrf = dve_fence(g_srow[0:1, 7:8])
            vb1 = kb.dma("gpsimd", lambda E: E.dma_start(out=d_scr8, in_=g_srow[0:1, :]),
                         waits=[(sDV, vsrf)])
            scl_ready = kb.dma("gpsimd", lambda E: E.dma_start(
                out=g_scl[:, :], in_=d_scr8.broadcast_to([128, 8])),
                waits=[vb1])

            ws_free2 = [(sDV, max(ws_free[0], ws_free[1])),
                        (sDV, max(ws_free[2], ws_free[3]))]
            qst = {"qi": 0, "first": True}
            qfc_gate = [0, 0, 0, 0]
            wp_gate = [0] * FC

            def emit_p2_step(which, src, dst, n):
                sc = 4 * which
                slot = qst["qi"] % 2
                qst["qi"] += 1
                vd = kb.dma("sync", lambda E, src=src, slot=slot, n=n: E.dma_start(
                    out=b_ws[:, slot * 1024: slot * 1024 + n], in_=src),
                    waits=[ws_free2[slot]])
                wq = [vd]
                if qst["first"]:
                    wq.append(scl_ready)
                    qst["first"] = False
                vz = kb.op("vector", lambda E, slot=slot, n=n, sc=sc: E.tensor_scalar(
                    b_d[:, 0:n], b_ws[:, slot * 1024: slot * 1024 + n],
                    g_scl[:, sc + 1:sc + 2], None, ALU.mult),
                    waits=wq, inc=sDV)
                ws_free2[slot] = (sDV, vz)
                kb.op("vector", lambda E, n=n: E.tensor_copy(
                    b_zi[:, 0:n], b_d[:, 0:n]))
                kb.op("vector", lambda E, n=n: E.tensor_copy(
                    b_r[:, 0:n], b_zi[:, 0:n]))
                kb.op("vector", lambda E, n=n: E.tensor_tensor(
                    b_d[:, 0:n], b_d[:, 0:n], b_r[:, 0:n], ALU.subtract))
                kb.op("vector", lambda E, n=n: E.scalar_tensor_tensor(
                    b_r[:, 0:n], b_d[:, 0:n], 0.5, b_r[:, 0:n],
                    ALU.is_ge, ALU.add))
                kb.op("vector", lambda E, n=n: E.scalar_tensor_tensor(
                    b_r[:, 0:n], b_d[:, 0:n], -0.5, b_r[:, 0:n],
                    ALU.is_ge, ALU.add))
                vfin = kb.op("vector", lambda E, dst=dst, n=n, sc=sc: E.tensor_scalar(
                    dst, b_r[:, 0:n], g_scl[:, sc + 2:sc + 3],
                    g_scl[:, sc + 3:sc + 4], ALU.mult, ALU.add), inc=sDV)
                return vfin

            wfc_steps = w_steps(0)
            wp_steps = w_steps(1)

            def emit_p2_all():
                fi = 0
                pi = 0
                for hh in range(4):
                    for _ in range(8):
                        vfin = emit_p2_step(0, *wfc_steps[fi])
                        fi += 1
                    qfc_gate[hh] = vfin
                    nwp = 4 if hh < 3 else 20
                    for _ in range(nwp):
                        if pi >= FC:
                            break
                        wp_gate[pi] = emit_p2_step(1, *wp_steps[pi])
                        pi += 1

            stB = {
                "x1f_free": [attn_done["dv"]] * 4,
                "x1rd": [attn_done["pe"]] * 4,
                "psT_free": attn_done["dv"],
                "X1T_free": attn_done["pe"],
                "h_free": [attn_done["pe"], attn_done["pe"]],
                "psF_free": [attn_done["ac"], attn_done["ac"]],
                "psA_free": attn_done["dv"],
                "psU_free": attn_done["dv"],
                "ln_busy": st["ln_busy"], "ln_busy_ac": st["ln_busy_ac"],
                "xo_free": st["xo_free"], "sm_free": st["sm_free"],
                "lnq": [],
            }
            st.update({})

            x1_all = attn_done["do"]

            gelu_done = {}

            def tile_prologue(s, it):
                rows0 = s * T + it * 256
                tp = (s * 8 + it) % 2
                x1v = [NONE, NONE]
                for tc in range(2):
                    fslot = tp * 2 + tc
                    x1v[tc] = kb.dma("sync", lambda E, tc=tc, rows0=rows0, fslot=fslot: E.dma_start(
                        out=b_x1f[:, fslot * C:(fslot + 1) * C],
                        in_=d_x1[rows0 + tc * 128: rows0 + (tc + 1) * 128, :]),
                        waits=[(sDV, stB["x1f_free"][fslot]),
                               (sPE, stB["x1rd"][fslot])] +
                              (x1_all if (s == 0 and it == 0) else []))
                    for b2 in range(2):
                        for i4 in range(4):
                            i = b2 * 4 + i4
                            w = []
                            if i4 == 0:
                                w = [x1v[tc], (sDV, stB["psT_free"]),
                                     (sPE, stB["X1T_free"])]
                            kb.op("tensor", lambda E, fslot=fslot, i=i, i4=i4: E.transpose(
                                psT[:, i4 * 128:(i4 + 1) * 128],
                                b_x1f[:, fslot * C + i * 128: fslot * C + (i + 1) * 128],
                                g_identf[:, :]),
                                waits=w, inc=(sPE if i4 == 3 else None))
                        ve = kb.op("vector", lambda E, tc=tc, b2=b2: E.tensor_copy(
                            b_X1T[:, :].rearrange("p (c t) -> p c t", c=CC)
                            [:, b2 * 4:(b2 + 1) * 4, tc * 128:(tc + 1) * 128],
                            psT[:, :].rearrange("p (c t) -> p c t", c=4)),
                            waits=[(sPE, sPE.n)], inc=sDV)
                        stB["psT_free"] = ve
                    stB["x1rd"][fslot] = sPE.n
                    dve_fence(b_X1T[:, 7 * 256 + tc * 128 + 127: 7 * 256 + tc * 128 + 128])
                x1t_ready = sDV.n
                wu = [(sDV, x1t_ready), (sDV, stB["psU_free"]),
                      (sDV, q_p1_psU)]
                for cc in range(CC):
                    kb.op("tensor", lambda E, cc=cc: E.matmul(
                        psU[0:16, 0:256],
                        g_afcT[:, cc * R:(cc + 1) * R],
                        b_X1T[:, cc * 256:(cc + 1) * 256],
                        start=(cc == 0), stop=(cc == CC - 1)),
                        waits=(wu if cc == 0 else []),
                        inc=(sPE if cc == CC - 1 else None))
                    if cc == 0:
                        wu = []
                vu = kb.op("vector", lambda E: E.tensor_scalar(
                    b_u[0:16, 0:256], psU[0:16, 0:256], 2.0, None, ALU.mult),
                    waits=[(sPE, sPE.n)], inc=sDV)
                u_ready = dve_fence(b_u[0:16, 255:256])
                return u_ready, tp, rows0

            tiles = [(s, it) for s in range(SEQ) for it in range(8)]
            pro = tile_prologue(*tiles[0])
            emit_p2_all()
            for tix, (s, it) in enumerate(tiles):
                if True:
                    u_ready, tp, rows0 = pro


                    proj_state = {"first": True}

                    def emit_fc(f):
                        pp = f % 2
                        psF = psO if pp == 0 else psBC
                        wfc = [(sAC, stB["psF_free"][pp]), (sDV, u_ready),
                               (sDV, qfc_gate[f // CC])]
                        for cc in range(CC):
                            kb.op("tensor", lambda E, psF=psF, cc=cc, f=f: E.matmul(
                                psF[:, 0:256],
                                b_Wfc[:, cc * FF + f * 128: cc * FF + (f + 1) * 128],
                                b_X1T[:, cc * 256:(cc + 1) * 256],
                                start=(cc == 0), stop=False),
                                waits=(wfc if cc == 0 else []))
                            if cc == 0:
                                wfc = []
                        kb.op("tensor", lambda E, psF=psF, f=f: E.matmul(
                            psF[:, 0:256], g_bfcT[0:16, f * 128:(f + 1) * 128],
                            b_u[0:16, 0:256], start=False, stop=True), inc=sPE)
                        return sPE.n

                    def emit_gelu(f, fc_done):
                        pp = f % 2
                        psF = psO if pp == 0 else psBC
                        vg = kb.op("scalar", lambda E, psF=psF, pp=pp, f=f: E.activation(
                            b_h[:, pp * 256:(pp + 1) * 256], psF[:, 0:256],
                            AF.Gelu, bias=g_bfccol[:, f:f + 1]),
                            waits=[(sPE, fc_done), (sPE, stB["h_free"][pp])],
                            inc=sAC)
                        stB["psF_free"][pp] = vg
                        vgf = act_fence(b_h[:, pp * 256 + 255: pp * 256 + 256])
                        gelu_done[f] = vgf

                    def emit_proj_u2(f):
                        pp = f % 2
                        wpj = [(sAC, gelu_done[f]), (sDV, wp_gate[f])]
                        if proj_state["first"]:
                            wpj.append((sDV, stB["psA_free"]))
                            for q in range(4):
                                oh = q % 2
                                kb.op("tensor", lambda E, q=q, oh=oh: E.matmul(
                                    psA[:, q * 512:(q + 1) * 512],
                                    g_onesb[0:1, 0:128],
                                    g_bprow2[0:1, oh * 512:(oh + 1) * 512],
                                    start=True, stop=False),
                                    waits=(wpj if q == 0 else []))
                                if q == 0:
                                    wpj = []
                            proj_state["first"] = False
                        for tc2 in range(2):
                            for oh in range(2):
                                q = tc2 * 2 + oh
                                kb.op("tensor", lambda E, q=q, tc2=tc2, oh=oh, pp=pp, f=f: E.matmul(
                                    psA[:, q * 512:(q + 1) * 512],
                                    b_h[:, pp * 256 + tc2 * 128: pp * 256 + (tc2 + 1) * 128],
                                    b_Wp[:, f * C + oh * 512: f * C + (oh + 1) * 512],
                                    start=False, stop=False),
                                    waits=(wpj if q == 0 else []))
                                if q == 0:
                                    wpj = []
                        kb.op("tensor", lambda E, f=f, pp=pp: E.matmul(
                            psU[0:16, 256:512],
                            g_aprojT[:, f * R:(f + 1) * R],
                            b_h[:, pp * 256:(pp + 1) * 256],
                            start=(f == 0), stop=(f == FC - 1)),
                            inc=sPE)
                        stB["h_free"][pp] = sPE.n

                    proj_state["first"] = True
                    fcd = emit_fc(0)
                    emit_gelu(0, fcd)
                    for f in range(1, FC):
                        if f == 1:
                            for epi_fn in stB["lnq"]:
                                epi_fn()
                            stB["lnq"] = []
                        fcd = emit_fc(f)
                        emit_proj_u2(f - 1)
                        emit_gelu(f, fcd)
                    emit_proj_u2(FC - 1)
                    vu2 = kb.op("vector", lambda E: E.tensor_scalar(
                        b_u[0:16, 256:512], psU[0:16, 256:512], 2.0, None,
                        ALU.mult), waits=[(sPE, sPE.n)], inc=sDV)
                    stB["psU_free"] = vu2
                    vu2f = dve_fence(b_u[0:16, 511:512])
                    wl = [(sDV, vu2f)]
                    for tc2 in range(2):
                        for oh in range(2):
                            q = tc2 * 2 + oh
                            kb.op("tensor", lambda E, q=q, tc2=tc2, oh=oh: E.matmul(
                                psA[:, q * 512:(q + 1) * 512],
                                b_u[0:16, 256 + tc2 * 128: 256 + (tc2 + 1) * 128],
                                g_bprojT[0:16, oh * 512:(oh + 1) * 512],
                                start=False, stop=True),
                                waits=(wl if q == 0 else []),
                                inc=(sPE if q == 3 else None))
                            if q == 0:
                                wl = []
                    ffn_done = sPE.n
                    if tix + 1 < len(tiles):
                        pro = tile_prologue(*tiles[tix + 1])

                    def epi(ffn_done=ffn_done, tp=tp, rows0=rows0):
                        st["ln_busy"] = stB["ln_busy"]
                        st["ln_busy_ac"] = stB["ln_busy_ac"]
                        st["xo_free"] = stB["xo_free"]
                        st["sm_free"] = stB["sm_free"]
                        last_apply = 0
                        for tc2 in range(2):
                            va, vo, vadds = ln_and_store(
                                [(psA[:, (tc2 * 2) * 512:(tc2 * 2 + 1) * 512],
                                  [(sPE, ffn_done)]),
                                 (psA[:, (tc2 * 2 + 1) * 512:(tc2 * 2 + 2) * 512], [])],
                                (b_x1f, (tp * 2 + tc2) * C),
                                (d_out, rows0 + tc2 * 128), "ln2", sqbuf=b_z)
                            last_apply = vadds
                            stB["x1f_free"][tp * 2 + tc2] = va
                        stB["psA_free"] = last_apply
                        stB["ln_busy"] = st["ln_busy"]
                        stB["ln_busy_ac"] = st["ln_busy_ac"]
                        stB["xo_free"] = st["xo_free"]
                        stB["sm_free"] = st["sm_free"]
                    stB["lnq"].append(epi)

            for epi_fn in stB["lnq"]:
                epi_fn()
            stB["lnq"] = []

            finals = [(s.h, s.n) for s in pOUT.sems]
            kb.raw("gpsimd", lambda E: [E.wait_ge(h, v) for (h, v) in finals if v > 0])
            kb.emit()

    return nc


_NC_CACHE = [None]


def _masks_host():
    kk = np.arange(128).reshape(128, 1)
    m = np.arange(896).reshape(1, 896)
    return (kk <= m - 384).astype(np.float32).astype(ml_dtypes.bfloat16)


def kernel(L0_attn_x, L0_attn_w_qkv, L0_attn_w_proj, L0_attn_b_proj,
           L0_ffn_W_fc, L0_ffn_b_fc, L0_ffn_A_fc, L0_ffn_B_fc,
           L0_ffn_W_proj, L0_ffn_b_proj, L0_ffn_A_proj, L0_ffn_B_proj):
    f32 = lambda a: np.ascontiguousarray(np.asarray(a, dtype=np.float32))
    x = f32(L0_attn_x).reshape(B, T, C)
    shared = {
        "wqkvT": f32(np.asarray(L0_attn_w_qkv, np.float32).T),
        "wprojT": f32(np.asarray(L0_attn_w_proj, np.float32).T),
        "bprow": f32(L0_attn_b_proj).reshape(1, C),
        "wfcT": f32(np.asarray(L0_ffn_W_fc, np.float32).T),
        "bfccol": f32(np.asarray(L0_ffn_b_fc, np.float32).reshape(FC, 128).T),
        "afcT": f32(np.asarray(L0_ffn_A_fc, np.float32).T),
        "bfcT": f32(np.asarray(L0_ffn_B_fc, np.float32).T),
        "wpF": f32(np.asarray(L0_ffn_W_proj, np.float32).T),
        "bprow2": f32(L0_ffn_b_proj).reshape(1, C),
        "aprojT": f32(np.asarray(L0_ffn_A_proj, np.float32).T),
        "bprojT": f32(np.asarray(L0_ffn_B_proj, np.float32).T),
        "identb": np.eye(128, dtype=ml_dtypes.bfloat16),
        "identf": np.eye(128, dtype=np.float32),
        "onesf": np.ones((1, 128), np.float32),
        "onesb": np.ones((1, 128), ml_dtypes.bfloat16),
        "ones16": np.ones((128, 16), ml_dtypes.bfloat16),
        "masks": _masks_host(),
    }
    if _NC_CACHE[0] is None:
        _NC_CACHE[0] = build_program()
    nc = _NC_CACHE[0]
    in_maps = []
    for c in range(NCORES):
        m = dict(shared)
        m["x"] = np.ascontiguousarray(
            x[c * SEQ:(c + 1) * SEQ].reshape(ROWS, C))
        m["xT"] = np.ascontiguousarray(m["x"].T)
        in_maps.append(m)
    res = run_bass_kernel_spmd(nc, in_maps, core_ids=list(range(NCORES)))
    out = np.concatenate(
        [np.asarray(r["out"], np.float32).reshape(SEQ, T, C)
         for r in res.results], axis=0)
    return out


# version 31 (final)
# speedup vs baseline: 1.0734x; 1.0018x over previous
import numpy as np
import ml_dtypes
from contextlib import ExitStack

import concourse.bass as bass
import concourse.mybir as mybir
from concourse.bass_utils import run_bass_kernel_spmd

F32 = mybir.dt.float32
BF16 = mybir.dt.bfloat16
I32 = mybir.dt.int32
AF = mybir.ActivationFunctionType
ALU = mybir.AluOpType
AX = mybir.AxisListType.X

B, T, C = 16, 2048, 1024
H, HS = 16, 64
FF, R = 4096, 16
NCORES = 8
SEQ = 2
ROWS = SEQ * T
CC = C // 128
FC = FF // 128


class Sem:
    def __init__(self, es, nc, name):
        self.h = es.enter_context(nc.semaphore(name))
        self.n = 0


class SemPool:

    def __init__(self, es, nc, name, n):
        self.sems = [Sem(es, nc, f"{name}{i}") for i in range(n)]
        self.i = 0

    def take(self):
        s = self.sems[self.i % len(self.sems)]
        self.i += 1
        return s


class KB:
    ENGINES = ("sync", "gpsimd", "vector", "scalar", "tensor")

    def __init__(self, nc):
        self.nc = nc
        self.prog = {e: [] for e in self.ENGINES}
        self.serial = False
        self.esem = {}
        self.pools = {}

    def dma(self, eng, fn, waits=(), pool=None):
        s = self.pools[pool or eng].take()
        w = list(waits) + [(s, s.n)]
        self.op(eng, fn, waits=w, inc=s, inck=16)
        return (s, s.n)

    def op(self, eng, fn, waits=(), inc=None, inck=1):
        if self.serial and self.esem:
            waits = list(waits) + [(s, s.n) for e2, s in self.esem.items()
                                   if e2 != eng]
            if inc is None and eng in self.esem:
                inc = self.esem[eng]
                inck = 1
        if eng in ("vector", "scalar", "gpsimd") and eng in self.esem:
            waits = list(waits) + [(self.esem[eng], self.esem[eng].n)]
            if inc is None:
                inc = self.esem[eng]
                inck = 1
        waits = [(s, v) for (s, v) in waits if s is not None]
        w = tuple((s.h, v) for (s, v) in waits)
        if inc is not None:
            inc.n += inck

        def run(E, fn=fn, w=w, inc=inc, inck=inck):
            for (h, v) in w:
                if callable(v):
                    v = v()
                if v > 0:
                    E.wait_ge(h, v)
            inst = fn(E)
            if inc is not None:
                inst.then_inc(inc.h, inck)

        self.prog[eng].append(run)
        return inc.n if inc is not None else None

    def raw(self, eng, fn):
        self.prog[eng].append(fn)

    def emit(self):
        with self.nc.Block() as block:
            for eng in self.ENGINES:
                fns = self.prog[eng]
                if not fns:
                    continue

                def section(E, fns=fns):
                    for f in fns:
                        f(E)

                getattr(block, eng)(section)
        self.prog = {e: [] for e in self.ENGINES}


def build_program():
    nc = bass.Bass("TRN2", target_bir_lowering=False)

    d_x = nc.dram_tensor("x", [ROWS, C], F32, kind="ExternalInput").ap()
    d_xT = nc.dram_tensor("xT", [C, ROWS], F32, kind="ExternalInput").ap()
    d_wqkvT = nc.dram_tensor("wqkvT", [C, 3 * C], F32, kind="ExternalInput").ap()
    d_wprojT = nc.dram_tensor("wprojT", [C, C], F32, kind="ExternalInput").ap()
    d_bprow = nc.dram_tensor("bprow", [1, C], F32, kind="ExternalInput").ap()
    d_wfcT = nc.dram_tensor("wfcT", [C, FF], F32, kind="ExternalInput").ap()
    d_bfccol = nc.dram_tensor("bfccol", [128, FC], F32, kind="ExternalInput").ap()
    d_afcT = nc.dram_tensor("afcT", [C, R], F32, kind="ExternalInput").ap()
    d_bfcT = nc.dram_tensor("bfcT", [R, FF], F32, kind="ExternalInput").ap()
    d_wpF = nc.dram_tensor("wpF", [FF, C], F32, kind="ExternalInput").ap()
    d_bprow2 = nc.dram_tensor("bprow2", [1, C], F32, kind="ExternalInput").ap()
    d_aprojT = nc.dram_tensor("aprojT", [FF, R], F32, kind="ExternalInput").ap()
    d_bprojT = nc.dram_tensor("bprojT", [R, C], F32, kind="ExternalInput").ap()
    d_identb = nc.dram_tensor("identb", [128, 128], BF16, kind="ExternalInput").ap()
    d_identf = nc.dram_tensor("identf", [128, 128], F32, kind="ExternalInput").ap()
    d_onesf = nc.dram_tensor("onesf", [1, 128], F32, kind="ExternalInput").ap()
    d_ones16 = nc.dram_tensor("ones16", [128, 16], BF16, kind="ExternalInput").ap()
    d_onesb = nc.dram_tensor("onesb", [1, 128], BF16, kind="ExternalInput").ap()
    d_masks = nc.dram_tensor("masks", [128, 896], BF16, kind="ExternalInput").ap()
    d_out = nc.dram_tensor("out", [ROWS, C], F32, kind="ExternalOutput").ap()
    d_x1 = nc.dram_tensor("x1scr", [ROWS, C], F32).ap()
    d_scr8 = nc.dram_tensor("scr8", [1, 8], F32).ap()
    d_recb = nc.dram_tensor("recb", [1, 512], F32).ap()

    es = ExitStack()
    with es:
        pIN = SemPool(es, nc, "dIN", 4)
        pGP = SemPool(es, nc, "dGP", 2)
        pOUT = SemPool(es, nc, "dOUT", 2)
        NONE = (None, 0)
        sPE = Sem(es, nc, "sPE")
        sDV = Sem(es, nc, "sDV")
        sAC = Sem(es, nc, "sAC")
        sPL = Sem(es, nc, "sPL")

        g_identb = es.enter_context(nc.sbuf_tensor("g_identb", [128, 128], BF16))
        g_identf = es.enter_context(nc.sbuf_tensor("g_identf", [128, 128], F32))
        g_onesf = es.enter_context(nc.sbuf_tensor("g_onesf", [1, 128], F32))

        g_bprow2 = es.enter_context(nc.sbuf_tensor("g_bprow2", [1, C], BF16))
        g_onesb = es.enter_context(nc.sbuf_tensor("g_onesb", [1, 128], BF16))
        g_bfccol = es.enter_context(nc.sbuf_tensor("g_bfccol", [128, FC], F32))
        g_afcT = es.enter_context(nc.sbuf_tensor("g_afcT", [128, CC * R], BF16))
        g_bfcT = es.enter_context(nc.sbuf_tensor("g_bfcT", [R, FF], BF16))
        g_aprojT = es.enter_context(nc.sbuf_tensor("g_aprojT", [128, FC * R], BF16))
        g_bprojT = es.enter_context(nc.sbuf_tensor("g_bprojT", [R, C], BF16))
        g_pre = es.enter_context(nc.sbuf_tensor("g_pre", [128, C], F32))
        g_xo = es.enter_context(nc.sbuf_tensor("g_xo", [128, C], F32))
        g_sm = es.enter_context(nc.sbuf_tensor("g_sm", [128, 16], F32))

        g_srow = es.enter_context(nc.sbuf_tensor("g_srow", [1, 8], F32))
        g_scl = es.enter_context(nc.sbuf_tensor("g_scl", [128, 8], F32))
        g_am = es.enter_context(nc.sbuf_tensor("g_am", [128, 2], F32))
        g_scr = es.enter_context(nc.sbuf_tensor("g_scr", [128, 4], F32))

        psA = es.enter_context(nc.psum_tensor("psA", [128, 2048], F32))
        psO = es.enter_context(nc.psum_tensor("psO", [128, 512], F32))
        psBC = es.enter_context(nc.psum_tensor("psBC", [128, 512], F32))
        psT = es.enter_context(nc.psum_tensor("psT", [128, 512], F32))
        psU = es.enter_context(nc.psum_tensor("psU", [128, 512], F32))

        kb = KB(nc)
        kb.esem = {"tensor": sPE, "vector": sDV, "scalar": sAC,
                   "gpsimd": sPL}
        kb.pools = {"sync": pIN, "gpsimd": pGP, "out": pOUT}
        import os
        kb.serial = bool(int(os.environ.get("KSERIAL", "0")))
        _conc = set(os.environ.get("KCONC", "").split(","))

        def region(name):
            if os.environ.get("KSERIAL", "0") == "2":
                kb.serial = name not in _conc

        def dve_fence(ap):
            return sDV.n

        def act_fence(ap):
            return sAC.n

        setup_pairs = [(g_identb, d_identb), (g_identf, d_identf),
                       (g_onesf, d_onesf), (g_bfccol, d_bfccol),
                       (g_onesb, d_onesb)]
        for (dst, src) in setup_pairs:
            kb.dma("sync", lambda E, dst=dst, src=src:
                   E.dma_start(out=dst[:, :], in_=src))


        stg_state = {"free": [NONE, NONE]}

        def staged_cast(dst_ap, src_ap, n):
            slot = stg_state.setdefault("slot", 0)
            stg_state["slot"] = 1 - slot
            vx = kb.dma("sync", lambda E, slot=slot, src_ap=src_ap, n=n: E.dma_start(
                out=g_wstg[0:src_ap.shape[0], slot * 512: slot * 512 + n],
                in_=src_ap),
                waits=[stg_state["free"][slot]])
            vc = kb.op("vector", lambda E, dst_ap=dst_ap, slot=slot, n=n, src_ap=src_ap:
                       E.tensor_copy(dst_ap,
                                     g_wstg[0:src_ap.shape[0], slot * 512: slot * 512 + n]),
                       waits=[vx], inc=sDV)
            stg_state["free"][slot] = (sDV, vc)
            return vc


        esA = ExitStack()
        with esA:
            a_wqkv = esA.enter_context(nc.sbuf_tensor("a_wqkv", [128, CC * 3 * C], BF16))
            a_wproj = esA.enter_context(nc.sbuf_tensor("a_wproj", [128, CC * C], BF16))
            a_K = esA.enter_context(nc.sbuf_tensor("a_K", [128, CC * T], BF16))
            a_V = esA.enter_context(nc.sbuf_tensor("a_V", [128, 16 * (H * 65)], BF16))
            a_xT = esA.enter_context(nc.sbuf_tensor("a_xT", [128, CC * 512], BF16))
            a_xf = esA.enter_context(nc.sbuf_tensor("a_xf", [128, 2 * C], F32))
            a_Q = esA.enter_context(nc.sbuf_tensor("a_Q", [128, CC * 512], BF16))
            a_tE = esA.enter_context(nc.sbuf_tensor("a_tE", [128, 1024], F32))
            a_p2 = esA.enter_context(nc.sbuf_tensor("a_p2", [128, 2048], BF16))
            a_ph = esA.enter_context(nc.sbuf_tensor("a_ph", [128, 1024], BF16))
            a_O = esA.enter_context(nc.sbuf_tensor("a_O", [128, CC * 512], BF16))
            g_wstg = esA.enter_context(nc.sbuf_tensor("g_wstg", [128, 1024], F32))
            g_masks = esA.enter_context(nc.sbuf_tensor("g_masks", [128, 896], BF16))
            g_bprow = esA.enter_context(nc.sbuf_tensor("g_bprow", [1, C], BF16))
            g_ones16 = esA.enter_context(nc.sbuf_tensor("g_ones16", [128, 16], BF16))
            g_rec = esA.enter_context(nc.sbuf_tensor("g_rec", [1, 1024], F32))
            g_bc = esA.enter_context(nc.sbuf_tensor("g_bc", [64, 512], F32))
            for (dst, src) in [(g_ones16, d_ones16), (g_masks, d_masks)]:
                kb.dma("sync", lambda E, dst=dst, src=src:
                       E.dma_start(out=dst[:, :], in_=src))
            consts_waits = [(s, s.n) for s in pIN.sems]
            for cc in range(CC):
                staged_cast(g_afcT[:, cc * R:(cc + 1) * R],
                            d_afcT[cc * 128:(cc + 1) * 128, :], R)
            for hh in range(FF // 512):
                staged_cast(g_bfcT[0:16, hh * 512:(hh + 1) * 512],
                            d_bfcT[:, hh * 512:(hh + 1) * 512], 512)
            for fc in range(FC):
                staged_cast(g_aprojT[:, fc * R:(fc + 1) * R],
                            d_aprojT[fc * 128:(fc + 1) * 128, :], R)
            for hh in range(C // 512):
                staged_cast(g_bprojT[0:16, hh * 512:(hh + 1) * 512],
                            d_bprojT[:, hh * 512:(hh + 1) * 512], 512)
            for hh in range(C // 512):
                staged_cast(g_bprow[0:1, hh * 512:(hh + 1) * 512],
                            d_bprow[:, hh * 512:(hh + 1) * 512], 512)
                staged_cast(g_bprow2[0:1, hh * 512:(hh + 1) * 512],
                            d_bprow2[:, hh * 512:(hh + 1) * 512], 512)
            lora_dv = sDV.n

            for cc in range(CC):
                for hh in range(3 * C // 512):
                    staged_cast(a_wqkv[:, cc * 3 * C + hh * 512: cc * 3 * C + (hh + 1) * 512],
                                d_wqkvT[cc * 128:(cc + 1) * 128, hh * 512:(hh + 1) * 512], 512)
            for cc in range(CC):
                for hh in range(C // 512):
                    staged_cast(a_wproj[:, cc * C + hh * 512: cc * C + (hh + 1) * 512],
                                d_wprojT[cc * 128:(cc + 1) * 128, hh * 512:(hh + 1) * 512], 512)
            wqkv_dv = dve_fence(a_wproj[:, CC * C - 1: CC * C])

            st = {
                "xT_rd": 0,
                "psA_free": [0, 0, 0, 0],
                "psO_free": 0,
                "psU_free": 0,
                "psBC_free": 0,
                "xf_free": [0, 0],
                "ln_busy": 0,
                "ln_busy_ac": 0,
                "xo_free": NONE,
                "sm_free": 0,
            }

            def load_xT(t0, extra_waits=(), tag="R1"):
                region(tag)
                last = 0
                for cc in range(CC):
                    slot = stg_state.setdefault("slot", 0)
                    stg_state["slot"] = 1 - slot
                    vx = kb.dma("sync", lambda E, cc=cc, slot=slot, t0=t0: E.dma_start(
                        out=g_wstg[:, slot * 512:(slot + 1) * 512],
                        in_=d_xT[cc * 128:(cc + 1) * 128, t0:t0 + 512]),
                        waits=[stg_state["free"][slot],
                               (sPE, st["xT_rd"])] + (list(extra_waits) if cc == 0 else []))
                    vc = kb.op("vector", lambda E, cc=cc, slot=slot: E.tensor_copy(
                        a_xT[:, cc * 512:(cc + 1) * 512],
                        g_wstg[:, slot * 512:(slot + 1) * 512]),
                        waits=[vx], inc=sDV)
                    stg_state["free"][slot] = (sDV, vc)
                    last = dve_fence(a_xT[:, cc * 512 + 511: cc * 512 + 512])
                return last

            def load_xT_pool(t0):
                last = 0
                for cc in range(CC):
                    slot = stg_state.setdefault("slot", 0)
                    stg_state["slot"] = 1 - slot
                    vx = kb.dma("sync", lambda E, cc=cc, slot=slot, t0=t0: E.dma_start(
                        out=g_wstg[:, slot * 512:(slot + 1) * 512],
                        in_=d_xT[cc * 128:(cc + 1) * 128, t0:t0 + 512]),
                        waits=[stg_state["free"][slot], (sPE, st["xT_rd"])])
                    vc = kb.op("gpsimd", lambda E, cc=cc, slot=slot: E.tensor_copy(
                        a_xT[:, cc * 512:(cc + 1) * 512],
                        g_wstg[:, slot * 512:(slot + 1) * 512]),
                        waits=[vx], inc=sPL)
                    stg_state["free"][slot] = (sPL, vc)
                    last = vc
                return (sPL, last)

            def ln_and_store(ps_halves_waits, resid_cols, dst_rows, tag,
                             sqbuf=None):
                rbuf, rcol = resid_cols
                adds = []
                for oh in range(2):
                    ps_ap, w = ps_halves_waits[oh]
                    w = list(w)
                    if oh == 0:
                        w += [(sAC, st["ln_busy_ac"]), (sDV, st["ln_busy"])]
                    va = kb.op("vector", lambda E, ps_ap=ps_ap, oh=oh, rbuf=rbuf, rcol=rcol:
                               E.tensor_tensor(
                                   g_pre[:, oh * 512:(oh + 1) * 512], ps_ap,
                                   rbuf[:, rcol + oh * 512: rcol + (oh + 1) * 512],
                                   ALU.add),
                               waits=w, inc=sDV)
                    adds.append(va)
                vaddf = dve_fence(g_pre[:, 1023:1024])
                v1 = kb.op("vector", lambda E: E.tensor_reduce(
                    g_sm[:, 0:1], g_pre[:, :], AX, ALU.add),
                    waits=[(sDV, st["sm_free"])], inc=sDV)
                if sqbuf is None:
                    vsq = kb.op("gpsimd", lambda E: E.tensor_tensor(
                        g_xo[:, :], g_pre[:, :], g_pre[:, :], ALU.mult),
                        waits=[(sDV, vaddf), st["xo_free"]], inc=sPL)
                    v2 = kb.op("vector", lambda E: E.tensor_reduce(
                        g_sm[:, 1:2], g_xo[:, :], AX, ALU.add),
                        waits=[(sPL, vsq)], inc=sDV)
                else:
                    sq = sqbuf
                    vsq = kb.op("gpsimd", lambda E, sq=sq: E.tensor_tensor(
                        sq[:, 0:C], g_pre[:, :], g_pre[:, :], ALU.mult),
                        waits=[(sDV, vaddf)], inc=sPL)
                    v2 = kb.op("vector", lambda E, sq=sq: E.tensor_reduce(
                        g_sm[:, 1:2], sq[:, 0:C], AX, ALU.add),
                        waits=[(sPL, vsq)], inc=sDV)
                v3 = kb.op("vector", lambda E: E.tensor_scalar(
                    g_sm[:, 2:3], g_sm[:, 0:1], 1.0 / C, None, ALU.mult),
                    waits=[(sDV, v2)], inc=sDV)
                v4 = kb.op("vector", lambda E: E.tensor_tensor(
                    g_sm[:, 3:4], g_sm[:, 2:3], g_sm[:, 2:3], ALU.mult),
                    waits=[(sDV, v3)], inc=sDV)
                v5 = kb.op("vector", lambda E: E.scalar_tensor_tensor(
                    g_sm[:, 4:5], g_sm[:, 1:2], 1.0 / C, g_sm[:, 3:4],
                    ALU.mult, ALU.subtract),
                    waits=[(sDV, v4)], inc=sDV)
                v6 = kb.op("vector", lambda E: E.tensor_scalar(
                    g_sm[:, 4:5], g_sm[:, 4:5], 1.0, 1e-5, ALU.mult, ALU.add),
                    waits=[(sDV, v5)], inc=sDV)
                v6f = dve_fence(g_sm[:, 4:5])
                v7 = kb.op("scalar", lambda E: E.activation(
                    g_sm[:, 5:6], g_sm[:, 4:5], AF.Ln),
                    waits=[(sDV, v6f)], inc=sAC)
                v8 = kb.op("scalar", lambda E: E.activation(
                    g_sm[:, 6:7], g_sm[:, 5:6], AF.Exp, scale=-0.5),
                    waits=[(sAC, v7)], inc=sAC)
                v8f = act_fence(g_sm[:, 6:7])
                st["ln_busy_ac"] = v8f
                v9 = kb.op("vector", lambda E: E.tensor_scalar(
                    g_xo[:, :], g_pre[:, :], g_sm[:, 2:3], g_sm[:, 6:7],
                    ALU.subtract, ALU.mult),
                    waits=[(sAC, v8f), st["xo_free"]], inc=sDV)
                st["ln_busy"] = v9
                st["sm_free"] = v9
                v9f = dve_fence(g_xo[:, 1023:1024])
                dram, row0 = dst_rows
                vo = kb.dma("gpsimd", lambda E, dram=dram, row0=row0: E.dma_start(
                    out=dram[row0:row0 + 128, :], in_=g_xo[:, :]),
                    waits=[(sDV, v9f)], pool="out")
                st["xo_free"] = vo
                return v9, vo, adds[1]

            kv_ready = 0
            scores_last = 0

            for s in range(SEQ):
                base = s * T
                kv_war = [(sPE, sPE.n)] if s > 0 else []
                region("R1")
                for tg in range(4):
                    w = kv_war if tg == 0 else []
                    xw = ([(sDV, wqkv_dv)] + consts_waits) if (s == 0 and tg == 0) else []
                    xtg_ready = load_xT(base + tg * 512, extra_waits=w + xw)
                    for kf in range(CC):
                        bank = kf % 4
                        w = [(sDV, xtg_ready), (sDV, st["psA_free"][bank])]
                        if s == 0 and tg == 0 and kf == 0:
                            w.append((sDV, wqkv_dv))
                        for cc in range(CC):
                            kb.op("tensor", lambda E, bank=bank, cc=cc, kf=kf: E.matmul(
                                psA[:, bank * 512:(bank + 1) * 512],
                                a_wqkv[:, cc * 3 * C + C + kf * 128: cc * 3 * C + C + (kf + 1) * 128],
                                a_xT[:, cc * 512:(cc + 1) * 512],
                                start=(cc == 0), stop=(cc == CC - 1)),
                                waits=(w if cc == 0 else []),
                                inc=(sPE if cc == CC - 1 else None))
                            if cc == 0:
                                w = []
                        vk = kb.op("vector", lambda E, bank=bank, kf=kf, tg=tg: E.tensor_copy(
                            a_K[:, kf * T + tg * 512: kf * T + (tg + 1) * 512],
                            psA[:, bank * 512:(bank + 1) * 512]),
                            waits=[(sPE, sPE.n)], inc=sDV)
                        st["psA_free"][bank] = vk
                        dve_fence(a_K[:, kf * T + (tg + 1) * 512 - 1: kf * T + (tg + 1) * 512])
                    for tc in range(4):
                        tch = tg * 4 + tc
                        for fh in range(2):
                            bank = (tc * 2 + fh) % 4
                            w = [(sDV, st["psA_free"][bank])]
                            for cc in range(CC):
                                kb.op("tensor", lambda E, bank=bank, cc=cc, tc=tc, fh=fh: E.matmul(
                                    psA[:, bank * 512:(bank + 1) * 512],
                                    a_xT[:, cc * 512 + tc * 128: cc * 512 + (tc + 1) * 128],
                                    a_wqkv[:, cc * 3 * C + 2 * C + fh * 512: cc * 3 * C + 2 * C + (fh + 1) * 512],
                                    start=(cc == 0), stop=(cc == CC - 1)),
                                    waits=(w if cc == 0 else []),
                                    inc=(sPE if cc == CC - 1 else None))
                                if cc == 0:
                                    w = []
                            vv = kb.op("vector", lambda E, bank=bank, tch=tch, fh=fh: E.tensor_copy(
                                a_V[:, tch * (H * 65) + fh * 520: tch * (H * 65) + (fh + 1) * 520]
                                .rearrange("p (h d) -> p h d", h=8)[:, :, 0:64],
                                psA[:, bank * 512:(bank + 1) * 512]
                                .rearrange("p (h d) -> p h d", h=8)),
                                waits=[(sPE, sPE.n)], inc=sDV)
                            st["psA_free"][bank] = vv
                            st["xT_rd"] = sPE.n
                            dve_fence(a_V[:, tch * (H * 65) + (fh + 1) * 520 - 2: tch * (H * 65) + (fh + 1) * 520 - 1])
                        kb.op("vector", lambda E, tch=tch: E.tensor_copy(
                            a_V[:, tch * (H * 65): (tch + 1) * (H * 65)]
                            .rearrange("p (h d) -> p h d", h=16)[:, :, 64:65],
                            g_ones16[:, :].rearrange("p (h o) -> p h o", o=1)),
                            waits=(consts_waits if (s == 0 and tg == 0 and tc == 0) else []),
                            inc=sDV)
                        dve_fence(a_V[:, tch * (H * 65) + 15 * 65 + 64: tch * (H * 65) + 15 * 65 + 65])
                dve_fence(a_K[:, CC * T - 1: CC * T])
                kv_ready = dve_fence(a_V[:, 16 * H * 65 - 2: 16 * H * 65 - 1])

                for j in range(4):
                    q0 = base + j * 512
                    if j == 0:
                        xtq_ready = load_xT(q0, tag="R2a")
                        region("R2b")
                        for qf in range(CC):
                            bank = qf % 4
                            w = [(sDV, xtq_ready), (sDV, st["psA_free"][bank]),
                                 (sAC, st.get("exp_bank", [0, 0])[bank // 2])]
                            for cc in range(CC):
                                kb.op("tensor", lambda E, bank=bank, cc=cc, qf=qf: E.matmul(
                                    psA[:, bank * 512:(bank + 1) * 512],
                                    a_wqkv[:, cc * 3 * C + qf * 128: cc * 3 * C + (qf + 1) * 128],
                                    a_xT[:, cc * 512:(cc + 1) * 512],
                                    start=(cc == 0), stop=(cc == CC - 1)),
                                    waits=(w if cc == 0 else []),
                                    inc=(sPE if cc == CC - 1 else None))
                                if cc == 0:
                                    w = []
                            st["xT_rd"] = sPE.n
                            vq = kb.op("vector", lambda E, bank=bank, qf=qf: E.tensor_copy(
                                a_Q[:, qf * 512:(qf + 1) * 512],
                                psA[:, bank * 512:(bank + 1) * 512]),
                                waits=[(sPE, sPE.n)], inc=sDV)
                            st["psA_free"][bank] = vq
                            dve_fence(a_Q[:, qf * 512 + 511: qf * 512 + 512])
                        st["q_rdy"] = (sDV, sDV.n)
                    q_ready_t = st["q_rdy"]

                    xf_dma = [NONE] * 4
                    for tc in range(2):
                        xf_dma[tc] = kb.dma("sync", lambda E, tc=tc, q0=q0: E.dma_start(
                            out=a_xf[:, (tc % 2) * C:((tc % 2) + 1) * C],
                            in_=d_x[q0 + tc * 128: q0 + (tc + 1) * 128, :]),
                            waits=[(sDV, st["xf_free"][tc % 2])])

                    region("R3")
                    jobs = []
                    for h in range(H):
                        for gg in range(2 * (j + 1)):
                            jobs.append((h, gg))
                    LA = 3
                    sq_of = {}
                    avpe_of = {}
                    pending = []

                    weave = j < 3
                    if weave:
                        xtq_next = load_xT_pool(base + (j + 1) * 512)

                    def emit_qb_chunk(k, scores_pe):
                        wq0 = [xtq_next, (sAC, st.get("psT_q", 0)),
                               (sPE, scores_pe)]
                        for cc in range(CC):
                            kb.op("tensor", lambda E, cc=cc, k=k: E.matmul(
                                psT[:, 0:512],
                                a_wqkv[:, cc * 3 * C + k * 128: cc * 3 * C + (k + 1) * 128],
                                a_xT[:, cc * 512:(cc + 1) * 512],
                                start=(cc == 0), stop=(cc == CC - 1)),
                                waits=(wq0 if cc == 0 else []),
                                inc=(sPE if cc == CC - 1 else None))
                        st["xT_rd"] = sPE.n
                        vq = kb.op("scalar", lambda E, k=k: E.activation(
                            a_Q[:, k * 512:(k + 1) * 512], psT[:, 0:512],
                            AF.Identity),
                            waits=[(sPE, sPE.n)], inc=sAC)
                        st["psT_q"] = vq
                        if k == CC - 1:
                            st["q_rdy"] = (sAC, vq)

                    def emit_scores_softplus(idx):
                        h, gg = jobs[idx]
                        pb = (h % 2) * 64
                        kchunk = h // 2
                        pp = idx % 2
                        w0 = [(sAC, st.get("exp_bank", [0, 0])[pp]),
                              q_ready_t, (sDV, kv_ready)]
                        for r2 in range(2):
                            kc = 2 * gg + r2
                            kb.op("tensor", lambda E, r2=r2, kc=kc, pb=pb, kchunk=kchunk, pp=pp: E.matmul(
                                psA[:, (2 * pp + r2) * 512:(2 * pp + r2 + 1) * 512],
                                a_K[pb:pb + 64, kchunk * T + kc * 128: kchunk * T + (kc + 1) * 128],
                                a_Q[pb:pb + 64, kchunk * 512:(kchunk + 1) * 512],
                                start=True, stop=True),
                                waits=(w0 if r2 == 0 else []),
                                inc=(sPE if r2 == 1 else None))
                        scores_done = sPE.n
                        if weave and h % 2 == 1 and gg == 2 * (j + 1) - 1:
                            emit_qb_chunk(h // 2, scores_done)
                        ve = kb.op("scalar", lambda E, pp=pp: E.activation(
                            a_tE[:, :], psA[:, pp * 1024:(pp + 1) * 1024],
                            AF.Exp, scale=0.125),
                            waits=[(sPE, scores_done)], inc=sAC)
                        st.setdefault("exp_bank", [0, 0])[pp] = ve
                        vl = kb.op("scalar", lambda E: E.activation(
                            a_ph[:, :], a_tE[:, :], AF.Ln, bias=1.0),
                            waits=[(sDV, st.get("ph_sq", 0))], inc=sAC)
                        vsq = kb.op("vector", lambda E, pp=pp: E.tensor_tensor(
                            a_p2[:, pp * 1024:(pp + 1) * 1024],
                            a_ph[:, :], a_ph[:, :], ALU.mult),
                            waits=[(sAC, vl),
                                   (sPE, (lambda i2=idx - 2, d=avpe_of:
                                          d.get(i2, 0))
                                    if idx >= 2 else 0)],
                            inc=sDV)
                        st["ph_sq"] = vsq
                        ngg = 2 * (j + 1)
                        if gg >= ngg - 2:
                            for r2 in range(2):
                                r4 = 2 * gg + r2 - 4 * j
                                vsq = kb.op("vector", lambda E, pp=pp, r2=r2, r4=r4: E.tensor_tensor(
                                    a_p2[:, pp * 1024 + r2 * 512: pp * 1024 + (r2 + 1) * 512],
                                    a_p2[:, pp * 1024 + r2 * 512: pp * 1024 + (r2 + 1) * 512],
                                    g_masks[:, 384 - 128 * r4: 896 - 128 * r4],
                                    ALU.mult), inc=sDV)
                        sq_of[idx] = vsq

                    def emit_av(idx):
                        h, gg = jobs[idx]
                        pp = idx % 2
                        psAV = psO if h % 2 == 0 else psU
                        avfree_key = "psO_free" if h % 2 == 0 else "psU_free"
                        wav = [(sDV, sq_of[idx])]
                        if gg == 0:
                            wav.append((sDV, st[avfree_key]))
                        for r2 in range(2):
                            kc = 2 * gg + r2
                            kb.op("tensor", lambda E, kc=kc, r2=r2, h=h, pp=pp, psAV=psAV: E.matmul(
                                psAV[0:65, :],
                                a_V[:, kc * (H * 65) + h * 65: kc * (H * 65) + (h + 1) * 65],
                                a_p2[:, pp * 1024 + r2 * 512: pp * 1024 + (r2 + 1) * 512],
                                start=(kc == 0), stop=(kc == 4 * j + 3),
                                skip_group_check=True),
                                waits=(wav if r2 == 0 else []),
                                inc=(sPE if r2 == 1 else None))
                        avpe_of[idx] = sPE.n
                        if gg == 2 * (j + 1) - 1:
                            for (rid, fn) in list(pending):
                                pending.remove((rid, fn))
                                fn()
                            emit_norm_part1(idx, h)

                    def emit_norm_part1(idx, h):
                        psAV = psO if h % 2 == 0 else psU
                        av_done = avpe_of[idx]
                        vr1 = kb.op("vector", lambda E, psAV=psAV: E.tensor_scalar(
                            g_rec[0:1, 0:512], psAV[64:65, :], 1e-9, None, ALU.add),
                            waits=[(sPE, av_done),
                                   (sAC, st.get("rec_ln_rd", 0)),
                                   st.get("bc_dma1", NONE)], inc=sDV)
                        vl1 = kb.op("scalar", lambda E: E.activation(
                            g_rec[0:1, 512:1024], g_rec[0:1, 0:512], AF.Ln),
                            waits=[(sDV, vr1)], inc=sAC)
                        st["rec_ln_rd"] = vl1
                        vl2 = kb.op("scalar", lambda E: E.activation(
                            g_rec[0:1, 0:512], g_rec[0:1, 512:1024],
                            AF.Exp, scale=-1.0), inc=sAC)
                        vl2f = act_fence(g_rec[0:1, 511:512])
                        pending.append((idx + 3,
                                        lambda h=h, vl2f=vl2f: emit_norm_part2(h, vl2f)))

                    def emit_norm_part2(h, vl2f):
                        psAV = psO if h % 2 == 0 else psU
                        avfree_key = "psO_free" if h % 2 == 0 else "psU_free"
                        pb = (h % 2) * 64
                        kchunk = h // 2
                        d1 = kb.dma("sync", lambda E: E.dma_start(
                            out=d_recb, in_=g_rec[0:1, 0:512]),
                            waits=[(sAC, vl2f)])
                        st["bc_dma1"] = d1
                        d2 = kb.dma("sync", lambda E: E.dma_start(
                            out=g_bc[:, :], in_=d_recb.broadcast_to([64, 512])),
                            waits=[d1, (sDV, st.get("bc_free", 0))])
                        vm = kb.op("vector", lambda E, pb=pb, kchunk=kchunk, psAV=psAV: E.tensor_tensor(
                            a_O[pb:pb + 64, kchunk * 512:(kchunk + 1) * 512],
                            psAV[0:64, :], g_bc[:, :], ALU.mult),
                            waits=[d2], inc=sDV)
                        st[avfree_key] = vm
                        st["bc_free"] = vm

                    n = len(jobs)
                    for idx in range(n + LA):
                        for (rid, fn) in [p for p in pending if p[0] <= idx]:
                            pending.remove((rid, fn))
                            fn()
                        if LA <= idx < n + LA:
                            emit_av(idx - LA)
                        if idx < n:
                            emit_scores_softplus(idx)
                    for (rid, fn) in list(pending):
                        pending.remove((rid, fn))
                        fn()
                    assert not pending
                    O_ready = sDV.n

                    region("R5")
                    for tc in range(4):
                        pair = (psO, psBC)
                        fkeys = ("psO_free", "psBC_free")
                        proj_done = [0, 0]
                        for oh in range(2):
                            ps = pair[oh]
                            wfree = [(sDV, st.get(fkeys[oh], 0))]
                            w = [(sDV, O_ready)] + wfree
                            if s == 0 and j == 0 and tc == 0:
                                w.extend(consts_waits)
                            kb.op("tensor", lambda E, ps=ps, oh=oh: E.matmul(
                                ps[:, :], g_onesb[0:1, 0:128],
                                g_bprow[0:1, oh * 512:(oh + 1) * 512],
                                start=True, stop=False), waits=w, inc=None)
                            for cc in range(CC):
                                kb.op("tensor", lambda E, ps=ps, cc=cc, tc=tc, oh=oh: E.matmul(
                                    ps[:, :],
                                    a_O[:, cc * 512 + tc * 128: cc * 512 + (tc + 1) * 128],
                                    a_wproj[:, cc * C + oh * 512: cc * C + (oh + 1) * 512],
                                    start=False, stop=(cc == CC - 1)),
                                    inc=(sPE if cc == CC - 1 else None))
                            proj_done[oh] = sPE.n
                        slot = tc % 2
                        va, vo, vadds = ln_and_store(
                            [(pair[0][:, :], [(sPE, proj_done[0]), xf_dma[tc]]),
                             (pair[1][:, :], [(sPE, proj_done[1])])],
                            (a_xf, slot * C),
                            (d_x1, q0 + tc * 128), "ln1", sqbuf=None)
                        st[fkeys[0]] = vadds
                        st[fkeys[1]] = vadds
                        st["xf_free"][slot] = vadds
                        if tc + 2 < 4:
                            xf_dma[tc + 2] = kb.dma("sync", lambda E, tc=tc, q0=q0, slot=slot: E.dma_start(
                                out=a_xf[:, slot * C:(slot + 1) * C],
                                in_=d_x[q0 + (tc + 2) * 128: q0 + (tc + 3) * 128, :]),
                                waits=[(sDV, va)])

            attn_done = dict(pe=sPE.n, dv=sDV.n, ac=sAC.n,
                             do=[(s, s.n) for s in pOUT.sems])
            kb.emit()

        esB = ExitStack()
        with esB:
            b_Wfc = esB.enter_context(nc.sbuf_tensor("b_Wfc", [128, CC * FF], BF16))
            b_Wp = esB.enter_context(nc.sbuf_tensor("b_Wp", [128, FC * C], BF16))
            b_ws = esB.enter_context(nc.sbuf_tensor("b_ws", [128, 3 * 1024], F32))
            b_z = esB.enter_context(nc.sbuf_tensor("b_z", [128, 1024], F32))
            b_zi = esB.enter_context(nc.sbuf_tensor("b_zi", [128, 1024], I32))
            b_d = esB.enter_context(nc.sbuf_tensor("b_d", [128, 1024], F32))
            b_r = esB.enter_context(nc.sbuf_tensor("b_r", [128, 1024], F32))
            b_x1f = esB.enter_context(nc.sbuf_tensor("b_x1f", [128, 4 * C], F32))
            b_X1T = esB.enter_context(nc.sbuf_tensor("b_X1T", [128, CC * 256], BF16))
            b_h = esB.enter_context(nc.sbuf_tensor("b_h", [128, 512], BF16))
            b_u = esB.enter_context(nc.sbuf_tensor("b_u", [16, 512], BF16))

            W = dict(pe=[(sPE, attn_done["pe"])], dv=[(sDV, attn_done["dv"])],
                     ac=[(sAC, attn_done["ac"])])
            barrier = W["pe"] + W["dv"] + W["ac"] + attn_done["do"]

            def w_steps(which):
                if which == 0:
                    return [(d_wfcT[cc * 128:(cc + 1) * 128, hh * 1024:(hh + 1) * 1024],
                             b_Wfc[:, cc * FF + hh * 1024: cc * FF + (hh + 1) * 1024], 1024)
                            for hh in range(4) for cc in range(CC)]
                return [(d_wpF[fc * 128:(fc + 1) * 128, :],
                         b_Wp[:, fc * C:(fc + 1) * C], 1024)
                        for fc in range(FC)]

            ws_free = [attn_done["dv"]] * 6
            am_v = [0, 0]
            accs = [(g_am[:, 0:1], g_am[:, 1:2]), (g_scr[:, 0:1], g_scr[:, 1:2])]
            for which in range(2):
                hsteps = []
                for (src, dst, n) in w_steps(which):
                    hsteps.append(src[:, 0:512])
                    hsteps.append(src[:, 512:n])
                red_prev = [0, 0]
                for i, hsrc in enumerate(hsteps):
                    slot = i % 6
                    ch = i % 2
                    accb, tmpb = accs[ch]
                    vd = kb.dma("sync", lambda E, hsrc=hsrc, slot=slot: E.dma_start(
                        out=b_ws[:, slot * 512:(slot + 1) * 512], in_=hsrc),
                        waits=[(sDV, ws_free[slot])] +
                              (barrier if (which == 0 and i == 0) else []))
                    vr = kb.op("vector", lambda E, slot=slot, tmpb=tmpb: E.tensor_reduce(
                        tmpb, b_ws[:, slot * 512:(slot + 1) * 512],
                        AX, ALU.max, apply_absolute_value=True),
                        waits=[vd, (sDV, red_prev[ch])], inc=sDV)
                    if i < 2:
                        vacc = kb.op("vector", lambda E, accb=accb, tmpb=tmpb: E.tensor_scalar(
                            accb, tmpb, 1.0, None, ALU.mult),
                            waits=[(sDV, vr)], inc=sDV)
                    else:
                        vacc = kb.op("vector", lambda E, accb=accb, tmpb=tmpb: E.tensor_tensor(
                            accb, accb, tmpb, ALU.max),
                            waits=[(sDV, vr)], inc=sDV)
                    red_prev[ch] = vacc
                    ws_free[slot] = vacc
                vmrg = kb.op("vector", lambda E: E.tensor_tensor(
                    g_am[:, 0:1], g_am[:, 0:1], g_scr[:, 0:1], ALU.max),
                    waits=[(sDV, red_prev[0]), (sDV, red_prev[1])], inc=sDV)
                vt = kb.op("tensor", lambda E: E.transpose(
                    psU[0:1, 0:128], g_am[:, 0:1], g_identf[:, :]),
                    waits=[(sDV, vmrg)], inc=sPE)
                vx = kb.op("vector", lambda E, which=which: E.tensor_reduce(
                    g_srow[0:1, 4 * which: 4 * which + 1], psU[0:1, 0:128],
                    AX, ALU.max), waits=[(sPE, vt)], inc=sDV)
                q_p1_psU = vx
                v1 = kb.op("vector", lambda E, which=which: E.reciprocal(
                    g_srow[0:1, 4 * which + 1: 4 * which + 2],
                    g_srow[0:1, 4 * which: 4 * which + 1]),
                    waits=[(sDV, vx)], inc=sDV)
                v2 = kb.op("vector", lambda E, which=which: E.tensor_scalar(
                    g_srow[0:1, 4 * which + 1: 4 * which + 2],
                    g_srow[0:1, 4 * which + 1: 4 * which + 2], 127.0, None,
                    ALU.mult), waits=[(sDV, v1)], inc=sDV)
                v3 = kb.op("vector", lambda E, which=which: E.tensor_scalar(
                    g_srow[0:1, 4 * which + 2: 4 * which + 3],
                    g_srow[0:1, 4 * which: 4 * which + 1], 1.0 / 127.0, None,
                    ALU.mult), waits=[(sDV, v2)], inc=sDV)
                v4 = kb.op("vector", lambda E, which=which: E.tensor_scalar(
                    g_srow[0:1, 4 * which + 3: 4 * which + 4],
                    g_srow[0:1, 4 * which: 4 * which + 1], -1.0 / 127.0, None,
                    ALU.mult), waits=[(sDV, v3)], inc=sDV)
                am_v[which] = v4
            vsrf = dve_fence(g_srow[0:1, 7:8])
            vb1 = kb.dma("gpsimd", lambda E: E.dma_start(out=d_scr8, in_=g_srow[0:1, :]),
                         waits=[(sDV, vsrf)])
            scl_ready = kb.dma("gpsimd", lambda E: E.dma_start(
                out=g_scl[:, :], in_=d_scr8.broadcast_to([128, 8])),
                waits=[vb1])

            ws_free2 = [(sDV, max(ws_free[0], ws_free[1])),
                        (sDV, max(ws_free[2], ws_free[3]))]
            qst = {"qi": 0, "first": True}
            qfc_gate = [0, 0, 0, 0]
            wp_gate = [0] * FC

            def emit_p2_step(which, src, dst, n):
                sc = 4 * which
                slot = qst["qi"] % 2
                qst["qi"] += 1
                vd = kb.dma("sync", lambda E, src=src, slot=slot, n=n: E.dma_start(
                    out=b_ws[:, slot * 1024: slot * 1024 + n], in_=src),
                    waits=[ws_free2[slot]])
                wq = [vd]
                if qst["first"]:
                    wq.append(scl_ready)
                    qst["first"] = False
                vz = kb.op("vector", lambda E, slot=slot, n=n, sc=sc: E.tensor_scalar(
                    b_d[:, 0:n], b_ws[:, slot * 1024: slot * 1024 + n],
                    g_scl[:, sc + 1:sc + 2], None, ALU.mult),
                    waits=wq, inc=sDV)
                ws_free2[slot] = (sDV, vz)
                kb.op("vector", lambda E, n=n: E.tensor_copy(
                    b_zi[:, 0:n], b_d[:, 0:n]))
                kb.op("vector", lambda E, n=n: E.tensor_copy(
                    b_r[:, 0:n], b_zi[:, 0:n]))
                kb.op("vector", lambda E, n=n: E.tensor_tensor(
                    b_d[:, 0:n], b_d[:, 0:n], b_r[:, 0:n], ALU.subtract))
                kb.op("vector", lambda E, n=n: E.scalar_tensor_tensor(
                    b_r[:, 0:n], b_d[:, 0:n], 0.5, b_r[:, 0:n],
                    ALU.is_ge, ALU.add))
                kb.op("vector", lambda E, n=n: E.scalar_tensor_tensor(
                    b_r[:, 0:n], b_d[:, 0:n], -0.5, b_r[:, 0:n],
                    ALU.is_ge, ALU.add))
                vfin = kb.op("vector", lambda E, dst=dst, n=n, sc=sc: E.tensor_scalar(
                    dst, b_r[:, 0:n], g_scl[:, sc + 2:sc + 3],
                    g_scl[:, sc + 3:sc + 4], ALU.mult, ALU.add), inc=sDV)
                return vfin

            wfc_steps = w_steps(0)
            wp_steps = w_steps(1)

            def emit_p2_all():
                fi = 0
                pi = 0
                for hh in range(4):
                    for _ in range(8):
                        vfin = emit_p2_step(0, *wfc_steps[fi])
                        fi += 1
                    qfc_gate[hh] = vfin
                    nwp = 4 if hh < 3 else 20
                    for _ in range(nwp):
                        if pi >= FC:
                            break
                        wp_gate[pi] = emit_p2_step(1, *wp_steps[pi])
                        pi += 1

            stB = {
                "x1f_free": [attn_done["dv"]] * 4,
                "x1rd": [attn_done["pe"]] * 4,
                "psT_free": attn_done["dv"],
                "X1T_free": attn_done["pe"],
                "h_free": [attn_done["pe"], attn_done["pe"]],
                "psF_free": [attn_done["ac"], attn_done["ac"]],
                "psA_free": attn_done["dv"],
                "psU_free": attn_done["dv"],
                "ln_busy": st["ln_busy"], "ln_busy_ac": st["ln_busy_ac"],
                "xo_free": st["xo_free"], "sm_free": st["sm_free"],
                "lnq": [],
            }
            st.update({})

            x1_all = attn_done["do"]

            gelu_done = {}

            def tile_prologue(s, it):
                rows0 = s * T + it * 256
                tp = (s * 8 + it) % 2
                x1v = [NONE, NONE]
                for tc in range(2):
                    fslot = tp * 2 + tc
                    x1v[tc] = kb.dma("sync", lambda E, tc=tc, rows0=rows0, fslot=fslot: E.dma_start(
                        out=b_x1f[:, fslot * C:(fslot + 1) * C],
                        in_=d_x1[rows0 + tc * 128: rows0 + (tc + 1) * 128, :]),
                        waits=[(sDV, stB["x1f_free"][fslot]),
                               (sPE, stB["x1rd"][fslot])] +
                              (x1_all if (s == 0 and it == 0) else []))
                    for b2 in range(2):
                        for i4 in range(4):
                            i = b2 * 4 + i4
                            w = []
                            if i4 == 0:
                                w = [x1v[tc], (sDV, stB["psT_free"]),
                                     (sPE, stB["X1T_free"])]
                            kb.op("tensor", lambda E, fslot=fslot, i=i, i4=i4: E.transpose(
                                psT[:, i4 * 128:(i4 + 1) * 128],
                                b_x1f[:, fslot * C + i * 128: fslot * C + (i + 1) * 128],
                                g_identf[:, :]),
                                waits=w, inc=(sPE if i4 == 3 else None))
                        ve = kb.op("vector", lambda E, tc=tc, b2=b2: E.tensor_copy(
                            b_X1T[:, :].rearrange("p (c t) -> p c t", c=CC)
                            [:, b2 * 4:(b2 + 1) * 4, tc * 128:(tc + 1) * 128],
                            psT[:, :].rearrange("p (c t) -> p c t", c=4)),
                            waits=[(sPE, sPE.n)], inc=sDV)
                        stB["psT_free"] = ve
                    stB["x1rd"][fslot] = sPE.n
                    dve_fence(b_X1T[:, 7 * 256 + tc * 128 + 127: 7 * 256 + tc * 128 + 128])
                x1t_ready = sDV.n
                wu = [(sDV, x1t_ready), (sDV, stB["psU_free"]),
                      (sDV, q_p1_psU)]
                for cc in range(CC):
                    kb.op("tensor", lambda E, cc=cc: E.matmul(
                        psU[0:16, 0:256],
                        g_afcT[:, cc * R:(cc + 1) * R],
                        b_X1T[:, cc * 256:(cc + 1) * 256],
                        start=(cc == 0), stop=(cc == CC - 1)),
                        waits=(wu if cc == 0 else []),
                        inc=(sPE if cc == CC - 1 else None))
                    if cc == 0:
                        wu = []
                vu = kb.op("vector", lambda E: E.tensor_scalar(
                    b_u[0:16, 0:256], psU[0:16, 0:256], 2.0, None, ALU.mult),
                    waits=[(sPE, sPE.n)], inc=sDV)
                u_ready = dve_fence(b_u[0:16, 255:256])
                return u_ready, tp, rows0

            tiles = [(s, it) for s in range(SEQ) for it in range(8)]
            pro = tile_prologue(*tiles[0])
            emit_p2_all()
            for tix, (s, it) in enumerate(tiles):
                if True:
                    u_ready, tp, rows0 = pro


                    proj_state = {"first": True}

                    def emit_fc(f):
                        pp = f % 2
                        psF = psO if pp == 0 else psBC
                        wfc = [(sAC, stB["psF_free"][pp]), (sDV, u_ready),
                               (sDV, qfc_gate[f // CC])]
                        for cc in range(CC):
                            kb.op("tensor", lambda E, psF=psF, cc=cc, f=f: E.matmul(
                                psF[:, 0:256],
                                b_Wfc[:, cc * FF + f * 128: cc * FF + (f + 1) * 128],
                                b_X1T[:, cc * 256:(cc + 1) * 256],
                                start=(cc == 0), stop=False),
                                waits=(wfc if cc == 0 else []))
                            if cc == 0:
                                wfc = []
                        kb.op("tensor", lambda E, psF=psF, f=f: E.matmul(
                            psF[:, 0:256], g_bfcT[0:16, f * 128:(f + 1) * 128],
                            b_u[0:16, 0:256], start=False, stop=True), inc=sPE)
                        return sPE.n

                    def emit_gelu(f, fc_done):
                        pp = f % 2
                        psF = psO if pp == 0 else psBC
                        vg = kb.op("scalar", lambda E, psF=psF, pp=pp, f=f: E.activation(
                            b_h[:, pp * 256:(pp + 1) * 256], psF[:, 0:256],
                            AF.Gelu, bias=g_bfccol[:, f:f + 1]),
                            waits=[(sPE, fc_done), (sPE, stB["h_free"][pp])],
                            inc=sAC)
                        stB["psF_free"][pp] = vg
                        vgf = act_fence(b_h[:, pp * 256 + 255: pp * 256 + 256])
                        gelu_done[f] = vgf

                    def emit_proj_u2(f):
                        pp = f % 2
                        wpj = [(sAC, gelu_done[f]), (sDV, wp_gate[f])]
                        if proj_state["first"]:
                            wpj.append((sDV, stB["psA_free"]))
                            for q in range(4):
                                oh = q % 2
                                kb.op("tensor", lambda E, q=q, oh=oh: E.matmul(
                                    psA[:, q * 512:(q + 1) * 512],
                                    g_onesb[0:1, 0:128],
                                    g_bprow2[0:1, oh * 512:(oh + 1) * 512],
                                    start=True, stop=False),
                                    waits=(wpj if q == 0 else []))
                                if q == 0:
                                    wpj = []
                            proj_state["first"] = False
                        for tc2 in range(2):
                            for oh in range(2):
                                q = tc2 * 2 + oh
                                kb.op("tensor", lambda E, q=q, tc2=tc2, oh=oh, pp=pp, f=f: E.matmul(
                                    psA[:, q * 512:(q + 1) * 512],
                                    b_h[:, pp * 256 + tc2 * 128: pp * 256 + (tc2 + 1) * 128],
                                    b_Wp[:, f * C + oh * 512: f * C + (oh + 1) * 512],
                                    start=False, stop=False),
                                    waits=(wpj if q == 0 else []))
                                if q == 0:
                                    wpj = []
                        kb.op("tensor", lambda E, f=f, pp=pp: E.matmul(
                            psU[0:16, 256:512],
                            g_aprojT[:, f * R:(f + 1) * R],
                            b_h[:, pp * 256:(pp + 1) * 256],
                            start=(f == 0), stop=(f == FC - 1)),
                            inc=sPE)
                        stB["h_free"][pp] = sPE.n

                    proj_state["first"] = True
                    fcd = emit_fc(0)
                    emit_gelu(0, fcd)
                    for f in range(1, FC):
                        if f == 1:
                            for epi_fn in stB["lnq"]:
                                epi_fn()
                            stB["lnq"] = []
                        fcd = emit_fc(f)
                        emit_proj_u2(f - 1)
                        emit_gelu(f, fcd)
                    emit_proj_u2(FC - 1)
                    vu2 = kb.op("vector", lambda E: E.tensor_scalar(
                        b_u[0:16, 256:512], psU[0:16, 256:512], 2.0, None,
                        ALU.mult), waits=[(sPE, sPE.n)], inc=sDV)
                    stB["psU_free"] = vu2
                    vu2f = dve_fence(b_u[0:16, 511:512])
                    wl = [(sDV, vu2f)]
                    for tc2 in range(2):
                        for oh in range(2):
                            q = tc2 * 2 + oh
                            kb.op("tensor", lambda E, q=q, tc2=tc2, oh=oh: E.matmul(
                                psA[:, q * 512:(q + 1) * 512],
                                b_u[0:16, 256 + tc2 * 128: 256 + (tc2 + 1) * 128],
                                g_bprojT[0:16, oh * 512:(oh + 1) * 512],
                                start=False, stop=True),
                                waits=(wl if q == 0 else []),
                                inc=(sPE if q == 3 else None))
                            if q == 0:
                                wl = []
                    ffn_done = sPE.n
                    if tix + 1 < len(tiles):
                        pro = tile_prologue(*tiles[tix + 1])

                    def epi(ffn_done=ffn_done, tp=tp, rows0=rows0):
                        st["ln_busy"] = stB["ln_busy"]
                        st["ln_busy_ac"] = stB["ln_busy_ac"]
                        st["xo_free"] = stB["xo_free"]
                        st["sm_free"] = stB["sm_free"]
                        last_apply = 0
                        for tc2 in range(2):
                            va, vo, vadds = ln_and_store(
                                [(psA[:, (tc2 * 2) * 512:(tc2 * 2 + 1) * 512],
                                  [(sPE, ffn_done)]),
                                 (psA[:, (tc2 * 2 + 1) * 512:(tc2 * 2 + 2) * 512], [])],
                                (b_x1f, (tp * 2 + tc2) * C),
                                (d_out, rows0 + tc2 * 128), "ln2", sqbuf=b_z)
                            last_apply = vadds
                            stB["x1f_free"][tp * 2 + tc2] = va
                        stB["psA_free"] = last_apply
                        stB["ln_busy"] = st["ln_busy"]
                        stB["ln_busy_ac"] = st["ln_busy_ac"]
                        stB["xo_free"] = st["xo_free"]
                        stB["sm_free"] = st["sm_free"]
                    stB["lnq"].append(epi)

            for epi_fn in stB["lnq"]:
                epi_fn()
            stB["lnq"] = []

            finals = [(s.h, s.n) for s in pOUT.sems]
            kb.raw("gpsimd", lambda E: [E.wait_ge(h, v) for (h, v) in finals if v > 0])
            kb.emit()

    return nc


_NC_CACHE = [None]


def _masks_host():
    kk = np.arange(128).reshape(128, 1)
    m = np.arange(896).reshape(1, 896)
    return (kk <= m - 384).astype(np.float32).astype(ml_dtypes.bfloat16)


def kernel(L0_attn_x, L0_attn_w_qkv, L0_attn_w_proj, L0_attn_b_proj,
           L0_ffn_W_fc, L0_ffn_b_fc, L0_ffn_A_fc, L0_ffn_B_fc,
           L0_ffn_W_proj, L0_ffn_b_proj, L0_ffn_A_proj, L0_ffn_B_proj):
    f32 = lambda a: np.ascontiguousarray(np.asarray(a, dtype=np.float32))
    x = f32(L0_attn_x).reshape(B, T, C)
    shared = {
        "wqkvT": f32(np.asarray(L0_attn_w_qkv, np.float32).T),
        "wprojT": f32(np.asarray(L0_attn_w_proj, np.float32).T),
        "bprow": f32(L0_attn_b_proj).reshape(1, C),
        "wfcT": f32(np.asarray(L0_ffn_W_fc, np.float32).T),
        "bfccol": f32(np.asarray(L0_ffn_b_fc, np.float32).reshape(FC, 128).T),
        "afcT": f32(np.asarray(L0_ffn_A_fc, np.float32).T),
        "bfcT": f32(np.asarray(L0_ffn_B_fc, np.float32).T),
        "wpF": f32(np.asarray(L0_ffn_W_proj, np.float32).T),
        "bprow2": f32(L0_ffn_b_proj).reshape(1, C),
        "aprojT": f32(np.asarray(L0_ffn_A_proj, np.float32).T),
        "bprojT": f32(np.asarray(L0_ffn_B_proj, np.float32).T),
        "identb": np.eye(128, dtype=ml_dtypes.bfloat16),
        "identf": np.eye(128, dtype=np.float32),
        "onesf": np.ones((1, 128), np.float32),
        "onesb": np.ones((1, 128), ml_dtypes.bfloat16),
        "ones16": np.ones((128, 16), ml_dtypes.bfloat16),
        "masks": _masks_host(),
    }
    if _NC_CACHE[0] is None:
        _NC_CACHE[0] = build_program()
    nc = _NC_CACHE[0]
    in_maps = []
    for c in range(NCORES):
        m = dict(shared)
        m["x"] = np.ascontiguousarray(
            x[c * SEQ:(c + 1) * SEQ].reshape(ROWS, C))
        m["xT"] = np.ascontiguousarray(m["x"].T)
        in_maps.append(m)
    res = run_bass_kernel_spmd(nc, in_maps, core_ids=list(range(NCORES)))
    out = np.concatenate(
        [np.asarray(r["out"], np.float32).reshape(SEQ, T, C)
         for r in res.results], axis=0)
    return out
